# version 10
# baseline (speedup 1.0000x reference)
import contextlib
import numpy as np
import ml_dtypes
import concourse.bass as bass
import concourse.mybir as mybir
from concourse.bass_utils import run_bass_kernel_spmd

F32 = mybir.dt.float32
BF16 = mybir.dt.bfloat16
AF = mybir.ActivationFunctionType
ALU = mybir.AluOpType
AX = mybir.AxisListType
BF = ml_dtypes.bfloat16

NCORES = 8
NEGM = -1.0e9
T = 2048
D = 1024
DC = 8


def _lhsT_blocks(W):
    K, M = W.shape
    assert K % 128 == 0 and M % 128 == 0
    Kc, Mc = K // 128, M // 128
    return np.ascontiguousarray(
        W.reshape(Kc, 128, Mc, 128).transpose(2, 1, 0, 3).reshape(Mc, 128, Kc * 128))


def _rhs_blocks(W):
    K, N = W.shape
    assert K % 128 == 0
    return np.ascontiguousarray(W.reshape(K // 128, 128, N))


def _colmat(v):
    return np.ascontiguousarray(v.reshape(-1, 128).T)


def _pmaj(blob):
    C_, P_, K_ = blob.shape
    return np.ascontiguousarray(blob.transpose(1, 0, 2).reshape(P_, C_ * K_))



class Prog:
    def __init__(self):
        self.nc = bass.Bass("TRN2", target_bir_lowering=False)
        self.stack = contextlib.ExitStack()
        self.block = None

    def sem(self, name):
        return self.stack.enter_context(self.nc.semaphore(name))

    def sb(self, name, shape, dtype):
        if getattr(self, "arena", None) is not None:
            import math
            per_part = 1
            for d in shape[1:]:
                per_part *= d
            per_part *= {F32: 4, BF16: 2, mybir.dt.uint8: 1}[dtype]
            off = self.arena_off
            self.arena_off = (off + per_part + 31) & ~31
            assert self.arena_off <= self.arena_size, (
                f"fused arena overflow at {name}: {self.arena_off}")
            return self.nc.alloc_sbuf_tensor_at(
                name, shape, dtype, offset=self.arena_base + off)
        return self.stack.enter_context(self.nc.sbuf_tensor(name, shape, dtype))

    def set_arena(self, tensor, size):
        self.arena = tensor
        self.arena_base = self.nc.lookup_mloc(tensor).addr
        self.arena_off = 0
        self.arena_size = size

    def ps(self, name, shape, dtype=F32):
        return self.stack.enter_context(self.nc.psum_tensor(name, shape, dtype))

    def sb_at(self, name, shape, dtype, base_tensor, byte_off=0):
        base = self.nc.lookup_mloc(base_tensor).addr
        return self.nc.alloc_sbuf_tensor_at(name, shape, dtype,
                                            offset=base + byte_off)

    def din(self, name, shape, dtype=F32):
        return self.nc.dram_tensor(name, shape, dtype, kind="ExternalInput").ap()

    def dout(self, name, shape, dtype=F32):
        return self.nc.dram_tensor(name, shape, dtype, kind="ExternalOutput").ap()

    def dint(self, name, shape, dtype=F32, **kw):
        return self.nc.dram_tensor(name, shape, dtype, **kw).ap()

    def begin(self):
        self.block = self.stack.enter_context(self.nc.Block())
        self.sv = {k: [self.sem(f"sv_{k}"), 0] for k in "vstg"}

    def svi(self, ins, k):
        sem, n = self.sv[k]
        ins.then_inc(sem, 1)
        self.sv[k][1] = n + 1
        return ins

    def svw(self, e, k):
        sem, n = self.sv[k]
        if n:
            e.wait_ge(sem, n)

    def end(self):
        self.stack.close()


RG = [list(range(NCORES))]



def decl_block0(p, fused=False):
    dio = {}
    dio["xT"] = p.din("xT", [D, T])
    dio["wq_r"] = p.din("wq_r", [128, DC * D])
    dio["wv_r"] = p.din("wv_r", [128, DC * D])
    dio["wk_l"] = p.din("wk_l", [128, DC * D])
    dio["wo_l"] = p.din("wo_l", [128, DC * D])
    dio["bo_pc"] = p.din("bo_pc", [128, DC])
    dio["gb2"] = p.din("gb2", [2, D])
    dio["ident"] = p.din("ident", [128, 128], BF16)
    dio["x1T"] = p.dint("x1T", [D, T]) if fused else p.dout("x1T", [D, T])
    dio["m_in"] = p.dint("m_in", [128, DC]); dio["m_out"] = p.dint("m_out", [128, DC])
    dio["s_in"] = p.dint("s_in", [128, DC]); dio["s_out"] = p.dint("s_out", [128, DC])
    dio["ctx_in"] = p.dint("ctx_in", [D, D])
    dio["ctx_out"] = p.dint("ctx_out", [D, D], addr_space="Shared")
    return dio


def emit_block0(p, dio, PFG, PBG, prev=None):
    xT = dio["xT"]; wq_r = dio["wq_r"]; wv_r = dio["wv_r"]; wk_l = dio["wk_l"]
    wo_l = dio["wo_l"]; bo_pc = dio["bo_pc"]; gb2_in = dio["gb2"]
    ident_in = dio["ident"]; x1T = dio["x1T"]
    m_in = dio["m_in"]; m_out = dio["m_out"]
    s_in = dio["s_in"]; s_out = dio["s_out"]
    ctx_in = dio["ctx_in"]; ctx_out = dio["ctx_out"]

    XB = [p.sb(f"XB{i}", [128, T], BF16) for i in range(DC)]
    KB = [p.sb(f"KB{i}", [128, T], BF16) for i in range(DC)]
    BUFB = [p.sb(f"BB{i}", [128, 1024], BF16) for i in range(16)]
    BUFC = [p.sb(f"BC{i}", [128, 1024], BF16) for i in range(16)]
    QF = KB
    ARENA = p.sb("ARENA", [128, 40960], mybir.dt.uint8)
    W8 = p.sb("W8", [128, DC * D], BF16)
    ident = p.sb("identS", [128, 128], BF16)
    ones_c = p.sb("ones_c", [128, 1], BF16)
    ones_r = p.sb("ones_r", [1, 128], BF16)
    Mloc = p.sb("Mloc", [128, DC], F32)
    Mpart = p.sb("Mpart", [128, 4], F32)
    negMg = p.sb("negMg", [128, DC], F32)
    Sloc = p.sb("Sloc", [128, DC], F32)
    Sg = p.sb("Sg", [128, DC], F32)
    rinvS = p.sb("rinvS", [128, DC], F32)
    bo_s = p.sb("bo_s", [128, DC], F32)
    gbr = p.sb("gbr", [2, D], BF16)
    cscr = [p.sb(f"cscr{i}", [128, 512], F32) for i in range(2)]
    scol = p.sb("scol", [128, 4], F32)
    U = [p.sb_at(f"U{i}", [128, 512], F32, ARENA, 2048 * i) for i in range(DC)]
    Y = [p.sb_at(f"Y{i}", [128, 512], F32, ARENA, 16384 + 2048 * i) for i in range(DC)]
    AT = [p.sb_at(f"AT{i}", [128, 512], BF16, ARENA, 32768 + 1024 * i) for i in range(DC)]
    Xt = [p.sb_at(f"Xt{i}", [128, 512], F32, XB[i]) for i in range(DC)]
    UB = [p.sb_at(f"UBa{i}", [128, 512], BF16, BUFC[i]) for i in range(DC)]
    USQ = [p.sb_at(f"USQa{i}", [128, 512], BF16, BUFC[8 + i]) for i in range(DC)]
    ROWS = p.sb("ROWS", [1, 2048], F32)
    EPS = p.sb("EPS", [1, 1], F32)
    RBF = p.sb("RBF", [1, 512], BF16)
    R2 = p.sb("R2", [2, 512], BF16)

    P = PFG[0:4]
    PT = PBG
    PG = PFG[4:6]

    s_ld = p.sem("s_ld")
    s_ldh = p.sem("s_ldh")
    s_w = p.sem("s_w")
    s_mm = p.sem("s_mm")
    s_ev = p.sem("s_ev")
    s_act = p.sem("s_act")
    s_cc1 = p.sem("s_cc1"); s_cc2 = p.sem("s_cc2"); s_cc3 = p.sem("s_cc3")
    s_dma = p.sem("s_dma")
    s_xt = p.sem("s_xt")
    s_cs = [p.sem("s_cs0"), p.sem("s_cs1")]
    s_st = p.sem("s_st")

    blk = p.block
    if prev:
        emit_barrier(blk, prev)

    @blk.sync
    def _(e):
        e.dma_start(out=ident[:], in_=ident_in).then_inc(s_ldh, 16)
        e.dma_start(out=bo_s[:], in_=bo_pc).then_inc(s_ldh, 16)

    @blk.gpsimd
    def _(e):
        for i in range(DC):
            e.dma_start(out=XB[i][:], in_=xT[128 * i:128 * (i + 1), :]).then_inc(s_ld, 16)
        e.dma_start(out=gbr[:], in_=gb2_in).then_inc(s_ld, 16)
        e.dma_start(out=W8[:], in_=wk_l).then_inc(s_w, 16)

    @blk.vector
    def _(e):
        p.svi(e.memset(ones_c[:], 1.0), "v")
        p.svi(e.memset(EPS[:], 1.0e-5), "v")
        p.svi(e.memset(ones_r[:], 1.0), "v")
        p.svi(e.memset(R2[:], 1.0), "v")

    LD_ALL = 16 * (DC + 1)

    @blk.tensor
    def _(e):
        e.wait_ge(s_ld, LD_ALL)
        e.wait_ge(s_ldh, 32)
        e.wait_ge(s_w, 16)
        p.svw(e, "v")
        for dc in range(DC):
            for tt in range(4):
                i = dc * 4 + tt
                b = i % 2
                if i >= 2:
                    e.wait_ge(s_ev, i - 1)
                for kc in range(DC):
                    ins = e.matmul(P[b][:], W8[:, dc * D + kc * 128: dc * D + (kc + 1) * 128],
                                   XB[kc][:, 512 * tt: 512 * (tt + 1)],
                                   start=(kc == 0), stop=(kc == DC - 1))
                ins.then_inc(s_mm, 1)

    @blk.vector
    def _(e):
        for dc in range(DC):
            for tt in range(4):
                i = dc * 4 + tt
                b = i % 2
                e.wait_ge(s_mm, i + 1)
                if tt == 0:
                    p.svw(e, "v")
                e.reduce_max(out=Mpart[:, tt:tt + 1], in_=P[b][:], axis=AX.X)
                e.tensor_copy(out=KB[dc][:, 512 * tt: 512 * (tt + 1)],
                              in_=P[b][:]).then_inc(s_ev, 1)
            e.wait_ge(s_ev, dc * 4 + 4)
            ins = e.reduce_max(out=Mloc[:, dc:dc + 1], in_=Mpart[:], axis=AX.X)
            if dc < DC - 1:
                p.svi(ins, "v")
        ins.then_inc(s_ev, 1)

    @blk.gpsimd
    def _(e):
        e.wait_ge(s_ev, 33)
        e.dma_start(out=m_in, in_=Mloc[:]).then_inc(s_dma, 16)
        e.wait_ge(s_dma, 16)
        e.collective_compute("AllReduce", ALU.max, replica_groups=RG,
                             ins=[m_in.opt()], outs=[m_out.opt()]).then_inc(s_cc1)
        e.wait_ge(s_cc1, 1)
        e.dma_start(out=negMg[:], in_=m_out).then_inc(s_dma, 16)
        e.wait_ge(s_mm, 32)
        e.dma_start(out=W8[:], in_=wv_r).then_inc(s_w, 16)

    @blk.tensor
    def _(e):
        e.wait_ge(s_w, 32)
        for tc in range(16):
            for oh in range(2):
                i = tc * 2 + oh
                b = i % 2
                if i >= 2:
                    e.wait_ge(s_ev, 33 + (i - 1))
                for kc in range(DC):
                    ins = e.matmul(P[b][:], XB[kc][:, 128 * tc: 128 * (tc + 1)],
                                   W8[:, kc * D + oh * 512: kc * D + (oh + 1) * 512],
                                   start=(kc == 0), stop=(kc == DC - 1))
                ins.then_inc(s_mm, 1)

    @blk.vector
    def _(e):
        for tc in range(16):
            for oh in range(2):
                i = tc * 2 + oh
                b = i % 2
                e.wait_ge(s_mm, 32 + i + 1)
                e.tensor_copy(out=BUFB[tc][:, 512 * oh: 512 * (oh + 1)],
                              in_=P[b][:]).then_inc(s_ev, 1)
        e.wait_ge(s_dma, 32)
        e.tensor_scalar_mul(out=negMg[:], in0=negMg[:],
                            scalar1=-1.0).then_inc(s_ev, 1)

    @blk.scalar
    def _(e):
        e.wait_ge(s_ev, 66)
        p.svw(e, "v")
        for dc in range(DC):
            ins = e.activation(out=KB[dc][:], in_=KB[dc][:], func=AF.Exp,
                               bias=negMg[:, dc:dc + 1], scale=1.0,
                               accum_out=Sloc[:, dc:dc + 1])
        ins.then_inc(s_act, 1)

    @blk.gpsimd
    def _(e):
        e.wait_ge(s_act, 1)
        e.dma_start(out=s_in, in_=Sloc[:]).then_inc(s_dma, 16)
        e.wait_ge(s_dma, 48)
        e.collective_compute("AllReduce", ALU.add, replica_groups=RG,
                             ins=[s_in.opt()], outs=[s_out.opt()]).then_inc(s_cc2)
        e.wait_ge(s_cc2, 1)
        e.dma_start(out=Sg[:], in_=s_out).then_inc(s_dma, 16)

    @blk.tensor
    def _(e):
        e.wait_ge(s_act, 1)
        for tc in range(16):
            b = tc % 2
            if tc >= 2:
                e.wait_ge(s_ev, 66 + (tc - 1))
            for dc in range(DC):
                ins = e.transpose(PT[b][:, 128 * dc:128 * (dc + 1)],
                                  KB[dc][:, 128 * tc:128 * (tc + 1)], ident[:])
            ins.then_inc(s_mm, 1)

    @blk.vector
    def _(e):
        for tc in range(16):
            b = tc % 2
            e.wait_ge(s_mm, 64 + tc + 1)
            e.tensor_copy(out=BUFC[tc][:], in_=PT[b][:]).then_inc(s_ev, 1)

    @blk.tensor
    def _(e):
        e.wait_ge(s_ev, 82)
        for i in range(16):
            dc, oh = divmod(i, 2)
            b = i % 2
            if i >= 2:
                e.wait_ge(s_ev, 82 + (i - 1))
            for tc in range(16):
                ins = e.matmul(P[b][:], BUFC[tc][:, 128 * dc:128 * (dc + 1)],
                               BUFB[tc][:, 512 * oh:512 * (oh + 1)],
                               start=(tc == 0), stop=(tc == 15))
            ins.then_inc(s_mm, 1)

    @blk.vector
    def _(e):
        for i in range(16):
            b = i % 2
            e.wait_ge(s_mm, 80 + i + 1)
            if i >= 2:
                e.wait_ge(s_cs[b], 16 * (i // 2))
            e.tensor_copy(out=cscr[b][:], in_=P[b][:]).then_inc(s_ev, 1)

    @blk.gpsimd
    def _(e):
        for i in range(16):
            dc, oh = divmod(i, 2)
            b = i % 2
            e.wait_ge(s_ev, 82 + i + 1)
            e.dma_start(out=ctx_in[128 * dc:128 * (dc + 1), 512 * oh:512 * (oh + 1)],
                        in_=cscr[b][:]).then_inc(s_cs[b], 16)
        e.wait_ge(s_cs[0], 128)
        e.wait_ge(s_cs[1], 128)
        e.collective_compute("AllReduce", ALU.add, replica_groups=RG,
                             ins=[ctx_in.opt()], outs=[ctx_out.opt()]).then_inc(s_cc3)
        e.dma_start(out=W8[:], in_=wq_r).then_inc(s_w, 16)
        e.wait_ge(s_cc3, 1)
        for dc in range(DC):
            e.dma_start(out=BUFB[dc][:], in_=ctx_out[128 * dc:128 * (dc + 1), :]
                        ).then_inc(s_dma, 16)

    @blk.tensor
    def _(e):
        e.wait_ge(s_w, 48)
        for tc in range(16):
            for oh in range(2):
                i = tc * 2 + oh
                b = 2 + (i % 2)
                if tc >= 1:
                    e.wait_ge(s_act, 1 + tc)
                for kc in range(DC):
                    ins = e.matmul(P[b][:], XB[kc][:, 128 * tc:128 * (tc + 1)],
                                   W8[:, kc * D + oh * 512: kc * D + (oh + 1) * 512],
                                   start=(kc == 0), stop=(kc == DC - 1))
                ins.then_inc(s_mm, 1)

    @blk.vector
    def _(e):
        for tc in range(16):
            if tc >= 1:
                e.wait_ge(s_act, 1 + tc)
            for oh in range(2):
                i = tc * 2 + oh
                b = 2 + (i % 2)
                e.wait_ge(s_mm, 96 + i + 1)
                ins = e.tensor_reduce(out=scol[:, oh:oh + 1], in_=P[b][:], axis=AX.X,
                                      op=ALU.max, negate=True)
            p.svi(ins, "v")
            p.svw(e, "v")
            e.tensor_tensor(out=scol[:, 0:1], in0=scol[:, 0:1], in1=scol[:, 1:2],
                            op=ALU.min).then_inc(s_ev, 1)
            e.wait_ge(s_act, 2 + tc)
            p.svi(e.tensor_tensor(out=scol[:, 2:3], in0=scol[:, 2:3], in1=scol[:, 3:4],
                                  op=ALU.add), "v")
            p.svw(e, "v")
            p.svi(e.reciprocal(out=scol[:, 2:3], in_=scol[:, 2:3]), "v")
            p.svw(e, "v")
            e.tensor_scalar(out=BUFC[tc][:], in0=BUFC[tc][:],
                            scalar1=scol[:, 2:3], scalar2=1.0 / 32.0,
                            op0=ALU.mult, op1=ALU.mult).then_inc(s_ev, 1)

    @blk.scalar
    def _(e):
        e.wait_ge(s_mm, 96)
        for tc in range(16):
            e.wait_ge(s_ev, 98 + 2 * tc + 1)
            for oh in range(2):
                b = 2 + ((tc * 2 + oh) % 2)
                ins = e.activation(out=BUFC[tc][:, 512 * oh:512 * (oh + 1)], in_=P[b][:],
                                   func=AF.Exp, bias=scol[:, 0:1], scale=1.0,
                                   accum_out=scol[:, 2 + oh:3 + oh])
            ins.then_inc(s_act, 1)

    @blk.tensor
    def _(e):
        for g in range(2):
            for dc in range(DC):
                i = g * DC + dc
                b = i % 2
                if i >= 2:
                    e.wait_ge(s_ev, 130 + (i - 1))
                for tj in range(8):
                    tc = g * 8 + tj
                    e.wait_ge(s_ev, 98 + 2 * tc + 2)
                    ins = e.transpose(PT[b][:, 128 * tj:128 * (tj + 1)],
                                      BUFC[tc][:, 128 * dc:128 * (dc + 1)], ident[:])
                ins.then_inc(s_mm, 1)

    @blk.vector
    def _(e):
        for g in range(2):
            for dc in range(DC):
                i = g * DC + dc
                b = i % 2
                e.wait_ge(s_mm, 128 + i + 1)
                e.tensor_copy(out=QF[dc][:, 1024 * g:1024 * (g + 1)],
                              in_=PT[b][:]).then_inc(s_ev, 1)

    @blk.gpsimd
    def _(e):
        e.wait_ge(s_mm, 144)
        e.dma_start(out=W8[:], in_=wo_l).then_inc(s_w, 16)

    @blk.vector
    def _(e):
        e.wait_ge(s_dma, 192)
        p.svi(e.reciprocal(out=rinvS[:], in_=Sg[:]), "v")
        p.svw(e, "v")
        for dc in range(DC):
            ins = e.tensor_scalar_mul(out=BUFB[dc][:], in0=BUFB[dc][:],
                                      scalar1=rinvS[:, dc:dc + 1])
        ins.then_inc(s_ev, 1)

    @blk.gpsimd
    def _(e):
        pass

    C = {"mm": 144, "ev": 147, "act": 17}

    def ln_fm(getin, getout, with_gb):
        mm0, ev0, act0 = C["mm"], C["ev"], C["act"]

        @blk.vector
        def _(e):
            e.wait_ge(s_ev, ev0)
            for dm in range(DC):
                ins = e.tensor_copy(out=UB[dm][:], in_=getin(dm))
            p.svi(ins, "v")
            p.svw(e, "v")
            for dm in range(DC):
                ins = e.tensor_tensor(out=USQ[dm][:], in0=UB[dm][:], in1=UB[dm][:],
                                      op=ALU.mult)
            ins.then_inc(s_ev, 1)

        @blk.tensor
        def _(e):
            e.wait_ge(s_ev, ev0 + 1)
            for dm in range(DC):
                e.matmul(PG[0][0:1, :], ones_c[:], UB[dm][:],
                         start=(dm == 0), stop=(dm == DC - 1))
            for dm in range(DC):
                ins = e.matmul(PG[1][0:1, :], ones_c[:], USQ[dm][:],
                               start=(dm == 0), stop=(dm == DC - 1))
            ins.then_inc(s_mm, 1)

        @blk.scalar
        def _(e):
            e.wait_ge(s_mm, mm0 + 1)
            e.activation(out=ROWS[0:1, 0:512], in_=PG[0][0:1, :], func=AF.Copy,
                         bias=0.0, scale=-1.0 / D).then_inc(s_act, 1)

        @blk.vector
        def _(e):
            e.wait_ge(s_act, act0 + 1)
            p.svi(e.tensor_tensor(out=ROWS[0:1, 512:1024], in0=ROWS[0:1, 0:512],
                                  in1=ROWS[0:1, 0:512], op=ALU.mult), "v")
            p.svw(e, "v")
            e.scalar_tensor_tensor(out=ROWS[0:1, 512:1024], in0=PG[1][0:1, :],
                                   scalar=1.0 / D, in1=ROWS[0:1, 512:1024],
                                   op0=ALU.mult, op1=ALU.subtract
                                   ).then_inc(s_ev, 1)

        @blk.scalar
        def _(e):
            e.wait_ge(s_ev, ev0 + 2)
            p.svi(e.activation(out=ROWS[0:1, 1024:1536], in_=ROWS[0:1, 512:1024],
                               func=AF.Ln, bias=EPS[0:1, 0:1], scale=1.0), "s")
            p.svw(e, "s")
            e.activation(out=RBF[0:1, :], in_=ROWS[0:1, 1024:1536],
                         func=AF.Exp, bias=0.0, scale=-0.5
                         ).then_inc(s_act, 1)

        @blk.vector
        def _(e):
            e.wait_ge(s_act, act0 + 2)
            e.tensor_tensor(out=R2[0:1, :], in0=ROWS[0:1, 0:512],
                            in1=RBF[0:1, :], op=ALU.mult
                            ).then_inc(s_ev, 1)

        @blk.tensor
        def _(e):
            e.wait_ge(s_ev, ev0 + 3)
            for dm in range(DC):
                if dm >= 1:
                    e.wait_ge(s_ev, ev0 + 3 + dm)
                if with_gb:
                    e.matmul(PG[0][:], gbr[0:1, 128 * dm:128 * (dm + 1)],
                             RBF[0:1, :], start=True, stop=True)
                    ins = e.matmul(PG[1][:], gbr[0:2, 128 * dm:128 * (dm + 1)],
                                   R2[0:2, :], start=True, stop=True)
                else:
                    e.matmul(PG[0][:], ones_r[:], RBF[0:1, :], start=True, stop=True)
                    ins = e.matmul(PG[1][:], ones_r[:], R2[0:1, :], start=True, stop=True)
                ins.then_inc(s_mm, 1)

        @blk.vector
        def _(e):
            for dm in range(DC):
                e.wait_ge(s_mm, mm0 + 1 + dm + 1)
                p.svi(e.tensor_tensor(out=getout(dm), in0=getin(dm), in1=PG[0][:],
                                      op=ALU.mult), "v")
                p.svw(e, "v")
                e.tensor_tensor(out=getout(dm), in0=getout(dm), in1=PG[1][:],
                                op=ALU.add).then_inc(s_ev, 1)

        C["mm"] = mm0 + 9
        C["ev"] = ev0 + 11
        C["act"] = act0 + 2

    def attn_tile(tt):
        mm0, ev0 = C["mm"], C["ev"]

        @blk.gpsimd
        def _(e):
            e.wait_ge(s_mm, 128)
            if tt >= 1:
                e.wait_ge(s_st, 128 * tt)
            for dm in range(DC):
                e.dma_start(out=Xt[dm][:],
                            in_=xT[128 * dm:128 * (dm + 1), 512 * tt:512 * (tt + 1)]
                            ).then_inc(s_xt, 16)

        @blk.tensor
        def _(e):
            e.wait_ge(s_ev, ev0)
            for ec in range(DC):
                b = ec % 2
                if ec >= 2:
                    e.wait_ge(s_ev, ev0 + (ec - 1))
                for dc in range(DC):
                    ins = e.matmul(P[b][:], BUFB[dc][:, 128 * ec:128 * (ec + 1)],
                                   QF[dc][:, 512 * tt:512 * (tt + 1)],
                                   start=(dc == 0), stop=(dc == DC - 1))
                ins.then_inc(s_mm, 1)

        @blk.vector
        def _(e):
            for ec in range(DC):
                b = ec % 2
                e.wait_ge(s_mm, mm0 + ec + 1)
                e.tensor_copy(out=AT[ec][:], in_=P[b][:]).then_inc(s_ev, 1)

        @blk.tensor
        def _(e):
            e.wait_ge(s_ev, ev0 + 8)
            e.wait_ge(s_w, 64)
            e.wait_ge(s_act, 17)
            for dm in range(DC):
                b = 2 + dm % 2
                if dm >= 2:
                    e.wait_ge(s_ev, ev0 + 8 + (dm - 1))
                for ec in range(DC):
                    ins = e.matmul(P[b][:], W8[:, dm * D + ec * 128: dm * D + (ec + 1) * 128],
                                   AT[ec][:], start=(ec == 0), stop=(ec == DC - 1))
                ins.then_inc(s_mm, 1)

        @blk.vector
        def _(e):
            for dm in range(DC):
                b = 2 + dm % 2
                e.wait_ge(s_mm, mm0 + 8 + dm + 1)
                if dm == 0:
                    e.wait_ge(s_xt, 128 * (tt + 1))
                e.scalar_tensor_tensor(out=U[dm][:], in0=P[b][:],
                                       scalar=bo_s[:, dm:dm + 1],
                                       in1=Xt[dm][:],
                                       op0=ALU.add, op1=ALU.add
                                       ).then_inc(s_ev, 1)

        C["mm"] = mm0 + 16
        C["ev"] = ev0 + 16

        ln_fm(lambda dm: U[dm][:], lambda dm: Y[dm][:], True)

        ev1 = C["ev"]

        @blk.vector
        def _(e):
            e.wait_ge(s_ev, ev1)
            for dm in range(DC):
                ins = e.tensor_add(out=Xt[dm][:], in0=Xt[dm][:], in1=Y[dm][:])
            ins.then_inc(s_ev, 1)
        C["ev"] = ev1 + 1

        def xsl(dm):
            return Xt[dm][:]
        ln_fm(xsl, xsl, False)

        ev2 = C["ev"]

        @blk.gpsimd
        def _(e):
            e.wait_ge(s_ev, ev2)
            for dm in range(DC):
                e.dma_start(out=x1T[128 * dm:128 * (dm + 1), 512 * tt:512 * (tt + 1)],
                            in_=Xt[dm][:]).then_inc(s_st, 16)

    for tt in range(4):
        attn_tile(tt)

    @blk.gpsimd
    def _(e):
        e.wait_ge(s_st, 16 * 32)

    return [(s_st, 16 * 32), (s_ev, C["ev"]), (s_mm, C["mm"]),
            (s_act, C["act"]), (s_dma, 192), (s_w, 64)]


def emit_barrier(blk, finals):
    for eng in ["tensor", "vector", "scalar", "gpsimd", "sync"]:
        def body(e, finals=finals):
            for sem, val in finals:
                e.wait_ge(sem, val)
        getattr(blk, eng)(body)


def build_block0():
    p = Prog()
    dio = decl_block0(p)
    PFG = [p.ps(f"PFg{i}", [128, 512], F32) for i in range(6)]
    PBG = [p.ps(f"PBg{i}", [128, 1024], BF16) for i in range(2)]
    p.begin()
    emit_block0(p, dio, PFG, PBG)
    p.end()
    return p.nc



def _prep_block0_inputs(inputs):
    x = np.asarray(inputs["L0_attn_x"], np.float32)[0]
    Wq = np.asarray(inputs["L0_attn_Wq"], np.float32)
    Wk = np.asarray(inputs["L0_attn_Wk"], np.float32)
    Wv = np.asarray(inputs["L0_attn_Wv"], np.float32)
    Wo = np.asarray(inputs["L0_attn_Wo"], np.float32)
    bo = np.asarray(inputs["L0_attn_bo"], np.float32)
    g = np.asarray(inputs["L0_attn_gamma"], np.float32)
    b = np.asarray(inputs["L0_attn_beta"], np.float32)
    shared = {
        "wq_r": _pmaj(_rhs_blocks(Wq)),
        "wv_r": _pmaj(_rhs_blocks(Wv)),
        "wk_l": _pmaj(_lhsT_blocks(Wk)),
        "wo_l": _pmaj(_lhsT_blocks(Wo)),
        "bo_pc": _colmat(bo),
        "gb2": np.ascontiguousarray(np.stack([g, b])),
        "ident": np.eye(128, dtype=BF),
    }
    maps = []
    for c in range(NCORES):
        m = dict(shared)
        m["xT"] = np.ascontiguousarray(x[T * c:T * (c + 1)].T)
        maps.append(m)
    return maps


def run_block0(inputs):
    nc = build_block0()
    maps = _prep_block0_inputs(inputs)
    res = run_bass_kernel_spmd(nc, maps, core_ids=list(range(NCORES)))
    return [r["x1T"] for r in res.results]


def _prep_fused_maps(inputs):
    m0 = _prep_block0_inputs(inputs)
    m1 = _prep_l0ffn_maps(inputs, 6)
    m2 = _prep_l1_maps(inputs)
    maps = []
    for c in range(NCORES):
        d = dict(m0[c])
        m1c = dict(m1[c])
        m1c.pop("ident", None)
        d.update(m1c)
        d.update(m2[c])
        maps.append(d)
    return maps


def kernel(**inputs):
    nc = build_fused(6)
    maps = _prep_fused_maps(inputs)
    res = run_bass_kernel_spmd(nc, maps, core_ids=list(range(NCORES)))
    xo = [r["xoT"] for r in res.results]
    out = np.concatenate([a.T for a in xo], axis=0)[None]
    return np.ascontiguousarray(out, dtype=np.float32)


def kernel_unfused(**inputs):
    x1 = run_block0(inputs)
    x2 = run_l0ffn(inputs, x1, 6)
    xo = run_l1(inputs, x2)
    out = np.concatenate([a.T for a in xo], axis=0)[None]
    return np.ascontiguousarray(out, dtype=np.float32)



INNER = 1024
DIMH = 64
NHEAD = 16
W = 256
DI = 2816
DIC = 22
QK_SCALE = 8.0


def _rotary_tables():
    N = 16384
    half = np.arange(0, DIMH, 2, dtype=np.float64)
    inv_freq = 1.0 / 10000.0 ** (half / DIMH)
    t = np.arange(-256, N, dtype=np.float64)
    freqs = t[:, None] * inv_freq[None, :]
    freqs = np.concatenate([freqs, freqs], axis=1)
    cos = np.cos(freqs)
    sin = np.sin(freqs)
    base_half = (half + 0.4 * DIMH) / (1.4 * DIMH)
    base = np.concatenate([base_half, base_half])
    tmod = ((t.astype(np.int64) % W) + W) % W
    srel = base[None, :] ** (tmod[:, None] / (W // 2))
    inv = base[None, :] ** (-tmod[:, None] / (W // 2))
    return (cos * srel, sin * srel, cos * inv, sin * inv), base


def _prep_l0ffn_shared(inputs):
    Wqkv = np.asarray(inputs["L0_ffn_Wqkv"], np.float32)
    Wo = np.asarray(inputs["L0_ffn_Wo"], np.float32)
    W1 = np.asarray(inputs["L0_ffn_W1"], np.float32)
    W2 = np.asarray(inputs["L0_ffn_W2"], np.float32)
    b1 = np.asarray(inputs["L0_ffn_b1"], np.float32)
    b2 = np.asarray(inputs["L0_ffn_b2"], np.float32)
    qs = np.asarray(inputs["L0_ffn_q_scale"], np.float32)
    ks = np.asarray(inputs["L0_ffn_k_scale"], np.float32)
    lng = np.asarray(inputs["L0_ffn_ln_g"], np.float32)
    lnb = np.asarray(inputs["L0_ffn_ln_b"], np.float32)
    DIM0 = 2730
    sh = {}
    for L in range(6):
        sh[f"wqk_l{L}"] = _pmaj(_lhsT_blocks(Wqkv[L][:, :2048]))
        sh[f"wv_r{L}"] = _pmaj(_rhs_blocks(Wqkv[L][:, 2048:3072]))
        sh[f"wo_l{L}"] = _pmaj(_lhsT_blocks(Wo[L]))
        w1a = np.zeros((D, DI), np.float32); w1a[:, :DIM0] = W1[L][:, :DIM0]
        w1g = np.zeros((D, DI), np.float32); w1g[:, :DIM0] = W1[L][:, DIM0:]
        sh[f"w1a_l{L}"] = _lhsT_blocks(w1a)
        sh[f"w1g_l{L}"] = _lhsT_blocks(w1g)
        w2p = np.zeros((DI, D), np.float32); w2p[:DIM0] = W2[L]
        sh[f"w2_l{L}"] = _lhsT_blocks(w2p)
        b1a = np.zeros(DI, np.float32); b1a[:DIM0] = b1[L][:DIM0]
        b1g = np.zeros(DI, np.float32); b1g[:DIM0] = b1[L][DIM0:]
        sh[f"b1a_pc{L}"] = _colmat(b1a)
        sh[f"b1g_pc{L}"] = _colmat(b1g)
        sh[f"b2_pc{L}"] = _colmat(b2[L])
        sh[f"lngb{L}"] = np.ascontiguousarray(np.stack([lng[L], lnb[L]]))
        sh[f"qs_c{L}"] = np.tile(qs[L], 2).reshape(128, 1).copy()
        sh[f"ks_c{L}"] = np.tile(ks[L], 2).reshape(128, 1).copy()
        sh[f"qsr_c{L}"] = np.tile(np.roll(qs[L].reshape(2, 32), 1, 0).ravel(), 2).reshape(128, 1).copy()
        sh[f"ksr_c{L}"] = np.tile(np.roll(ks[L].reshape(2, 32), 1, 0).ravel(), 2).reshape(128, 1).copy()
    (csq, ssq, csk, ssk), base = _rotary_tables()
    sh["_tab_csq"] = csq; sh["_tab_ssq"] = ssq
    sh["_tab_csk"] = csk; sh["_tab_ssk"] = ssk
    base2 = (base ** 2).astype(np.float32)
    sh["base2_c"] = np.tile(base2, 2).reshape(128, 1).copy()
    Rp = np.zeros((128, 128), np.float32)
    for h in range(2):
        o = 64 * h
        for j in range(32):
            Rp[o + 32 + j, o + j] = -1.0
            Rp[o + j, o + 32 + j] = 1.0
    sh["rperm"] = Rp.astype(BF)
    e16 = np.zeros((8, 128, 16), np.float32)
    e16t = np.zeros((8, 16, 128), np.float32)
    for c in range(8):
        for p in range(128):
            h = 2 * c + p // 64
            e16[c, p, h] = 1.0
            e16t[c, h, p] = 1.0
    sh["e16"] = _pmaj(e16).astype(BF)
    sh["e16t"] = _pmaj(e16t).astype(BF)
    sh["e16t8"] = (_pmaj(e16t) * QK_SCALE).astype(BF)
    mk = np.zeros((128, 384), np.float32)
    for i in range(128):
        mk[i, 0:128][np.arange(128) < i] = NEGM
        mk[i, 256:384][np.arange(128) > i] = NEGM
    sh["_maskW"] = mk
    mk0a = mk.copy()
    mk0a[:, 0:256] = NEGM
    sh["_maskW0A"] = mk0a
    mk0b = mk.copy()
    mk0b[:, 0:128] = NEGM
    sh["_maskW0B"] = mk0b
    return sh


def decl_l0ffn(p, nlayers=6, fused=False):
    dio = {}
    dio["x1T"] = dio0 = None
    if not fused:
        dio["x1T"] = p.din("x1T", [D, T])
        dio["ident"] = p.din("ident", [128, 128], BF16)
    win = {}
    for L in range(nlayers):
        for nm in ["wqk_l", "wv_r", "wo_l", "w1a_l", "w1g_l", "w2_l"]:
            shp = {"wqk_l": [128, 16 * 1024], "wv_r": [128, 8 * 1024],
                   "wo_l": [128, 8 * 1024], "w1a_l": [22, 128, 1024],
                   "w1g_l": [22, 128, 1024], "w2_l": [8, 128, 2816]}[nm]
            win[nm + str(L)] = p.din(nm + str(L), shp)
        win["b1a_pc%d" % L] = p.din("b1a_pc%d" % L, [128, DIC])
        win["b1g_pc%d" % L] = p.din("b1g_pc%d" % L, [128, DIC])
        win["b2_pc%d" % L] = p.din("b2_pc%d" % L, [128, DC])
        win["lngb%d" % L] = p.din("lngb%d" % L, [2, D])
        win["scl%d" % L] = p.din("scl%d" % L, [128, 4])
    dio["win"] = win
    dio["x2T"] = p.dint("x2T", [D, T]) if fused else p.dout("x2T", [D, T])
    dio["tabs"] = p.din("tabs", [4, 128, T + 2 * W])
    dio["maskW"] = p.din("maskW", [128, 384], BF16)
    dio["mask0"] = p.din("mask0", [128, 384], BF16)
    dio["mask0b"] = p.din("mask0b", [128, 384], BF16)
    dio["rperm"] = p.din("rperm", [128, 128], BF16)
    dio["e16"] = p.din("e16", [128, 8 * 16], BF16)
    dio["e16t"] = p.din("e16t", [16, 8 * 128], BF16)
    dio["e16t8"] = p.din("e16t8", [16, 8 * 128], BF16)
    dio["base2_c"] = p.din("base2_c", [128, 1])
    dio["hsc"] = p.din("hsc", [128, 1])
    dio["zd"] = [p.dint(f"zd{i}", [D, T]) for i in range(2)]
    dio["hz_in"] = p.dint("hz_in", [128, DC * 257])
    dio["hz_out"] = p.dint("hz_out", [NCORES * 128, DC * 257], addr_space="Shared")
    dio["zt_in"] = p.dint("zt_in", [128, DC])
    dio["zt_out"] = p.dint("zt_out", [NCORES * 128, DC], addr_space="Shared")
    return dio


def emit_l0ffn(p, dio, PFG, PBG, nlayers=6, prev=None):
    x1T = dio["x1T"]; x2T = dio["x2T"]; win = dio["win"]
    tabs_d = dio["tabs"]; maskW_d = dio["maskW"]; mask0_d = dio["mask0"]
    mask0b_d = dio["mask0b"]; rperm_d = dio["rperm"]; e16_d = dio["e16"]
    e16t_d = dio["e16t"]; e16t8_d = dio["e16t8"]
    base2_d = dio["base2_c"]; hsc_d = dio["hsc"]
    ident_in = dio["ident"]
    zd = dio["zd"]; hz_in = dio["hz_in"]; hz_out = dio["hz_out"]
    zt_in = dio["zt_in"]; zt_out = dio["zt_out"]

    AR = p.sb("ARN", [128, 132384], mybir.dt.uint8)

    def at(name, shape, dtype, off):
        return p.sb_at(name, shape, dtype, AR, off)
    FB = [at(f"FBf{i}", [128, 1025], BF16, 2080 * i) for i in range(DC)]
    XN = [at(f"XNf{i}", [128, 1024], BF16, 16640 + 2048 * i) for i in range(DC)]
    HT = [at(f"HTf{i}", [128, 1024], BF16, 33024 + 2048 * i) for i in range(DIC)]
    ZO = [at(f"ZOf{i}", [128, 512], F32, 78080 + 2048 * i) for i in range(DC)]
    W1S = [at(f"W1S{i}", [128, 1024], BF16, 94464 + 2048 * i) for i in range(4)]
    W2S = [at(f"W2S{i}", [128, 2816], BF16, 102656 + 5632 * i) for i in range(2)]
    GELO = [at(f"GELO{i}", [128, 512], BF16, 113920 + 1024 * i) for i in range(2)]
    USQ = [at(f"USQf{i}", [128, 512], BF16, 115968 + 1024 * i) for i in range(DC)]
    UB = [at(f"UBf{i}", [128, 512], BF16, 94464 + 1024 * i) for i in range(DC)]
    F2 = [at(f"F2f{i}", [128, 512], F32, 94464 + 2048 * i) for i in range(DC)]
    KP = [at(f"KP{i}", [128, 1024], BF16, 2048 * i) for i in range(DC)]
    VB = [at(f"VB{i}", [128, 1024], BF16, 16384 + 2048 * i) for i in range(6)]
    QH = [at(f"QH{i}", [128, 512], BF16, 28672 + 1024 * i) for i in range(DC)]
    ZBALL = at("ZBALL", [128, DC * 528], BF16, 36864)
    ZBp = [ZBALL[:, 528 * i:528 * i + 516] for i in range(DC)]
    AO = [at(f"AO{i}", [128, 256], BF16, 45312 + 512 * i) for i in range(DC)]
    ZRW = [at(f"ZRW{i}", [128, 256], F32, 49408 + 1024 * i) for i in range(DC)]
    ZOW = [at(f"ZOW{i}", [128, 256], F32, 57600 + 1024 * i) for i in range(DC)]
    QR = [at(f"QR{i}", [128, 512], BF16, 65792 + 1024 * i) for i in range(2)]
    QSQ = at("QSQ", [128, 512], BF16, 67840)
    RBS = at("RBS", [128, 512], BF16, 68864)
    T1 = at("T1", [128, 512], BF16, 69888)
    T3 = at("T3", [128, 512], BF16, 70912)
    PEx = [at(f"PEx{i}", [128, 384], BF16, 71936 + 768 * i) for i in range(2)]
    PTS = [at(f"PTS{i}", [128, 384], BF16, 73472 + 768 * i) for i in range(2)]
    TAB = [at(f"TAB{i}", [128, 520], BF16, 75008 + 1056 * i) for i in range(4)]
    SSR2 = at("SSR2", [16, 512], F32, 79232)
    RN16B = at("RN16B", [16, 512], BF16, 81280)
    HZALL = at("HZALL", [128, 64 * 257], BF16, 90752)
    SSR = at("SSR", [16, 512], F32, 87680)
    RN16 = at("RN16", [16, 512], BF16, 89728)
    WQK = p.sb("WQK", [128, 16 * 1024], BF16)
    WV = p.sb("WV", [128, 8 * 1024], BF16)
    WO = p.sb("WO", [128, 8 * 1024], BF16)
    NM = p.sb("NM", [128, 16], F32)
    SS = p.sb("SS", [128, 16], F32)
    RI = p.sb("RI", [128, 16], F32)
    ZP = p.sb("ZPf", [128, 8], F32)
    ZPALL = p.sb("ZPALLf", [128, 64], F32)
    ZPF = p.sb("ZPFf", [128, 8], F32)
    TCOL = p.sb("TCOLf", [128, 8], F32)
    MK = p.sb("MKs", [128, 384], BF16)
    MK0 = p.sb("MK0s", [128, 384], BF16)
    MK0B = p.sb("MK0Bs", [128, 384], BF16)
    RPERM = p.sb("RPERMs", [128, 128], BF16)
    E16 = p.sb("E16s", [128, 8 * 16], BF16)
    E16T = p.sb("E16Ts", [16, 8 * 128], BF16)
    E16T8 = p.sb("E16T8s", [16, 8 * 128], BF16)
    B2C = p.sb("B2Cs", [128, 1], F32)
    HSC = p.sb("HSCs", [128, 1], F32)
    IDENT = p.sb("IDENTs", [128, 128], BF16)
    SCL = p.sb("SCLs", [128, 4], F32)
    B1A = p.sb("B1As", [128, DIC], F32)
    B1G = p.sb("B1Gs", [128, DIC], F32)
    B2F = p.sb("B2Fs", [128, DC], F32)
    GBR = p.sb("GBRs", [2, D], BF16)
    LN8 = p.sb("LN8s", [16, 1], F32)
    E24 = p.sb("E24s", [16, 1], F32)
    EPS = p.sb("EPSs", [1, 1], F32)
    RMU = at("RMUs", [1, 512], F32, 124160)
    RVA = at("RVAs", [1, 512], F32, 126208)
    RBF = at("RBFs", [1, 512], BF16, 128256)
    R2 = at("R2s", [2, 512], BF16, 129280)
    ONEC = p.sb("ONECs", [128, 1], BF16)
    ONER = p.sb("ONERs", [1, 128], BF16)

    PF = PFG
    PB = PBG

    s_ld = p.sem("l_ld")
    s_w = p.sem("l_w")
    s_w1 = [p.sem("l_w1a"), p.sem("l_w1b")]
    s_w2 = [p.sem("l_w2a"), p.sem("l_w2b")]
    s_wh = p.sem("l_wh")
    s_mm = p.sem("l_mm")
    s_ev = p.sem("l_ev")
    s_ac = p.sem("l_ac")
    s_dg = p.sem("l_dg")
    s_dh = p.sem("l_dh")
    s_st = p.sem("l_st")
    s_sta = p.sem("l_sta")
    s_pl = p.sem("l_pl")
    ccs = [p.sem(f"l_cc{i}") for i in range(2 * nlayers + 1)]

    blk = p.block
    if prev:
        emit_barrier(blk, prev)
    C = {"mm": 0, "ev": 0, "ac": 0, "ld": 0, "w": 0, "wh": 0, "dg": 0,
         "dh": 0, "st": 0, "sta": 0, "pl": 0, "w1": [0, 0], "w2": [0, 0]}

    @blk.gpsimd
    def _(e):
        e.dma_start(out=MK[:], in_=maskW_d).then_inc(s_ld, 16)
        e.dma_start(out=MK0[:], in_=mask0_d).then_inc(s_ld, 16)
        e.dma_start(out=MK0B[:], in_=mask0b_d).then_inc(s_ld, 16)
        e.dma_start(out=E16[:], in_=e16_d).then_inc(s_ld, 16)
        e.dma_start(out=E16T[:], in_=e16t_d).then_inc(s_ld, 16)
        e.dma_start(out=E16T8[:], in_=e16t8_d).then_inc(s_ld, 16)
        e.dma_start(out=B2C[:], in_=base2_d).then_inc(s_ld, 16)
        e.dma_start(out=HSC[:], in_=hsc_d).then_inc(s_ld, 16)
        e.dma_start(out=WQK[:], in_=win["wqk_l0"]).then_inc(s_w, 16)
        e.dma_start(out=WV[:], in_=win["wv_r0"]).then_inc(s_w, 16)
        e.dma_start(out=WO[:], in_=win["wo_l0"]).then_inc(s_w, 16)
    C["ld"] += 8 * 16
    C["w"] += 3 * 16

    @blk.sync
    def _(e):
        e.dma_start(out=RPERM[:], in_=rperm_d).then_inc(s_wh, 16)
        e.dma_start(out=IDENT[:], in_=ident_in).then_inc(s_wh, 16)
    C["wh"] += 2 * 16

    @blk.vector
    def _(e):
        p.svi(e.memset(ONEC[:], 1.0), "v")
        p.svi(e.memset(ONER[:], 1.0), "v")
        p.svi(e.memset(R2[:], 1.0), "v")
        p.svi(e.memset(EPS[:], 1.0e-5), "v")
        p.svi(e.memset(LN8[:], float(np.log(QK_SCALE))), "v")
        p.svi(e.memset(E24[:], 1.0e-24), "v")

    def zsrc(L):
        return x1T if L == 0 else zd[L % 2]

    def ln_fm_g(getin_bf, getout, gb_rows, mmck, n_tok):
        mm0, ev0, ac0 = C["mm"], C["ev"], C["ac"]

        @blk.vector
        def _(e):
            e.wait_ge(s_ev, ev0)
            for dm in range(DC):
                ins = e.tensor_tensor(out=USQ[dm][:, 0:n_tok], in0=getin_bf(dm),
                                      in1=getin_bf(dm), op=ALU.mult)
            ins.then_inc(s_ev, 1)

        @blk.tensor
        def _(e):
            e.wait_ge(s_ev, ev0 + 1)
            for dm in range(DC):
                e.matmul(PF[4][0:1, 0:n_tok], ONEC[:], getin_bf(dm),
                         start=(dm == 0), stop=(dm == DC - 1))
            for dm in range(DC):
                ins = e.matmul(PF[5][0:1, 0:n_tok], ONEC[:], USQ[dm][:, 0:n_tok],
                               start=(dm == 0), stop=(dm == DC - 1))
            ins.then_inc(s_mm, 1)

        @blk.scalar
        def _(e):
            e.wait_ge(s_mm, mm0 + 1)
            e.activation(out=RMU[0:1, 0:n_tok], in_=PF[4][0:1, 0:n_tok],
                         func=AF.Copy, bias=0.0, scale=-1.0 / D
                         ).then_inc(s_ac, 1)

        @blk.vector
        def _(e):
            e.wait_ge(s_ac, ac0 + 1)
            p.svi(e.tensor_tensor(out=RVA[0:1, 0:n_tok], in0=RMU[0:1, 0:n_tok],
                                  in1=RMU[0:1, 0:n_tok], op=ALU.mult), "v")
            p.svw(e, "v")
            e.scalar_tensor_tensor(out=RVA[0:1, 0:n_tok], in0=PF[5][0:1, 0:n_tok],
                                   scalar=1.0 / D, in1=RVA[0:1, 0:n_tok],
                                   op0=ALU.mult, op1=ALU.subtract
                                   ).then_inc(s_ev, 1)

        @blk.scalar
        def _(e):
            e.wait_ge(s_ev, ev0 + 2)
            p.svi(e.activation(out=RVA[0:1, 0:n_tok], in_=RVA[0:1, 0:n_tok],
                               func=AF.Ln, bias=EPS[0:1, 0:1], scale=1.0), "s")
            p.svw(e, "s")
            e.activation(out=RBF[0:1, 0:n_tok], in_=RVA[0:1, 0:n_tok],
                         func=AF.Exp, bias=0.0, scale=-0.5
                         ).then_inc(s_ac, 1)

        @blk.vector
        def _(e):
            e.wait_ge(s_ac, ac0 + 2)
            e.tensor_tensor(out=R2[0:1, 0:n_tok], in0=RMU[0:1, 0:n_tok],
                            in1=RBF[0:1, 0:n_tok], op=ALU.mult
                            ).then_inc(s_ev, 1)

        @blk.tensor
        def _(e):
            e.wait_ge(s_ev, ev0 + 3)
            for dm in range(DC):
                if dm >= 1:
                    e.wait_ge(s_ev, ev0 + 3 + dm)
                if gb_rows is not None:
                    e.matmul(PF[4][:, 0:n_tok], gb_rows[0:1, 128 * dm:128 * (dm + 1)],
                             RBF[0:1, 0:n_tok], start=True, stop=True)
                    ins = e.matmul(PF[5][:, 0:n_tok], gb_rows[0:2, 128 * dm:128 * (dm + 1)],
                                   R2[0:2, 0:n_tok], start=True, stop=True)
                else:
                    e.matmul(PF[4][:, 0:n_tok], ONER[:], RBF[0:1, 0:n_tok],
                             start=True, stop=True)
                    ins = e.matmul(PF[5][:, 0:n_tok], ONER[:], R2[0:1, 0:n_tok],
                                   start=True, stop=True)
                ins.then_inc(s_mm, 1)

        @blk.vector
        def _(e):
            for dm in range(DC):
                e.wait_ge(s_mm, mm0 + 1 + dm + 1)
                p.svi(e.tensor_tensor(out=getout(dm), in0=getin_bf(dm),
                                      in1=PF[4][:, 0:n_tok], op=ALU.mult), "v")
                p.svw(e, "v")
                e.tensor_tensor(out=getout(dm), in0=getout(dm),
                                in1=PF[5][:, 0:n_tok], op=ALU.add
                                ).then_inc(s_ev, 1)
        C["mm"] = mm0 + 9
        C["ev"] = ev0 + 11
        C["ac"] = ac0 + 2

    def emit_layer(L):
        zin = zsrc(L)
        zout = zd[(L + 1) % 2]
        cc1, cc2 = ccs[2 * L], ccs[2 * L + 1]

        ld0 = C["ld"]

        @blk.gpsimd
        def _(e):
            e.wait_ge(s_st, C["st"])
            e.wait_ge(s_sta, C["sta"])
            e.wait_ge(s_ev, C["ev"])
            e.dma_start(out=hz_in.rearrange("p (c t) -> p c t", c=DC),
                        in_=zin[:, T - 257:T].rearrange("(c p) t -> p c t", p=128)
                        ).then_inc(s_dg, 16)
            e.wait_ge(s_dg, C["dg"] + 16)
            e.collective_compute("AllGather", ALU.bypass, replica_groups=RG,
                                 ins=[hz_in.opt()], outs=[hz_out.opt()]
                                 ).then_inc(cc1)
            e.wait_ge(cc1, 1)
            e.dma_start(out=HZALL[:].rearrange("p (r ct) -> p r ct", r=NCORES),
                        in_=hz_out.rearrange("(r p) ct -> p r ct", p=128)
                        ).then_inc(s_ld, 16)
            e.dma_start(out=SCL[:], in_=win["scl%d" % L]).then_inc(s_ld, 16)
            e.dma_start(out=B1A[:], in_=win["b1a_pc%d" % L]).then_inc(s_ld, 16)
            e.dma_start(out=B1G[:], in_=win["b1g_pc%d" % L]).then_inc(s_ld, 16)
            e.dma_start(out=B2F[:], in_=win["b2_pc%d" % L]).then_inc(s_ld, 16)
            e.dma_start(out=GBR[:], in_=win["lngb%d" % L]).then_inc(s_ld, 16)
        C["dg"] += 16
        C["ld"] += 16 * (1 + 5)
        hz_ld = C["ld"]

        hz_ev = C["ev"]

        for a in range(5):
            emit_qkv_step(L, a, zin)
            if a == 0:
                emit_attn_window(L, 0, a, zin, zout)
            elif a < 4:
                emit_attn_window(L, 2 * a - 1, a, zin, zout)
                emit_attn_window(L, 2 * a, a, zin, zout)
            else:
                emit_attn_window(L, 7, a, zin, zout)

        @blk.gpsimd
        def _(e):
            e.wait_ge(s_mm, C["mm"])
            if L + 1 < nlayers:
                e.dma_start(out=WQK[:], in_=win["wqk_l%d" % (L + 1)]
                            ).then_inc(s_w, 16)
                e.dma_start(out=WV[:], in_=win["wv_r%d" % (L + 1)]
                            ).then_inc(s_w, 16)
                e.dma_start(out=WO[:], in_=win["wo_l%d" % (L + 1)]
                            ).then_inc(s_w, 16)
                C["w"] += 48

        tc_ev0 = C["ev"]

        @blk.vector
        def _(e):
            e.wait_ge(s_ev, tc_ev0)
            for i in range(DC):
                ins = e.tensor_copy(out=TCOL[:, i:i + 1], in_=ZOW[i][:, 255:256])
            ins.then_inc(s_ev, 1)
        C["ev"] += 1

        @blk.gpsimd
        def _(e):
            e.wait_ge(s_ev, C["ev"])
            e.dma_start(out=zt_in, in_=TCOL[:]).then_inc(s_dg, 16)
            e.wait_ge(s_dg, C["dg"] + 16)
            e.collective_compute("AllGather", ALU.bypass, replica_groups=RG,
                                 ins=[zt_in.opt()], outs=[zt_out.opt()]
                                 ).then_inc(cc2)
        C["dg"] += 16

        @blk.sync
        def _(e):
            e.wait_ge(cc2, 1)
            e.dma_start(out=ZPALL[:].rearrange("p (r c) -> p r c", c=DC),
                        in_=zt_out.rearrange("(r p) c -> p r c", p=128)
                        ).then_inc(s_dh, 16)
        C["dh"] += 16

        @blk.vector
        def _(e):
            e.wait_ge(s_dh, C["dh"])
            for arm in e.switch_core_id(n=128):
                rr = (int(arm.logical) % NCORES + NCORES - 1) % NCORES
                e.tensor_copy(out=ZP[:],
                              in_=ZPALL[:, rr * DC:(rr + 1) * DC]).then_inc(p.sv["v"][0], 1)
            p.sv["v"][1] += 1
            p.svw(e, "v")
            e.tensor_scalar_mul(out=ZP[:], in0=ZP[:], scalar1=HSC[:, 0:1]
                                ).then_inc(s_ev, 1)
        C["ev"] += 1

        for g in range(2):
            emit_ffn_group(L, g, zout)

    def emit_qkv_step(L, a, zin):
        t0 = -256 + 512 * a
        TT = 512 if a < 4 else 256
        ld0 = C["ld"]

        @blk.gpsimd
        def _(e):
            e.wait_ge(s_ev, C["ev"])
            e.wait_ge(s_mm, C["mm"])
            base = 2 * W + (T * 0) + t0
            for j in range(4):
                e.dma_start(out=TAB[j][:, 0:TT],
                            in_=tabs_d[j, :, 256 + t0:256 + t0 + TT]
                            ).then_inc(s_ld, 16)
            if a == 0:
                for i in range(DC):
                    e.dma_start(out=ZBp[i][:, 257:513], in_=zin[128 * i:128 * (i + 1), 0:256]
                                ).then_inc(s_ld, 16)
            else:
                for i in range(DC):
                    e.dma_start(out=ZBp[i][:, 0:TT + 1],
                                in_=zin[128 * i:128 * (i + 1), t0 - 1:t0 + TT]
                                ).then_inc(s_ld, 16)
        C["ld"] += 16 * (4 + 8)
        zb_ld = C["ld"]


        ev0 = C["ev"]

        @blk.vector
        def _(e):
            e.wait_ge(s_ld, zb_ld)
            if a == 0:
                zv = ZBALL[:].rearrange("p (i t) -> p i t", t=528)[:, :, 0:257]
                for arm in e.switch_core_id(n=128):
                    rr = (int(arm.logical) % NCORES + NCORES - 1) % NCORES
                    e.tensor_copy(
                        out=zv,
                        in_=HZALL[:, rr * DC * 257:(rr + 1) * DC * 257].rearrange(
                            "p (i t) -> p i t", t=257)).then_inc(p.sv["v"][0], 1)
                p.sv["v"][1] += 1
                p.svw(e, "v")
                p.svi(e.tensor_scalar_mul(out=zv, in0=zv, scalar1=HSC[:, 0:1]), "v")
                p.svw(e, "v")
            e.tensor_scalar_mul(out=TAB[0][:, 0:TT], in0=TAB[0][:, 0:TT],
                                scalar1=SCL[:, 0:1])
            e.tensor_scalar_mul(out=TAB[1][:, 0:TT], in0=TAB[1][:, 0:TT],
                                scalar1=SCL[:, 1:2])
            e.tensor_scalar_mul(out=TAB[2][:, 0:TT], in0=TAB[2][:, 0:TT],
                                scalar1=SCL[:, 2:3])
            ins = e.tensor_scalar_mul(out=TAB[3][:, 0:TT], in0=TAB[3][:, 0:TT],
                                      scalar1=SCL[:, 3:4])
            ins.then_inc(s_ev, 1)
        C["ev"] += 1

        @blk.vector
        def _(e):
            e.wait_ge(s_ev, ev0 + 1)
            e.wait_ge(s_mm, C["mm"])
            if a >= 1:
                for i in range(DC):
                    e.tensor_scalar_mul(out=KP[i][:, 0:256], in0=KP[i][:, 768:1024],
                                        scalar1=B2C[:, 0:1])
                for i in range(2):
                    ins = e.tensor_copy(out=VB[i][:], in_=VB[4 + i][:])
                ins.then_inc(s_ev, 1)
            else:
                e.nop().then_inc(s_ev, 1)
        C["ev"] += 1
        cp_ev = C["ev"]

        nq = 16
        cw_snap = C["w"]
        ld_snap = C["ld"]
        mm_entry = C["mm"]
        ac_entry = C["ac"]

        qkd = {}; ssd = {}; rotd = {}; bcd = {}
        mmv = mm_entry
        for it in range(nq + 3):
            if it < nq:
                mmv += 1; qkd[it] = mmv
            if 0 <= it - 1 < nq:
                mmv += 1; ssd[it - 1] = mmv
            if 0 <= it - 2 < nq:
                mmv += 1; rotd[it - 2] = mmv
                mmv += 1; bcd[it - 2] = mmv
        vq3d = {}; evacsqd = {}
        evv = C["ev"]
        for it in range(nq + 2):
            if 0 <= it - 2 < nq:
                evv += 1; vq3d[it - 2] = evv
            if it < nq:
                evv += 1; evacsqd[it] = evv
        ln_done = {}; exp_done = {}
        for m in range(nq // 2):
            ln_done[2 * m] = ac_entry + 4 * m + 1
            ln_done[2 * m + 1] = ac_entry + 4 * m + 2
            exp_done[2 * m] = ac_entry + 4 * m + 3
            exp_done[2 * m + 1] = ac_entry + 4 * m + 4

        def emit_Tq1(i):
            mc = i
            b = i % 2

            @blk.tensor
            def _(e):
                if i == 0:
                    e.wait_ge(s_ev, cp_ev)
                    e.wait_ge(s_w, cw_snap)
                    e.wait_ge(s_wh, 32)
                    e.wait_ge(s_ld, ld_snap)
                if i >= 2:
                    e.wait_ge(s_ev, evacsqd[i - 2])
                for kc in range(DC):
                    sh = 1 if kc < 4 else 0
                    ins = e.matmul(PF[b][:, 0:TT],
                                   WQK[:, mc * 1024 + kc * 128: mc * 1024 + (kc + 1) * 128],
                                   ZBp[kc][:, sh:sh + TT],
                                   start=(kc == 0), stop=(kc == DC - 1))
                ins.then_inc(s_mm, 1)

        def emit_Tq2(j):
            hp = j % 8

            @blk.tensor
            def _(e):
                e.wait_ge(s_ev, evacsqd[j])
                if j >= 1:
                    e.wait_ge(s_ac, ln_done[j - 1])
                e.matmul(PF[4][0:16, 0:TT], E16[:, 16 * hp:16 * (hp + 1)],
                         QSQ[:, 0:TT], start=True, stop=True
                         ).then_inc(s_mm, 1)

        def emit_Trot(k):
            @blk.tensor
            def _(e):
                e.wait_ge(s_ev, evacsqd[k])
                e.matmul(PF[2 + k % 2][:, 0:TT], RPERM[:], QR[k % 2][:, 0:TT],
                         start=True, stop=True).then_inc(s_mm, 1)

        def emit_Tbc(m):
            hp = m % 8
            ET = E16T8 if m < 8 else E16T
            RNS = RN16 if m % 2 == 0 else RN16B

            @blk.tensor
            def _(e):
                e.wait_ge(s_ac, exp_done[m])
                if m >= 1:
                    e.wait_ge(s_ev, vq3d[m - 1])
                e.matmul(PF[5][:, 0:TT], ET[:, 128 * hp:128 * (hp + 1)],
                         RNS[0:16, 0:TT], start=True, stop=True
                         ).then_inc(s_mm, 1)

        for it in range(nq + 3):
            if it < nq:
                emit_Tq1(it)
            if 0 <= it - 1 < nq:
                emit_Tq2(it - 1)
            if 0 <= it - 2 < nq:
                emit_Trot(it - 2)
                emit_Tbc(it - 2)

        def emit_Vq1(i):
            b = i % 2

            @blk.vector
            def _(e):
                if i >= 2:
                    e.wait_ge(s_mm, rotd[i - 2])
                elif i == 1:
                    e.wait_ge(s_mm, ssd[0])
                else:
                    e.wait_ge(s_mm, qkd[0])
                p.svi(e.tensor_copy(out=QR[i % 2][:, 0:TT], in_=PF[b][:, 0:TT]), "v")
                p.svw(e, "v")
                e.tensor_tensor(out=QSQ[:, 0:TT], in0=QR[i % 2][:, 0:TT],
                                in1=QR[i % 2][:, 0:TT], op=ALU.mult
                                ).then_inc(s_ev, 1)

        def emit_Vq3(j):
            hp = j % 8
            qk = j // 8

            @blk.vector
            def _(e):
                e.wait_ge(s_mm, bcd[j])
                p.svi(e.tensor_tensor(out=T3[:, 0:TT], in0=PF[2 + j % 2][:, 0:TT],
                                      in1=TAB[2 * qk + 1][:, 0:TT],
                                      op=ALU.mult), "v")
                p.svi(e.tensor_tensor(out=T1[:, 0:TT], in0=QR[j % 2][:, 0:TT],
                                      in1=TAB[2 * qk][:, 0:TT], op=ALU.mult), "v")
                p.svw(e, "v")
                p.svi(e.tensor_tensor(out=T1[:, 0:TT], in0=T1[:, 0:TT],
                                      in1=T3[:, 0:TT], op=ALU.add), "v")
                p.svw(e, "v")
                if qk == 0:
                    ins = e.tensor_tensor(out=QH[hp][:, 0:TT], in0=T1[:, 0:TT],
                                          in1=PF[5][:, 0:TT], op=ALU.mult)
                else:
                    if a == 4:
                        ins = e.tensor_tensor(out=KP[hp][:, 256:512],
                                              in0=T1[:, 0:256],
                                              in1=PF[5][:, 0:256], op=ALU.mult)
                    else:
                        e.tensor_tensor(out=KP[hp][:, 256:512], in0=T1[:, 0:256],
                                        in1=PF[5][:, 0:256], op=ALU.mult)
                        ins = e.tensor_tensor(out=KP[hp][:, 768:1024],
                                              in0=T1[:, 256:512],
                                              in1=PF[5][:, 256:512], op=ALU.mult)
                ins.then_inc(s_ev, 1)

        for it in range(nq + 2):
            if 0 <= it - 2 < nq:
                emit_Vq3(it - 2)
            if it < nq:
                emit_Vq1(it)

        def emit_Spair(m):
            j0, j1 = 2 * m, 2 * m + 1

            @blk.scalar
            def _(e):
                e.wait_ge(s_mm, ssd[j0])
                e.activation(out=SSR[0:16, 0:TT], in_=PF[4][0:16, 0:TT],
                             func=AF.Ln, bias=E24[0:16, 0:1], scale=1.0
                             ).then_inc(s_ac, 1)
                e.wait_ge(s_mm, ssd[j1])
                e.activation(out=SSR2[0:16, 0:TT], in_=PF[4][0:16, 0:TT],
                             func=AF.Ln, bias=E24[0:16, 0:1], scale=1.0
                             ).then_inc(s_ac, 1)
                e.wait_ge(s_ac, ln_done[j1])
                if m >= 1:
                    e.wait_ge(s_mm, bcd[j0 - 2])
                else:
                    e.wait_ge(s_mm, mm_entry)
                e.activation(out=RN16[0:16, 0:TT], in_=SSR[0:16, 0:TT],
                             func=AF.Exp, bias=0.0, scale=-0.5).then_inc(s_ac, 1)
                if m >= 1:
                    e.wait_ge(s_mm, bcd[j1 - 2])
                e.activation(out=RN16B[0:16, 0:TT], in_=SSR2[0:16, 0:TT],
                             func=AF.Exp, bias=0.0, scale=-0.5).then_inc(s_ac, 1)

        for m in range(nq // 2):
            emit_Spair(m)

        C["mm"] = mmv
        C["ev"] = evv
        C["ac"] = ac_entry + 2 * nq

        ev0 = C["ev"]

        @blk.vector
        def _(e):
            e.wait_ge(s_ev, ev0)
            if a < 4:
                for i in range(DC):
                    ins = e.tensor_scalar_mul(out=KP[i][:, 512:768],
                                              in0=KP[i][:, 256:512],
                                              scalar1=B2C[:, 0:1])
                ins.then_inc(s_ev, 1)
            else:
                e.nop().then_inc(s_ev, 1)
        C["ev"] += 1

        ntc = TT // 128
        for tc in range(ntc):
            mm0, ev0 = C["mm"], C["ev"]

            @blk.tensor
            def _(e):
                for oh in range(2):
                    if tc == 0 and oh == 0:
                        e.wait_ge(s_ev, ev0)
                    elif tc >= 1:
                        e.wait_ge(s_ev, ev0 + oh - 1)
                    for kc in range(DC):
                        sh = 1 if kc < 4 else 0
                        ins = e.matmul(PF[oh][:, 0:512],
                                       ZBp[kc][:, sh + 128 * tc: sh + 128 * (tc + 1)],
                                       WV[:, kc * 1024 + oh * 512: kc * 1024 + (oh + 1) * 512],
                                       start=(kc == 0), stop=(kc == DC - 1))
                    ins.then_inc(s_mm, 1)

            @blk.vector
            def _(e):
                for oh in range(2):
                    e.wait_ge(s_mm, mm0 + oh + 1)
                    e.tensor_copy(out=VB[2 + tc][:, 512 * oh:512 * (oh + 1)],
                                  in_=PF[oh][:, 0:512]).then_inc(s_ev, 1)
            C["mm"] += 2
            C["ev"] += 2

    def emit_attn_window(L, w, a, zin, zout):
        if a == 0:
            pat_hi = True
            qoff = 256
        elif a == 4:
            pat_hi = False
            qoff = 0
        else:
            pat_hi = (w == 2 * a)
            qoff = 0 if w == 2 * a - 1 else 256
        kbase = 512 if pat_hi else 0
        vbase = 2 if pat_hi else 0

        wo_st0 = C["sta"]
        zld0 = C["ld"]

        @blk.gpsimd
        def _(e):
            e.wait_ge(s_st, C["st"])
            e.wait_ge(s_sta, C["sta"])
            e.wait_ge(s_ev, C["ev"])
            for i in range(DC):
                e.dma_start(out=ZRW[i][:],
                            in_=zin[128 * i:128 * (i + 1), 256 * w:256 * (w + 1)]
                            ).then_inc(s_ld, 16)
        C["ld"] += 16 * DC
        zrw_ld = C["ld"]

        first_ev = C["ev"]
        mm_entry = C["mm"]
        ac_entry = C["ac"]
        n = 32
        LOOK = 2

        simd = {}; transd = {}; avd = {}
        mmv = mm_entry
        for i in range(n + LOOK):
            if i < n:
                mmv += 1; simd[i] = mmv
            j = i - LOOK
            if j >= 0:
                mmv += 1; transd[j] = mmv
                mmv += 1; avd[j] = mmv
        maxd = {}; recipd = {}; copyd = {}; aoev = {}
        evv = first_ev
        for i in range(n + 2):
            if i < n:
                evv += 1; maxd[i] = evv
            if 0 <= i - 1 < n:
                evv += 1; recipd[i - 1] = evv
            j = i - 2
            if j >= 0:
                evv += 1; copyd[j] = evv
                if j % 4 == 3:
                    evv += 1; aoev[j // 4] = evv
        expd = {i: ac_entry + i + 1 for i in range(n)}
        pl_entry = C["pl"]
        scaled = {i: pl_entry + i + 1 for i in range(n)}

        def wchain(i):
            hp, r = divmod(i, 4)
            qt, hh = divmod(r, 2)
            return hp, qt, hh

        def emit_T1(i):
            hp, qt, hh = wchain(i)
            hb = 64 * hh
            bs = i % 2
            mask = (MK0 if qt == 0 else MK0B) if w == 0 else MK

            @blk.tensor
            def _(e):
                if i == 0:
                    e.wait_ge(s_ev, first_ev)
                if i < LOOK:
                    e.wait_ge(s_ac, ac_entry)
                else:
                    e.wait_ge(s_ac, expd[i - LOOK])
                e.matmul(PF[bs][:, 0:384], IDENT[:], mask[:],
                         start=True, stop=False)
                e.matmul(PF[bs][:, 0:384],
                         QH[hp][hb:hb + 64, qoff + 128 * qt: qoff + 128 * (qt + 1)],
                         KP[hp][hb:hb + 64, kbase + 128 * qt: kbase + 128 * qt + 384],
                         start=False, stop=True,
                         tile_position=(hb, 0)).then_inc(s_mm, 1)

        def emit_T2T3(j):
            hp, qt, hh = wchain(j)
            hb = 64 * hh
            bs = j % 2
            bt = j % 2

            @blk.tensor
            def _(e):
                e.wait_ge(s_pl, scaled[j])
                if j >= 2:
                    e.wait_ge(s_ev, copyd[j - 2])
                for c in range(3):
                    ins = e.transpose(PB[bt][:, 128 * c:128 * (c + 1)],
                                      PEx[bs][:, 128 * c:128 * (c + 1)],
                                      IDENT[:])
                ins.then_inc(s_mm, 1)

            @blk.tensor
            def _(e):
                e.wait_ge(s_ev, copyd[j])
                for c in range(3):
                    kch = qt + c
                    vslot = vbase + kch
                    hcol = 64 * (2 * hp + hh)
                    ins = e.matmul(PF[2 + hp % 2][hb:hb + 64, 128 * qt:128 * (qt + 1)],
                                   VB[vslot][:, hcol:hcol + 64],
                                   PTS[bt][:, 128 * c:128 * (c + 1)],
                                   start=(c == 0), stop=(c == 2),
                                   tile_position=(0, hb))
                ins.then_inc(s_mm, 1)

        for i in range(n + LOOK):
            if i < n:
                emit_T1(i)
            if i - LOOK >= 0:
                emit_T2T3(i - LOOK)

        def emit_V1(i):
            hp, qt, hh = wchain(i)
            h = 2 * hp + hh
            bs = i % 2

            @blk.vector
            def _(e):
                e.wait_ge(s_mm, simd[i])
                e.tensor_reduce(out=NM[:, h:h + 1], in_=PF[bs][:, 0:384],
                                axis=AX.X, op=ALU.max, negate=True
                                ).then_inc(s_ev, 1)

        def emit_V2(i):
            hp, qt, hh = wchain(i)
            h = 2 * hp + hh

            @blk.vector
            def _(e):
                e.wait_ge(s_ac, expd[i])
                e.reciprocal(out=RI[:, h:h + 1], in_=SS[:, h:h + 1]
                             ).then_inc(s_ev, 1)

        def emit_P1(i):
            hp, qt, hh = wchain(i)
            h = 2 * hp + hh
            bs = i % 2

            @blk.gpsimd
            def _(e):
                e.wait_ge(s_ev, recipd[i])
                e.tensor_scalar_mul(out=PEx[bs][:, 0:384],
                                    in0=PEx[bs][:, 0:384],
                                    scalar1=RI[:, h:h + 1]).then_inc(s_pl, 1)

        def emit_V3(j):
            bt = j % 2

            @blk.vector
            def _(e):
                e.wait_ge(s_mm, transd[j])
                e.tensor_copy(out=PTS[bt][:], in_=PB[bt][:, 0:384]
                              ).then_inc(s_ev, 1)

        def emit_AO(hp):
            @blk.vector
            def _(e):
                e.wait_ge(s_mm, avd[4 * hp + 3])
                e.tensor_copy(out=AO[hp][:], in_=PF[2 + hp % 2][:, 0:256]
                              ).then_inc(s_ev, 1)

        for i in range(n + 2):
            if i < n:
                emit_V1(i)
            if 0 <= i - 1 < n:
                emit_V2(i - 1)
            j = i - 2
            if j >= 0:
                emit_V3(j)
                if j % 4 == 3:
                    emit_AO(j // 4)
        for i in range(n):
            emit_P1(i)
        C["pl"] = pl_entry + n

        def emit_S1(i):
            hp, qt, hh = wchain(i)
            h = 2 * hp + hh
            bs = i % 2

            @blk.scalar
            def _(e):
                e.wait_ge(s_ev, maxd[i])
                if i >= LOOK:
                    e.wait_ge(s_mm, transd[i - LOOK])
                else:
                    e.wait_ge(s_mm, mm_entry)
                e.activation(out=PEx[bs][:, 0:384], in_=PF[bs][:, 0:384],
                             func=AF.Exp, bias=NM[:, h:h + 1], scale=1.0,
                             accum_out=SS[:, h:h + 1]).then_inc(s_ac, 1)

        for i in range(n):
            emit_S1(i)

        C["mm"] = mmv
        C["ev"] = evv
        C["ac"] = ac_entry + n

        ao_ev = C["ev"]
        for dm in range(DC):
            mm0, ev0 = C["mm"], C["ev"]

            @blk.tensor
            def _(e):
                if dm == 0:
                    e.wait_ge(s_ev, ao_ev)
                if dm >= 2:
                    e.wait_ge(s_ev, ev0 - 1)
                for ec in range(DC):
                    ins = e.matmul(PF[4 + dm % 2][:, 0:256],
                                   WO[:, dm * 1024 + ec * 128: dm * 1024 + (ec + 1) * 128],
                                   AO[ec][:], start=(ec == 0), stop=(ec == DC - 1))
                ins.then_inc(s_mm, 1)

            @blk.vector
            def _(e):
                e.wait_ge(s_mm, mm0 + 1)
                if dm == 0:
                    e.wait_ge(s_ld, zrw_ld)
                    e.wait_ge(s_sta, wo_st0)
                e.tensor_add(out=ZOW[dm][:], in0=PF[4 + dm % 2][:, 0:256],
                             in1=ZRW[dm][:]).then_inc(s_ev, 1)
            C["mm"] += 1
            C["ev"] += 1
        zo_ev = C["ev"]

        @blk.sync
        def _(e):
            e.wait_ge(s_ev, zo_ev)
            for i in range(DC):
                e.dma_start(out=zout[128 * i:128 * (i + 1), 256 * w:256 * (w + 1)],
                            in_=ZOW[i][:]).then_inc(s_sta, 16)
        C["sta"] += 16 * DC

    def emit_ffn_group(L, g, zq):
        t0 = 1024 * g

        st_grp0 = C["st"]
        sta_grp0 = C["sta"]

        if g == 1:
            sv_ev = C["ev"]

            @blk.vector
            def _(e):
                e.wait_ge(s_ev, sv_ev)
                for i in range(DC):
                    ins = e.tensor_copy(out=ZPF[:, i:i + 1], in_=FB[i][:, 1024:1025])
                ins.then_inc(s_ev, 1)
            C["ev"] += 1

        @blk.gpsimd
        def _(e):
            e.wait_ge(s_st, st_grp0)
            e.wait_ge(s_sta, sta_grp0)
            e.wait_ge(s_ev, C["ev"])
            e.wait_ge(s_mm, C["mm"])
            for i in range(DC):
                e.dma_start(out=FB[i][:, 1:1025],
                            in_=zq[128 * i:128 * (i + 1), t0:t0 + 1024]
                            ).then_inc(s_ld, 16)
            for j in range(2):
                e.dma_start(out=W1S[j][:], in_=win["w1a_l%d" % L][j]).then_inc(s_w1[j], 16)
                e.dma_start(out=W1S[2 + j][:], in_=win["w1g_l%d" % L][j]).then_inc(s_w1[j], 16)
        C["ld"] += 16 * DC
        C["w1"][0] += 32
        C["w1"][1] += 32
        w1_snap = [C["w1"][0], C["w1"][1]]
        fb_ld = C["ld"]

        @blk.vector
        def _(e):
            e.wait_ge(s_ld, fb_ld)
            for i in range(DC):
                ins = e.tensor_copy(out=FB[i][:, 0:1],
                                    in_=ZP[:, i:i + 1] if g == 0 else ZPF[:, i:i + 1])
            ins.then_inc(s_ev, 1)
        C["ev"] += 1

        for u in range(2):
            def gin(dm, u=u):
                sh = 0 if dm < 4 else -1
                return FB[dm][:, 1 + sh + 512 * u: 1 + sh + 512 * u + 512]

            def gout(dm, u=u):
                return XN[dm][:, 512 * u:512 * (u + 1)]
            ln_fm_g(gin, gout, GBR, None, 512)
        xn_ev = C["ev"]

        for mc in range(DIC):
            for sub in range(2):
                j = mc * 2 + sub
                mm0, ev0, ac0 = C["mm"], C["ev"], C["ac"]
                pa = j % 2
                pg = 2 + j % 2

                @blk.tensor
                def _(e):
                    if j == 0:
                        e.wait_ge(s_ev, xn_ev)
                    e.wait_ge(s_w1[mc % 2], w1_snap[mc % 2] + 32 * (mc // 2))
                    if j >= 2:
                        e.wait_ge(s_ac, ac0 - 1)
                        e.wait_ge(s_ev, ev0 - 1)
                    for kc in range(DC):
                        e.matmul(PF[pa][:],
                                 W1S[mc % 2][:, kc * 128:(kc + 1) * 128],
                                 XN[kc][:, 512 * sub:512 * (sub + 1)],
                                 start=(kc == 0), stop=(kc == DC - 1))
                    for kc in range(DC):
                        ins = e.matmul(PF[pg][:],
                                       W1S[2 + mc % 2][:, kc * 128:(kc + 1) * 128],
                                       XN[kc][:, 512 * sub:512 * (sub + 1)],
                                       start=(kc == 0), stop=(kc == DC - 1))
                    ins.then_inc(s_mm, 1)

                @blk.scalar
                def _(e):
                    e.wait_ge(s_mm, mm0 + 1)
                    e.activation(out=GELO[j % 2][:], in_=PF[pg][:], func=AF.Gelu,
                                 bias=B1G[:, mc:mc + 1], scale=1.0
                                 ).then_inc(s_ac, 1)

                @blk.vector
                def _(e):
                    e.wait_ge(s_ac, ac0 + 1)
                    e.scalar_tensor_tensor(out=HT[mc][:, 512 * sub:512 * (sub + 1)],
                                           in0=PF[pa][:],
                                           scalar=B1A[:, mc:mc + 1],
                                           in1=GELO[j % 2][:],
                                           op0=ALU.add, op1=ALU.mult
                                           ).then_inc(s_ev, 1)
                C["mm"] = mm0 + 1
                C["ev"] = ev0 + 1
                C["ac"] = ac0 + 1

            @blk.gpsimd
            def _(e):
                e.wait_ge(s_mm, C["mm"])
                if mc + 2 < DIC:
                    e.dma_start(out=W1S[mc % 2][:],
                                in_=win["w1a_l%d" % L][mc + 2]).then_inc(s_w1[mc % 2], 16)
                    e.dma_start(out=W1S[2 + mc % 2][:],
                                in_=win["w1g_l%d" % L][mc + 2]).then_inc(s_w1[mc % 2], 16)
                    C["w1"][mc % 2] += 32
        h_ev = C["ev"]


        @blk.gpsimd
        def _(e):
            e.wait_ge(s_ev, C["ev"])
            e.wait_ge(s_st, st_grp0)
            for i in range(2):
                e.dma_start(out=W2S[i][:], in_=win["w2_l%d" % L][i]).then_inc(s_w2[i], 16)
        C["w2"][0] += 16
        C["w2"][1] += 16
        w2_snap = [C["w2"][0], C["w2"][1]]

        for dm in range(DC):
            for sub in range(2):
                j = dm * 2 + sub
                mm0, ev0 = C["mm"], C["ev"]
                pb_ = j % 4

                @blk.tensor
                def _(e):
                    if j == 0:
                        e.wait_ge(s_ev, h_ev)
                    e.wait_ge(s_w2[dm % 2], w2_snap[dm % 2] + 16 * (dm // 2))
                    if j >= 4:
                        e.wait_ge(s_ev, ev0 - 3)
                    for kc in range(DIC):
                        ins = e.matmul(PF[pb_][:],
                                       W2S[dm % 2][:, kc * 128:(kc + 1) * 128],
                                       HT[kc][:, 512 * sub:512 * (sub + 1)],
                                       start=(kc == 0), stop=(kc == DIC - 1))
                    ins.then_inc(s_mm, 1)

                @blk.vector
                def _(e):
                    e.wait_ge(s_mm, mm0 + 1)
                    e.wait_ge(s_st, C["st"])
                    e.tensor_scalar_add(out=ZO[dm][:],
                                        in0=PF[pb_][:], scalar1=B2F[:, dm:dm + 1]
                                        ).then_inc(s_ev, 1)
                C["mm"] = mm0 + 1
                C["ev"] = ev0 + 1

                @blk.gpsimd
                def _(e):
                    e.wait_ge(s_ev, C["ev"])
                    e.dma_start(out=zq[128 * dm:128 * (dm + 1),
                                       t0 + 512 * sub:t0 + 512 * (sub + 1)],
                                in_=ZO[dm][:], accum_op=ALU.add).then_inc(s_st, 16)
                C["st"] += 16

            @blk.gpsimd
            def _(e):
                e.wait_ge(s_mm, C["mm"])
                if dm + 2 < DC:
                    e.dma_start(out=W2S[dm % 2][:],
                                in_=win["w2_l%d" % L][dm + 2]).then_inc(s_w2[dm % 2], 16)
                    C["w2"][dm % 2] += 16

    for L in range(nlayers):
        emit_layer(L)

    zfin = zd[nlayers % 2] if nlayers > 0 else x1T
    for g4 in range(4):
        t0 = 512 * g4

        @blk.gpsimd
        def _(e):
            e.wait_ge(s_st, C["st"])
            e.wait_ge(s_sta, C["sta"])
            e.wait_ge(s_ev, C["ev"])
            e.wait_ge(s_mm, C["mm"])
            for i in range(DC):
                e.dma_start(out=ZO[i][:],
                            in_=zfin[128 * i:128 * (i + 1), t0:t0 + 512]
                            ).then_inc(s_ld, 16)
            for i in range(DC):
                e.dma_start(out=F2[i][:],
                            in_=x1T[128 * i:128 * (i + 1), t0:t0 + 512]
                            ).then_inc(s_ld, 16)
        C["ld"] += 16 * 2 * DC
        f_ld = C["ld"]

        @blk.vector
        def _(e):
            e.wait_ge(s_ld, f_ld)
            for i in range(DC):
                p.svi(e.tensor_add(out=ZO[i][:], in0=ZO[i][:], in1=F2[i][:]), "v")
            p.svw(e, "v")
            for i in range(DC):
                ins = e.tensor_copy(out=UB[i][:], in_=ZO[i][:])
            ins.then_inc(s_ev, 1)
        C["ev"] += 1

        def ginf(dm):
            return UB[dm][:]

        def goutf(dm):
            return ZO[dm][:]
        ln_fm_g(ginf, goutf, None, None, 512)
        fin_ev = C["ev"]

        @blk.sync
        def _(e):
            e.wait_ge(s_ev, fin_ev)
            for i in range(DC):
                e.dma_start(out=x2T[128 * i:128 * (i + 1), t0:t0 + 512],
                            in_=ZO[i][:]).then_inc(s_sta, 16)
        C["sta"] += 16 * DC

    @blk.sync
    def _(e):
        e.wait_ge(s_sta, C["sta"])
        e.wait_ge(s_st, C["st"])

    return [(s_sta, C["sta"]), (s_st, C["st"]), (s_ev, C["ev"]),
            (s_mm, C["mm"]), (s_ac, C["ac"]), (s_ld, C["ld"]),
            (s_w, C["w"]), (s_pl, C["pl"]),
            (s_w1[0], C["w1"][0]), (s_w1[1], C["w1"][1]),
            (s_w2[0], C["w2"][0]), (s_w2[1], C["w2"][1])]


def build_l0ffn(nlayers=6):
    p = Prog()
    dio = decl_l0ffn(p, nlayers)
    PFG = [p.ps(f"PFg{i}", [128, 512], F32) for i in range(6)]
    PBG = [p.ps(f"PBg{i}", [128, 1024], BF16) for i in range(2)]
    p.begin()
    emit_l0ffn(p, dio, PFG, PBG, nlayers)
    p.end()
    return p.nc


def _prep_l0ffn_maps(inputs, nlayers=6):
    sh = _prep_l0ffn_shared(inputs)
    tabs = np.stack([sh.pop("_tab_csq"), sh.pop("_tab_ssq"),
                     sh.pop("_tab_csk"), sh.pop("_tab_ssk")])
    maskW = sh.pop("_maskW").astype(BF)
    mask0a = sh.pop("_maskW0A").astype(BF)
    mask0b = sh.pop("_maskW0B").astype(BF)
    shared = {"maskW": maskW, "rperm": sh["rperm"], "e16": sh["e16"],
              "e16t": sh["e16t"], "e16t8": sh["e16t8"],
              "base2_c": sh["base2_c"],
              "ident": np.eye(128, dtype=BF)}
    for L in range(nlayers):
        for nm in ["wqk_l", "wv_r", "wo_l", "w1a_l", "w1g_l", "w2_l",
                   "b1a_pc", "b1g_pc", "b2_pc", "lngb"]:
            shared[nm + str(L)] = sh[nm + str(L)]
        shared["scl%d" % L] = np.concatenate(
            [sh[f"qs_c{L}"], sh[f"qsr_c{L}"], sh[f"ks_c{L}"], sh[f"ksr_c{L}"]],
            axis=1).astype(np.float32)
    maps = []
    for c in range(NCORES):
        m = dict(shared)
        s0 = T * c
        sl = tabs[:, s0 - 256 + 256: s0 + T + 256, :]
        slT = np.ascontiguousarray(np.tile(sl.transpose(0, 2, 1), (1, 2, 1))
                                   ).astype(np.float32)
        pad = np.zeros((4, 128, 2 * W + T - slT.shape[2]), np.float32)
        m["tabs"] = np.concatenate([slT, pad], axis=2)
        m["mask0"] = mask0a if c == 0 else maskW
        m["mask0b"] = mask0b if c == 0 else maskW
        m["hsc"] = np.full((128, 1), 0.0 if c == 0 else 1.0, np.float32)
        maps.append(m)
    return maps


def run_l0ffn(inputs, x1_list, nlayers=6):
    nc = build_l0ffn(nlayers)
    maps = _prep_l0ffn_maps(inputs, nlayers)
    for c in range(NCORES):
        maps[c]["x1T"] = np.ascontiguousarray(x1_list[c])
    res = run_bass_kernel_spmd(nc, maps, core_ids=list(range(NCORES)))
    return [r["x2T"] for r in res.results]



def decl_l1(p, fused=False):
    dio = {}
    if not fused:
        dio["x2T"] = p.din("x2T", [D, T])
        dio["hsc"] = p.din("hsc", [128, 1])
    dio["wth_l"] = p.din("wth_l", [128, 2 * D])
    dio["wph_r"] = p.din("wph_r", [128, DC * 256])
    dio["wg_r"] = p.din("wg_r", [128, DC * D])
    dio["wr_l"] = p.din("wr_l", [128, DC * D])
    dio["wk1_l"] = p.din("wk1_l", [128, DC * D])
    dio["wv1_l"] = p.din("wv1_l", [128, DC * D])
    dio["mur_pc"] = p.din("mur_pc", [128, DC])
    dio["muk_pc"] = p.din("muk_pc", [128, DC])
    dio["xoT"] = p.dout("xoT", [D, T])
    dio["c2_in"] = p.dint("c2_in", [256, D])
    dio["c2_out"] = p.dint("c2_out", [256, D], addr_space="Shared")
    dio["xt_in"] = p.dint("xt_in", [128, DC])
    dio["xt_out"] = p.dint("xt_out", [NCORES * 128, DC], addr_space="Shared")
    return dio


def emit_l1(p, dio, PFG, PBG, prev=None):
    x2T = dio["x2T"]; wth_l = dio["wth_l"]; wph_r = dio["wph_r"]
    wg_r = dio["wg_r"]; wr_l = dio["wr_l"]; wk_l = dio["wk1_l"]
    wv_l = dio["wv1_l"]; mur_pc = dio["mur_pc"]; muk_pc = dio["muk_pc"]
    hsc_d = dio["hsc"]; xoT = dio["xoT"]
    c2_in = dio["c2_in"]; c2_out = dio["c2_out"]
    xt_in = dio["xt_in"]; xt_out = dio["xt_out"]

    X2 = [p.sb(f"X2_{i}", [128, T + 8], F32) for i in range(DC)]
    X2B = [p.sb(f"X2B{i}", [128, T], BF16) for i in range(DC)]
    AR = p.sb("ARN1", [128, 81952], mybir.dt.uint8)

    def at(name, shape, dtype, off):
        return p.sb_at(name, shape, dtype, AR, off)
    THW = at("THW", [128, 2048], BF16, 0)
    THT = [at(f"THT{i}", [128, T], BF16, 4096 + 4096 * i) for i in range(2)]
    PH = [at(f"PH{i}", [128, 256], BF16, 12288 + 512 * i) for i in range(16)]
    GT = [at(f"GT{i}", [128, 1024], BF16, 20480 + 2048 * i) for i in range(16)]
    WGr = at("WGr", [128, DC * D], BF16, 53248)
    WPH = at("WPH", [128, 2048], BF16, 69632)
    WR = at("WR", [128, DC * D], BF16, 0)
    WK2 = at("WK2", [128, DC * D], BF16, 16384)
    WV2 = at("WV2", [128, DC * D], BF16, 32768)
    XRB = [at(f"XRB{i}", [128, 512], BF16, 49152 + 1024 * i) for i in range(DC)]
    XKB = [at(f"XKB{i}", [128, 512], BF16, 57344 + 1024 * i) for i in range(DC)]
    RS = [at(f"RS{i}", [128, 512], BF16, 65536 + 1024 * i) for i in range(DC)]
    KA = [at(f"KA{i}", [128, 512], BF16, 73728 + 1024 * i) for i in range(DC)]
    C2B = p.sb("C2B", [128, 2 * D], BF16)
    CS = [p.sb(f"CS{i}", [128, 512], F32) for i in range(2)]
    U = [p.sb_at(f"U1_{i}", [128, 512], F32, X2B[i], 0) for i in range(DC)]
    UB = [p.sb_at(f"UB1_{i}", [128, 512], BF16, X2B[i], 2048) for i in range(DC)]
    USQ = [p.sb_at(f"USQ1_{i}", [128, 512], BF16, X2B[i], 3072) for i in range(DC)]
    MUR = p.sb("MUR", [128, DC], F32)
    MUK = p.sb("MUK", [128, DC], F32)
    HSC = p.sb("HSC1", [128, 1], F32)
    TC2 = p.sb("TC2", [128, DC], F32)
    ZPA = p.sb("ZPA1", [128, 64], F32)
    ZP3 = p.sb("ZP31", [128, DC], F32)
    RMU = p.sb("RMU1", [1, 512], F32)
    RVA = p.sb("RVA1", [1, 512], F32)
    RBF = p.sb("RBF1", [1, 512], BF16)
    R2 = p.sb("R2_1", [2, 512], BF16)
    EPS = p.sb("EPS1", [1, 1], F32)
    ONEC = p.sb("ONEC1", [128, 1], BF16)
    ONER = p.sb("ONER1", [1, 128], BF16)

    PF = PFG

    s_ld = p.sem("m_ld")
    s_ldh = p.sem("m_ldh")
    s_w = p.sem("m_w")
    s_mm = p.sem("m_mm")
    s_ev = p.sem("m_ev")
    s_ac = p.sem("m_ac")
    s_dg = p.sem("m_dg")
    s_st = p.sem("m_st")
    cc1 = p.sem("m_cc1")
    cc2 = p.sem("m_cc2")
    s_cs = [p.sem("m_cs0"), p.sem("m_cs1")]

    blk = p.block
    if prev:
        emit_barrier(blk, prev)
    C = {"mm": 0, "ev": 0, "ac": 0, "ld": 0, "ldh": 0, "w": 0, "st": 0}

    @blk.gpsimd
    def _(e):
        for i in range(DC):
            e.dma_start(out=X2B[i][:], in_=x2T[128 * i:128 * (i + 1), :]).then_inc(s_ld, 16)
        e.dma_start(out=MUR[:], in_=mur_pc).then_inc(s_ld, 16)
        e.dma_start(out=MUK[:], in_=muk_pc).then_inc(s_ld, 16)
        e.dma_start(out=HSC[:], in_=hsc_d).then_inc(s_ld, 16)
        e.dma_start(out=THW[:], in_=wth_l).then_inc(s_w, 16)
        e.dma_start(out=WPH[:], in_=wph_r).then_inc(s_w, 16)
        e.dma_start(out=WGr[:], in_=wg_r).then_inc(s_w, 16)
    C["ld"] += 16 * (DC + 3)
    C["w"] += 48

    @blk.sync
    def _(e):
        for i in range(DC):
            e.dma_start(out=X2[i][:, 1:T + 1], in_=x2T[128 * i:128 * (i + 1), :]
                        ).then_inc(s_ldh, 16)
    C["ldh"] += 16 * DC

    @blk.vector
    def _(e):
        p.svi(e.memset(ONEC[:], 1.0), "v")
        p.svi(e.memset(ONER[:], 1.0), "v")
        p.svi(e.memset(R2[:], 1.0), "v")
        p.svi(e.memset(EPS[:], 1.0e-5), "v")

    def ln1(getin_bf, getout, n_tok=512):
        mm0, ev0, ac0 = C["mm"], C["ev"], C["ac"]

        @blk.vector
        def _(e):
            e.wait_ge(s_ev, ev0)
            for dm in range(DC):
                ins = e.tensor_tensor(out=USQ[dm][:, 0:n_tok], in0=getin_bf(dm),
                                      in1=getin_bf(dm), op=ALU.mult)
            ins.then_inc(s_ev, 1)

        @blk.tensor
        def _(e):
            e.wait_ge(s_ev, ev0 + 1)
            for dm in range(DC):
                e.matmul(PF[4][0:1, 0:n_tok], ONEC[:], getin_bf(dm),
                         start=(dm == 0), stop=(dm == DC - 1))
            for dm in range(DC):
                ins = e.matmul(PF[5][0:1, 0:n_tok], ONEC[:], USQ[dm][:, 0:n_tok],
                               start=(dm == 0), stop=(dm == DC - 1))
            ins.then_inc(s_mm, 1)

        @blk.scalar
        def _(e):
            e.wait_ge(s_mm, mm0 + 1)
            e.activation(out=RMU[0:1, 0:n_tok], in_=PF[4][0:1, 0:n_tok],
                         func=AF.Copy, bias=0.0, scale=-1.0 / D).then_inc(s_ac, 1)

        @blk.vector
        def _(e):
            e.wait_ge(s_ac, ac0 + 1)
            p.svi(e.tensor_tensor(out=RVA[0:1, 0:n_tok], in0=RMU[0:1, 0:n_tok],
                                  in1=RMU[0:1, 0:n_tok], op=ALU.mult), "v")
            p.svw(e, "v")
            e.scalar_tensor_tensor(out=RVA[0:1, 0:n_tok], in0=PF[5][0:1, 0:n_tok],
                                   scalar=1.0 / D, in1=RVA[0:1, 0:n_tok],
                                   op0=ALU.mult, op1=ALU.subtract).then_inc(s_ev, 1)

        @blk.scalar
        def _(e):
            e.wait_ge(s_ev, ev0 + 2)
            p.svi(e.activation(out=RVA[0:1, 0:n_tok], in_=RVA[0:1, 0:n_tok],
                               func=AF.Ln, bias=EPS[0:1, 0:1], scale=1.0), "s")
            p.svw(e, "s")
            e.activation(out=RBF[0:1, 0:n_tok], in_=RVA[0:1, 0:n_tok],
                         func=AF.Exp, bias=0.0, scale=-0.5).then_inc(s_ac, 1)

        @blk.vector
        def _(e):
            e.wait_ge(s_ac, ac0 + 2)
            e.tensor_tensor(out=R2[0:1, 0:n_tok], in0=RMU[0:1, 0:n_tok],
                            in1=RBF[0:1, 0:n_tok], op=ALU.mult).then_inc(s_ev, 1)

        @blk.tensor
        def _(e):
            e.wait_ge(s_ev, ev0 + 3)
            for dm in range(DC):
                if dm >= 1:
                    e.wait_ge(s_ev, ev0 + 3 + dm)
                e.matmul(PF[4][:, 0:n_tok], ONER[:], RBF[0:1, 0:n_tok],
                         start=True, stop=True)
                ins = e.matmul(PF[5][:, 0:n_tok], ONER[:], R2[0:1, 0:n_tok],
                               start=True, stop=True)
                ins.then_inc(s_mm, 1)

        @blk.vector
        def _(e):
            for dm in range(DC):
                e.wait_ge(s_mm, mm0 + 1 + dm + 1)
                p.svi(e.tensor_tensor(out=getout(dm), in0=getin_bf(dm),
                                      in1=PF[4][:, 0:n_tok], op=ALU.mult), "v")
                p.svw(e, "v")
                e.tensor_tensor(out=getout(dm), in0=getout(dm),
                                in1=PF[5][:, 0:n_tok], op=ALU.add).then_inc(s_ev, 1)
        C["mm"] = mm0 + 9
        C["ev"] = ev0 + 11
        C["ac"] = ac0 + 2

    th_mm0 = C["mm"]

    @blk.tensor
    def _(e):
        e.wait_ge(s_ld, 16 * (DC + 3))
        e.wait_ge(s_w, 48)
        p.svw(e, "v")
        for lc in range(2):
            for tt in range(4):
                i = lc * 4 + tt
                b = i % 2
                if i >= 2:
                    e.wait_ge(s_ev, i - 1)
                for kc in range(DC):
                    ins = e.matmul(PF[b][:], THW[:, lc * 1024 + kc * 128:lc * 1024 + (kc + 1) * 128],
                                   X2B[kc][:, 512 * tt:512 * (tt + 1)],
                                   start=(kc == 0), stop=(kc == DC - 1))
                ins.then_inc(s_mm, 1)

    @blk.vector
    def _(e):
        for lc in range(2):
            for tt in range(4):
                i = lc * 4 + tt
                b = i % 2
                e.wait_ge(s_mm, i + 1)
                e.tensor_copy(out=THT[lc][:, 512 * tt:512 * (tt + 1)],
                              in_=PF[b][:]).then_inc(s_ev, 1)

    @blk.tensor
    def _(e):
        for tc in range(16):
            b = 2 + tc % 2
            if tc >= 2:
                e.wait_ge(s_ev, 8 + tc - 1)
            for kc in range(DC):
                ins = e.matmul(PF[b][:, 0:256], X2B[kc][:, 128 * tc:128 * (tc + 1)],
                               WPH[:, kc * 256:(kc + 1) * 256],
                               start=(kc == 0), stop=(kc == DC - 1))
            ins.then_inc(s_mm, 1)

    @blk.vector
    def _(e):
        for tc in range(16):
            b = 2 + tc % 2
            e.wait_ge(s_mm, 8 + tc + 1)
            e.tensor_copy(out=PH[tc][:], in_=PF[b][:, 0:256]).then_inc(s_ev, 1)

    @blk.tensor
    def _(e):
        for tc in range(16):
            for oh in range(2):
                i = tc * 2 + oh
                b = 4 + i % 2
                if i >= 2:
                    e.wait_ge(s_ev, 24 + i - 1)
                for kc in range(DC):
                    ins = e.matmul(PF[b][:], X2B[kc][:, 128 * tc:128 * (tc + 1)],
                                   WGr[:, kc * 1024 + oh * 512:kc * 1024 + (oh + 1) * 512],
                                   start=(kc == 0), stop=(kc == DC - 1))
                ins.then_inc(s_mm, 1)

    @blk.vector
    def _(e):
        for tc in range(16):
            for oh in range(2):
                i = tc * 2 + oh
                b = 4 + i % 2
                e.wait_ge(s_mm, 24 + i + 1)
                e.tensor_copy(out=GT[tc][:, 512 * oh:512 * (oh + 1)],
                              in_=PF[b][:]).then_inc(s_ev, 1)
    C["mm"] = 56
    C["ev"] = 56
    @blk.tensor
    def _(e):
        e.wait_ge(s_ev, 56)
        for i in range(4):
            lc, oh = divmod(i, 2)
            b = i % 2
            if i >= 2:
                e.wait_ge(s_ev, 56 + i - 1)
            for tc in range(16):
                ins = e.matmul(PF[b][:], PH[tc][:, 128 * lc:128 * (lc + 1)],
                               GT[tc][:, 512 * oh:512 * (oh + 1)],
                               start=(tc == 0), stop=(tc == 15))
            ins.then_inc(s_mm, 1)

    @blk.vector
    def _(e):
        for i in range(4):
            b = i % 2
            e.wait_ge(s_mm, 56 + i + 1)
            if i >= 2:
                e.wait_ge(s_cs[b], 16)
            e.tensor_copy(out=CS[b][:], in_=PF[b][:]).then_inc(s_ev, 1)

    @blk.gpsimd
    def _(e):
        for i in range(4):
            lc, oh = divmod(i, 2)
            b = i % 2
            e.wait_ge(s_ev, 56 + i + 1)
            e.dma_start(out=c2_in[128 * lc:128 * (lc + 1), 512 * oh:512 * (oh + 1)],
                        in_=CS[b][:]).then_inc(s_cs[b], 16)
        e.wait_ge(s_cs[0], 32)
        e.wait_ge(s_cs[1], 32)
        e.collective_compute("AllReduce", ALU.add, replica_groups=RG,
                             ins=[c2_in.opt()], outs=[c2_out.opt()]).then_inc(cc1)
        e.wait_ge(cc1, 1)
        for lc in range(2):
            e.dma_start(out=C2B[:, 1024 * lc:1024 * (lc + 1)],
                        in_=c2_out[128 * lc:128 * (lc + 1), :]).then_inc(s_ld, 16)
    C["ld"] += 16 * 2
    C["mm"] = 60
    C["ev"] = 60

    @blk.vector
    def _(e):
        e.wait_ge(s_ld, C["ld"])
        for lc in range(2):
            ins = e.tensor_scalar_mul(out=C2B[:, 1024 * lc:1024 * (lc + 1)],
                                      in0=C2B[:, 1024 * lc:1024 * (lc + 1)],
                                      scalar1=1.0 / 32.0)
        ins.then_inc(s_ev, 1)
    C["ev"] = 61

    for tt in range(4):
        mm0, ev0 = C["mm"], C["ev"]

        @blk.tensor
        def _(e):
            if tt == 0:
                e.wait_ge(s_ev, 61)
            for oc in range(DC):
                b = oc % 2
                if oc >= 2:
                    e.wait_ge(s_ev, ev0 + oc - 1)
                elif tt > 0:
                    e.wait_ge(s_ev, ev0)
                for lc in range(2):
                    ins = e.matmul(PF[b][:], C2B[:, lc * 1024 + oc * 128:lc * 1024 + (oc + 1) * 128],
                                   THT[lc][:, 512 * tt:512 * (tt + 1)],
                                   start=(lc == 0), stop=(lc == 1))
                ins.then_inc(s_mm, 1)

        @blk.vector
        def _(e):
            for oc in range(DC):
                b = oc % 2
                e.wait_ge(s_mm, mm0 + oc + 1)
                if tt == 0 and oc == 0:
                    e.wait_ge(s_ldh, 16 * DC)
                e.tensor_add(out=U[oc][:], in0=PF[b][:],
                             in1=X2[oc][:, 1 + 512 * tt:1 + 512 * (tt + 1)]
                             ).then_inc(s_ev, 1)
        C["mm"] = mm0 + 8
        C["ev"] = ev0 + 8

        @blk.vector
        def _(e):
            e.wait_ge(s_ev, C["ev"])
            for dm in range(DC):
                ins = e.tensor_copy(out=UB[dm][:], in_=U[dm][:])
            ins.then_inc(s_ev, 1)
        C["ev"] += 1

        def gi3(dm):
            return UB[dm][:]

        def go3(dm, tt=tt):
            return X2[dm][:, 1 + 512 * tt:1 + 512 * (tt + 1)]
        ln1(gi3, go3)

    x3_ev = C["ev"]

    @blk.vector
    def _(e):
        e.wait_ge(s_ev, x3_ev)
        for i in range(DC):
            ins = e.tensor_copy(out=TC2[:, i:i + 1], in_=X2[i][:, T:T + 1])
        ins.then_inc(s_ev, 1)
    C["ev"] += 1

    @blk.gpsimd
    def _(e):
        e.wait_ge(s_ev, C["ev"])
        e.dma_start(out=xt_in, in_=TC2[:]).then_inc(s_dg, 16)
        e.wait_ge(s_dg, 16)
        e.collective_compute("AllGather", ALU.bypass, replica_groups=RG,
                             ins=[xt_in.opt()], outs=[xt_out.opt()]).then_inc(cc2)
        e.wait_ge(s_mm, C["mm"])
        e.dma_start(out=WR[:], in_=wr_l).then_inc(s_w, 16)
        e.dma_start(out=WK2[:], in_=wk_l).then_inc(s_w, 16)
        e.dma_start(out=WV2[:], in_=wv_l).then_inc(s_w, 16)
    C["w"] += 48

    @blk.sync
    def _(e):
        e.wait_ge(cc2, 1)
        e.dma_start(out=ZPA[:].rearrange("p (r c) -> p r c", c=DC),
                    in_=xt_out.rearrange("(r p) c -> p r c", p=128)).then_inc(s_ldh, 16)
    C["ldh"] += 16
    zp_ld = C["ldh"]

    @blk.vector
    def _(e):
        e.wait_ge(s_ldh, zp_ld)
        for arm in e.switch_core_id(n=128):
            rr = (int(arm.logical) % NCORES + NCORES - 1) % NCORES
            e.tensor_copy(out=ZP3[:], in_=ZPA[:, rr * DC:(rr + 1) * DC]).then_inc(p.sv["v"][0], 1)
        p.sv["v"][1] += 1
        p.svw(e, "v")
        p.svi(e.tensor_scalar_mul(out=ZP3[:], in0=ZP3[:], scalar1=HSC[:, 0:1]), "v")
        p.svw(e, "v")
        for i in range(DC):
            ins = e.tensor_copy(out=X2[i][:, 0:1], in_=ZP3[:, i:i + 1])
        ins.then_inc(s_ev, 1)
    C["ev"] += 1

    for tt in range(4):
        t0 = 512 * tt
        mm0, ev0, ac0 = C["mm"], C["ev"], C["ac"]

        @blk.vector
        def _(e):
            e.wait_ge(s_ev, ev0)
            for i in range(DC):
                p.svi(e.tensor_sub(out=U[i][:], in0=X2[i][:, 1 + t0:1 + t0 + 512],
                                   in1=X2[i][:, t0:t0 + 512]), "v")
                p.svw(e, "v")
                e.scalar_tensor_tensor(out=XRB[i][:], in0=U[i][:],
                                       scalar=MUR[:, i:i + 1],
                                       in1=X2[i][:, t0:t0 + 512],
                                       op0=ALU.mult, op1=ALU.add)
                ins = e.scalar_tensor_tensor(out=XKB[i][:], in0=U[i][:],
                                             scalar=MUK[:, i:i + 1],
                                             in1=X2[i][:, t0:t0 + 512],
                                             op0=ALU.mult, op1=ALU.add)
            ins.then_inc(s_ev, 1)
        C["ev"] += 1

        for which in range(2):
            for mc in range(DC):
                mm0b, ev0b, ac0b = C["mm"], C["ev"], C["ac"]
                b = which * 2 + mc % 2

                @blk.tensor
                def _(e):
                    if tt == 0 and which == 0 and mc == 0:
                        e.wait_ge(s_w, C["w"])
                    e.wait_ge(s_ev, C["ev"])
                    if mc >= 2:
                        e.wait_ge(s_ac, ac0b - 1)
                    W_ = WR if which == 0 else WK2
                    for kc in range(DC):
                        ins = e.matmul(PF[b][:], W_[:, mc * 1024 + kc * 128:mc * 1024 + (kc + 1) * 128],
                                       (XRB if which == 0 else XKB)[kc][:],
                                       start=(kc == 0), stop=(kc == DC - 1))
                    ins.then_inc(s_mm, 1)

                @blk.scalar
                def _(e):
                    e.wait_ge(s_mm, mm0b + 1)
                    e.activation(out=(RS if which == 0 else KA)[mc][:], in_=PF[b][:],
                                 func=AF.Sigmoid if which == 0 else AF.Relu,
                                 bias=0.0, scale=1.0).then_inc(s_ac, 1)
                C["mm"] = mm0b + 1
                C["ac"] = ac0b + 1

        @blk.vector
        def _(e):
            e.wait_ge(s_ac, C["ac"])
            for i in range(DC):
                ins = e.tensor_tensor(out=KA[i][:], in0=KA[i][:], in1=KA[i][:],
                                      op=ALU.mult)
            ins.then_inc(s_ev, 1)
        C["ev"] += 1
        ka_ev = C["ev"]

        for mc in range(DC):
            mm0b, ev0b = C["mm"], C["ev"]
            b = 4 + mc % 2

            @blk.tensor
            def _(e):
                if mc == 0:
                    e.wait_ge(s_ev, ka_ev)
                if mc >= 2:
                    e.wait_ge(s_ev, ev0b - 1)
                for kc in range(DC):
                    ins = e.matmul(PF[b][:], WV2[:, mc * 1024 + kc * 128:mc * 1024 + (kc + 1) * 128],
                                   KA[kc][:], start=(kc == 0), stop=(kc == DC - 1))
                ins.then_inc(s_mm, 1)

            @blk.vector
            def _(e):
                e.wait_ge(s_mm, mm0b + 1)
                p.svi(e.tensor_tensor(out=U[mc][:], in0=PF[b][:], in1=RS[mc][:],
                                      op=ALU.mult), "v")
                p.svw(e, "v")
                e.tensor_add(out=U[mc][:], in0=U[mc][:],
                             in1=X2[mc][:, 1 + t0:1 + t0 + 512]).then_inc(s_ev, 1)
            C["mm"] = mm0b + 1
            C["ev"] = ev0b + 1

        @blk.vector
        def _(e):
            e.wait_ge(s_ev, C["ev"])
            for dm in range(DC):
                ins = e.tensor_copy(out=UB[dm][:], in_=U[dm][:])
            ins.then_inc(s_ev, 1)
        C["ev"] += 1

        def gi4(dm):
            return UB[dm][:]

        def go4(dm, t0=t0):
            return X2[dm][:, 1 + t0:1 + t0 + 512]
        ln1(gi4, go4)
        fin_ev = C["ev"]

        @blk.sync
        def _(e):
            e.wait_ge(s_ev, fin_ev)
            for i in range(DC):
                e.dma_start(out=xoT[128 * i:128 * (i + 1), t0:t0 + 512],
                            in_=X2[i][:, 1 + t0:1 + t0 + 512]).then_inc(s_st, 16)
        C["st"] += 16 * DC

    @blk.sync
    def _(e):
        e.wait_ge(s_st, C["st"])

    return [(s_st, C["st"])]


def build_l1():
    p = Prog()
    dio = decl_l1(p)
    PFG = [p.ps(f"PFg{i}", [128, 512], F32) for i in range(6)]
    PBG = [p.ps(f"PBg{i}", [128, 1024], BF16) for i in range(2)]
    p.begin()
    emit_l1(p, dio, PFG, PBG)
    p.end()
    return p.nc


def build_fused(nlayers=6):
    p = Prog()
    d0 = decl_block0(p, fused=True)
    d1 = decl_l0ffn(p, nlayers, fused=True)
    d1["x1T"] = d0["x1T"]
    d1["ident"] = d0["ident"]
    d2 = decl_l1(p, fused=True)
    d2["x2T"] = d1["x2T"]
    d2["hsc"] = d1["hsc"]
    PFG = [p.ps(f"PFg{i}", [128, 512], F32) for i in range(6)]
    PBG = [p.ps(f"PBg{i}", [128, 1024], BF16) for i in range(2)]
    AU = p.stack.enter_context(
        p.nc.sbuf_tensor("AU", [128, 211808], mybir.dt.uint8))
    p.begin()
    p.set_arena(AU, 211808)
    f0 = emit_block0(p, d0, PFG, PBG)
    p.set_arena(AU, 211808)
    f1 = emit_l0ffn(p, d1, PFG, PBG, nlayers, prev=f0)
    p.set_arena(AU, 211808)
    f2 = emit_l1(p, d2, PFG, PBG, prev=f1)
    p.end()
    return p.nc


def _prep_l1_maps(inputs):
    Wth = np.asarray(inputs["L1_attn_W_theta"], np.float32)
    Wph = np.asarray(inputs["L1_attn_W_phi"], np.float32)
    Wg = np.asarray(inputs["L1_attn_W_g"], np.float32)
    Wr = np.asarray(inputs["L1_ffn_W_r"], np.float32)
    Wk = np.asarray(inputs["L1_ffn_W_k"], np.float32)
    Wv = np.asarray(inputs["L1_ffn_W_v"], np.float32)
    mur = np.asarray(inputs["L1_ffn_mu_r"], np.float32)
    muk = np.asarray(inputs["L1_ffn_mu_k"], np.float32)
    shared = {
        "wth_l": _pmaj(_lhsT_blocks(np.ascontiguousarray(Wth.T))),
        "wph_r": _pmaj(_rhs_blocks(np.ascontiguousarray(Wph.T))),
        "wg_r": _pmaj(_rhs_blocks(np.ascontiguousarray(Wg.T))),
        "wr_l": _pmaj(_lhsT_blocks(np.ascontiguousarray(Wr.T))),
        "wk1_l": _pmaj(_lhsT_blocks(np.ascontiguousarray(Wk.T))),
        "wv1_l": _pmaj(_lhsT_blocks(np.ascontiguousarray(Wv.T))),
        "mur_pc": _colmat(mur),
        "muk_pc": _colmat(muk),
    }
    maps = []
    for c in range(NCORES):
        m = dict(shared)
        m["hsc"] = np.full((128, 1), 0.0 if c == 0 else 1.0, np.float32)
        maps.append(m)
    return maps


def run_l1(inputs, x2_list):
    nc = build_l1()
    maps = _prep_l1_maps(inputs)
    for c in range(NCORES):
        maps[c]["x2T"] = np.ascontiguousarray(x2_list[c])
    res = run_bass_kernel_spmd(nc, maps, core_ids=list(range(NCORES)))
    return [r["xoT"] for r in res.results]


# version 11
# speedup vs baseline: 1.2781x; 1.2781x over previous
import contextlib
import numpy as np
import ml_dtypes
import concourse.bass as bass
import concourse.mybir as mybir
from concourse.bass_utils import run_bass_kernel_spmd

F32 = mybir.dt.float32
BF16 = mybir.dt.bfloat16
AF = mybir.ActivationFunctionType
ALU = mybir.AluOpType
AX = mybir.AxisListType
BF = ml_dtypes.bfloat16

NCORES = 8
NEGM = -1.0e9
T = 2048
D = 1024
DC = 8


def _lhsT_blocks(W):
    K, M = W.shape
    assert K % 128 == 0 and M % 128 == 0
    Kc, Mc = K // 128, M // 128
    return np.ascontiguousarray(
        W.reshape(Kc, 128, Mc, 128).transpose(2, 1, 0, 3).reshape(Mc, 128, Kc * 128))


def _rhs_blocks(W):
    K, N = W.shape
    assert K % 128 == 0
    return np.ascontiguousarray(W.reshape(K // 128, 128, N))


def _colmat(v):
    return np.ascontiguousarray(v.reshape(-1, 128).T)


def _pmaj(blob):
    C_, P_, K_ = blob.shape
    return np.ascontiguousarray(blob.transpose(1, 0, 2).reshape(P_, C_ * K_))



class Prog:
    def __init__(self):
        self.nc = bass.Bass("TRN2", target_bir_lowering=False)
        self.stack = contextlib.ExitStack()
        self.block = None

    def sem(self, name):
        return self.stack.enter_context(self.nc.semaphore(name))

    def sb(self, name, shape, dtype):
        if getattr(self, "arena", None) is not None:
            import math
            per_part = 1
            for d in shape[1:]:
                per_part *= d
            per_part *= {F32: 4, BF16: 2, mybir.dt.uint8: 1}[dtype]
            off = self.arena_off
            self.arena_off = (off + per_part + 31) & ~31
            assert self.arena_off <= self.arena_size, (
                f"fused arena overflow at {name}: {self.arena_off}")
            return self.nc.alloc_sbuf_tensor_at(
                name, shape, dtype, offset=self.arena_base + off)
        return self.stack.enter_context(self.nc.sbuf_tensor(name, shape, dtype))

    def set_arena(self, tensor, size):
        self.arena = tensor
        self.arena_base = self.nc.lookup_mloc(tensor).addr
        self.arena_off = 0
        self.arena_size = size

    def ps(self, name, shape, dtype=F32):
        return self.stack.enter_context(self.nc.psum_tensor(name, shape, dtype))

    def sb_at(self, name, shape, dtype, base_tensor, byte_off=0):
        base = self.nc.lookup_mloc(base_tensor).addr
        return self.nc.alloc_sbuf_tensor_at(name, shape, dtype,
                                            offset=base + byte_off)

    def din(self, name, shape, dtype=F32):
        return self.nc.dram_tensor(name, shape, dtype, kind="ExternalInput").ap()

    def dout(self, name, shape, dtype=F32):
        return self.nc.dram_tensor(name, shape, dtype, kind="ExternalOutput").ap()

    def dint(self, name, shape, dtype=F32, **kw):
        return self.nc.dram_tensor(name, shape, dtype, **kw).ap()

    def begin(self):
        self.block = self.stack.enter_context(self.nc.Block())
        self.sv = {k: [self.sem(f"sv_{k}"), 0] for k in "vstg"}

    def svi(self, ins, k):
        sem, n = self.sv[k]
        ins.then_inc(sem, 1)
        self.sv[k][1] = n + 1
        return ins

    def svw(self, e, k):
        sem, n = self.sv[k]
        if n:
            e.wait_ge(sem, n)

    def end(self):
        self.stack.close()


RG = [list(range(NCORES))]



def decl_block0(p, fused=False):
    dio = {}
    dio["xT"] = p.din("xT", [D, T])
    dio["wq_r"] = p.din("wq_r", [128, DC * D])
    dio["wv_r"] = p.din("wv_r", [128, DC * D])
    dio["wk_l"] = p.din("wk_l", [128, DC * D])
    dio["wo_l"] = p.din("wo_l", [128, DC * D])
    dio["bo_pc"] = p.din("bo_pc", [128, DC])
    dio["gb2"] = p.din("gb2", [2, D])
    dio["ident"] = p.din("ident", [128, 128], BF16)
    dio["x1T"] = p.dint("x1T", [D, T]) if fused else p.dout("x1T", [D, T])
    dio["m_in"] = p.dint("m_in", [128, DC]); dio["m_out"] = p.dint("m_out", [128, DC])
    dio["s_in"] = p.dint("s_in", [128, DC]); dio["s_out"] = p.dint("s_out", [128, DC])
    dio["ctx_in"] = p.dint("ctx_in", [D, D])
    dio["ctx_out"] = p.dint("ctx_out", [D, D], addr_space="Shared")
    return dio


def emit_block0(p, dio, PFG, PBG, prev=None):
    xT = dio["xT"]; wq_r = dio["wq_r"]; wv_r = dio["wv_r"]; wk_l = dio["wk_l"]
    wo_l = dio["wo_l"]; bo_pc = dio["bo_pc"]; gb2_in = dio["gb2"]
    ident_in = dio["ident"]; x1T = dio["x1T"]
    m_in = dio["m_in"]; m_out = dio["m_out"]
    s_in = dio["s_in"]; s_out = dio["s_out"]
    ctx_in = dio["ctx_in"]; ctx_out = dio["ctx_out"]

    XB = [p.sb(f"XB{i}", [128, T], BF16) for i in range(DC)]
    KB = [p.sb(f"KB{i}", [128, T], BF16) for i in range(DC)]
    BUFB = [p.sb(f"BB{i}", [128, 1024], BF16) for i in range(16)]
    BUFC = [p.sb(f"BC{i}", [128, 1024], BF16) for i in range(16)]
    QF = KB
    ARENA = p.sb("ARENA", [128, 40960], mybir.dt.uint8)
    W8 = p.sb("W8", [128, DC * D], BF16)
    ident = p.sb("identS", [128, 128], BF16)
    ones_c = p.sb("ones_c", [128, 1], BF16)
    ones_r = p.sb("ones_r", [1, 128], BF16)
    Mloc = p.sb("Mloc", [128, DC], F32)
    Mpart = p.sb("Mpart", [128, 4], F32)
    negMg = p.sb("negMg", [128, DC], F32)
    Sloc = p.sb("Sloc", [128, DC], F32)
    Sg = p.sb("Sg", [128, DC], F32)
    rinvS = p.sb("rinvS", [128, DC], F32)
    bo_s = p.sb("bo_s", [128, DC], F32)
    gbr = p.sb("gbr", [2, D], BF16)
    cscr = [p.sb(f"cscr{i}", [128, 512], F32) for i in range(2)]
    scol = p.sb("scol", [128, 4], F32)
    U = [p.sb_at(f"U{i}", [128, 512], F32, ARENA, 2048 * i) for i in range(DC)]
    Y = [p.sb_at(f"Y{i}", [128, 512], F32, ARENA, 16384 + 2048 * i) for i in range(DC)]
    AT = [p.sb_at(f"AT{i}", [128, 512], BF16, ARENA, 32768 + 1024 * i) for i in range(DC)]
    Xt = [p.sb_at(f"Xt{i}", [128, 512], F32, XB[i]) for i in range(DC)]
    UB = [p.sb_at(f"UBa{i}", [128, 512], BF16, BUFC[i]) for i in range(DC)]
    USQ = [p.sb_at(f"USQa{i}", [128, 512], BF16, BUFC[8 + i]) for i in range(DC)]
    ROWS = p.sb("ROWS", [1, 2048], F32)
    EPS = p.sb("EPS", [1, 1], F32)
    RBF = p.sb("RBF", [1, 512], BF16)
    R2 = p.sb("R2", [2, 512], BF16)

    P = PFG[0:4]
    PT = PBG
    PG = PFG[4:6]

    s_ld = p.sem("s_ld")
    s_ldh = p.sem("s_ldh")
    s_w = p.sem("s_w")
    s_mm = p.sem("s_mm")
    s_ev = p.sem("s_ev")
    s_act = p.sem("s_act")
    s_cc1 = p.sem("s_cc1"); s_cc2 = p.sem("s_cc2"); s_cc3 = p.sem("s_cc3")
    s_dma = p.sem("s_dma")
    s_xt = p.sem("s_xt")
    s_cs = [p.sem("s_cs0"), p.sem("s_cs1")]
    s_st = p.sem("s_st")

    blk = p.block
    if prev:
        emit_barrier(blk, prev)

    @blk.sync
    def _(e):
        e.dma_start(out=ident[:], in_=ident_in).then_inc(s_ldh, 16)
        e.dma_start(out=bo_s[:], in_=bo_pc).then_inc(s_ldh, 16)

    @blk.gpsimd
    def _(e):
        for i in range(DC):
            e.dma_start(out=XB[i][:], in_=xT[128 * i:128 * (i + 1), :]).then_inc(s_ld, 16)
        e.dma_start(out=gbr[:], in_=gb2_in).then_inc(s_ld, 16)
        e.dma_start(out=W8[:], in_=wk_l).then_inc(s_w, 16)

    @blk.vector
    def _(e):
        p.svi(e.memset(ones_c[:], 1.0), "v")
        p.svi(e.memset(EPS[:], 1.0e-5), "v")
        p.svi(e.memset(ones_r[:], 1.0), "v")
        p.svi(e.memset(R2[:], 1.0), "v")

    LD_ALL = 16 * (DC + 1)

    @blk.tensor
    def _(e):
        e.wait_ge(s_ld, LD_ALL)
        e.wait_ge(s_ldh, 32)
        e.wait_ge(s_w, 16)
        p.svw(e, "v")
        for dc in range(DC):
            for tt in range(4):
                i = dc * 4 + tt
                b = i % 2
                if i >= 2:
                    e.wait_ge(s_ev, i - 1)
                for kc in range(DC):
                    ins = e.matmul(P[b][:], W8[:, dc * D + kc * 128: dc * D + (kc + 1) * 128],
                                   XB[kc][:, 512 * tt: 512 * (tt + 1)],
                                   start=(kc == 0), stop=(kc == DC - 1))
                ins.then_inc(s_mm, 1)

    @blk.vector
    def _(e):
        for dc in range(DC):
            for tt in range(4):
                i = dc * 4 + tt
                b = i % 2
                e.wait_ge(s_mm, i + 1)
                if tt == 0:
                    p.svw(e, "v")
                e.reduce_max(out=Mpart[:, tt:tt + 1], in_=P[b][:], axis=AX.X)
                e.tensor_copy(out=KB[dc][:, 512 * tt: 512 * (tt + 1)],
                              in_=P[b][:]).then_inc(s_ev, 1)
            e.wait_ge(s_ev, dc * 4 + 4)
            ins = e.reduce_max(out=Mloc[:, dc:dc + 1], in_=Mpart[:], axis=AX.X)
            if dc < DC - 1:
                p.svi(ins, "v")
        ins.then_inc(s_ev, 1)

    @blk.gpsimd
    def _(e):
        e.wait_ge(s_ev, 33)
        e.dma_start(out=m_in, in_=Mloc[:]).then_inc(s_dma, 16)
        e.wait_ge(s_dma, 16)
        e.collective_compute("AllReduce", ALU.max, replica_groups=RG,
                             ins=[m_in.opt()], outs=[m_out.opt()]).then_inc(s_cc1)
        e.wait_ge(s_cc1, 1)
        e.dma_start(out=negMg[:], in_=m_out).then_inc(s_dma, 16)
        e.wait_ge(s_mm, 32)
        e.dma_start(out=W8[:], in_=wv_r).then_inc(s_w, 16)

    @blk.tensor
    def _(e):
        e.wait_ge(s_w, 32)
        for tc in range(16):
            for oh in range(2):
                i = tc * 2 + oh
                b = i % 2
                if i >= 2:
                    e.wait_ge(s_ev, 33 + (i - 1))
                for kc in range(DC):
                    ins = e.matmul(P[b][:], XB[kc][:, 128 * tc: 128 * (tc + 1)],
                                   W8[:, kc * D + oh * 512: kc * D + (oh + 1) * 512],
                                   start=(kc == 0), stop=(kc == DC - 1))
                ins.then_inc(s_mm, 1)

    @blk.vector
    def _(e):
        for tc in range(16):
            for oh in range(2):
                i = tc * 2 + oh
                b = i % 2
                e.wait_ge(s_mm, 32 + i + 1)
                e.tensor_copy(out=BUFB[tc][:, 512 * oh: 512 * (oh + 1)],
                              in_=P[b][:]).then_inc(s_ev, 1)
        e.wait_ge(s_dma, 32)
        e.tensor_scalar_mul(out=negMg[:], in0=negMg[:],
                            scalar1=-1.0).then_inc(s_ev, 1)

    @blk.scalar
    def _(e):
        e.wait_ge(s_ev, 66)
        p.svw(e, "v")
        for dc in range(DC):
            ins = e.activation(out=KB[dc][:], in_=KB[dc][:], func=AF.Exp,
                               bias=negMg[:, dc:dc + 1], scale=1.0,
                               accum_out=Sloc[:, dc:dc + 1])
        ins.then_inc(s_act, 1)

    @blk.gpsimd
    def _(e):
        e.wait_ge(s_act, 1)
        e.dma_start(out=s_in, in_=Sloc[:]).then_inc(s_dma, 16)
        e.wait_ge(s_dma, 48)
        e.collective_compute("AllReduce", ALU.add, replica_groups=RG,
                             ins=[s_in.opt()], outs=[s_out.opt()]).then_inc(s_cc2)
        e.wait_ge(s_cc2, 1)
        e.dma_start(out=Sg[:], in_=s_out).then_inc(s_dma, 16)

    @blk.tensor
    def _(e):
        e.wait_ge(s_act, 1)
        for tc in range(16):
            b = tc % 2
            if tc >= 2:
                e.wait_ge(s_ev, 66 + (tc - 1))
            for dc in range(DC):
                ins = e.transpose(PT[b][:, 128 * dc:128 * (dc + 1)],
                                  KB[dc][:, 128 * tc:128 * (tc + 1)], ident[:])
            ins.then_inc(s_mm, 1)

    @blk.vector
    def _(e):
        for tc in range(16):
            b = tc % 2
            e.wait_ge(s_mm, 64 + tc + 1)
            e.tensor_copy(out=BUFC[tc][:], in_=PT[b][:]).then_inc(s_ev, 1)

    @blk.tensor
    def _(e):
        e.wait_ge(s_ev, 82)
        for i in range(16):
            dc, oh = divmod(i, 2)
            b = i % 2
            if i >= 2:
                e.wait_ge(s_ev, 82 + (i - 1))
            for tc in range(16):
                ins = e.matmul(P[b][:], BUFC[tc][:, 128 * dc:128 * (dc + 1)],
                               BUFB[tc][:, 512 * oh:512 * (oh + 1)],
                               start=(tc == 0), stop=(tc == 15))
            ins.then_inc(s_mm, 1)

    @blk.vector
    def _(e):
        for i in range(16):
            b = i % 2
            e.wait_ge(s_mm, 80 + i + 1)
            if i >= 2:
                e.wait_ge(s_cs[b], 16 * (i // 2))
            e.tensor_copy(out=cscr[b][:], in_=P[b][:]).then_inc(s_ev, 1)

    @blk.gpsimd
    def _(e):
        for i in range(16):
            dc, oh = divmod(i, 2)
            b = i % 2
            e.wait_ge(s_ev, 82 + i + 1)
            e.dma_start(out=ctx_in[128 * dc:128 * (dc + 1), 512 * oh:512 * (oh + 1)],
                        in_=cscr[b][:]).then_inc(s_cs[b], 16)
        e.wait_ge(s_cs[0], 128)
        e.wait_ge(s_cs[1], 128)
        e.collective_compute("AllReduce", ALU.add, replica_groups=RG,
                             ins=[ctx_in.opt()], outs=[ctx_out.opt()]).then_inc(s_cc3)
        e.dma_start(out=W8[:], in_=wq_r).then_inc(s_w, 16)
        e.wait_ge(s_cc3, 1)
        for dc in range(DC):
            e.dma_start(out=BUFB[dc][:], in_=ctx_out[128 * dc:128 * (dc + 1), :]
                        ).then_inc(s_dma, 16)

    @blk.tensor
    def _(e):
        e.wait_ge(s_w, 48)
        for tc in range(16):
            for oh in range(2):
                i = tc * 2 + oh
                b = 2 + (i % 2)
                if tc >= 1:
                    e.wait_ge(s_act, 1 + tc)
                for kc in range(DC):
                    ins = e.matmul(P[b][:], XB[kc][:, 128 * tc:128 * (tc + 1)],
                                   W8[:, kc * D + oh * 512: kc * D + (oh + 1) * 512],
                                   start=(kc == 0), stop=(kc == DC - 1))
                ins.then_inc(s_mm, 1)

    @blk.vector
    def _(e):
        for tc in range(16):
            if tc >= 1:
                e.wait_ge(s_act, 1 + tc)
            for oh in range(2):
                i = tc * 2 + oh
                b = 2 + (i % 2)
                e.wait_ge(s_mm, 96 + i + 1)
                ins = e.tensor_reduce(out=scol[:, oh:oh + 1], in_=P[b][:], axis=AX.X,
                                      op=ALU.max, negate=True)
            p.svi(ins, "v")
            p.svw(e, "v")
            e.tensor_tensor(out=scol[:, 0:1], in0=scol[:, 0:1], in1=scol[:, 1:2],
                            op=ALU.min).then_inc(s_ev, 1)
            e.wait_ge(s_act, 2 + tc)
            p.svi(e.tensor_tensor(out=scol[:, 2:3], in0=scol[:, 2:3], in1=scol[:, 3:4],
                                  op=ALU.add), "v")
            p.svw(e, "v")
            p.svi(e.reciprocal(out=scol[:, 2:3], in_=scol[:, 2:3]), "v")
            p.svw(e, "v")
            e.tensor_scalar(out=BUFC[tc][:], in0=BUFC[tc][:],
                            scalar1=scol[:, 2:3], scalar2=1.0 / 32.0,
                            op0=ALU.mult, op1=ALU.mult).then_inc(s_ev, 1)

    @blk.scalar
    def _(e):
        e.wait_ge(s_mm, 96)
        for tc in range(16):
            e.wait_ge(s_ev, 98 + 2 * tc + 1)
            for oh in range(2):
                b = 2 + ((tc * 2 + oh) % 2)
                ins = e.activation(out=BUFC[tc][:, 512 * oh:512 * (oh + 1)], in_=P[b][:],
                                   func=AF.Exp, bias=scol[:, 0:1], scale=1.0,
                                   accum_out=scol[:, 2 + oh:3 + oh])
            ins.then_inc(s_act, 1)

    @blk.tensor
    def _(e):
        for g in range(2):
            for dc in range(DC):
                i = g * DC + dc
                b = i % 2
                if i >= 2:
                    e.wait_ge(s_ev, 130 + (i - 1))
                for tj in range(8):
                    tc = g * 8 + tj
                    e.wait_ge(s_ev, 98 + 2 * tc + 2)
                    ins = e.transpose(PT[b][:, 128 * tj:128 * (tj + 1)],
                                      BUFC[tc][:, 128 * dc:128 * (dc + 1)], ident[:])
                ins.then_inc(s_mm, 1)

    @blk.vector
    def _(e):
        for g in range(2):
            for dc in range(DC):
                i = g * DC + dc
                b = i % 2
                e.wait_ge(s_mm, 128 + i + 1)
                e.tensor_copy(out=QF[dc][:, 1024 * g:1024 * (g + 1)],
                              in_=PT[b][:]).then_inc(s_ev, 1)

    @blk.gpsimd
    def _(e):
        e.wait_ge(s_mm, 144)
        e.dma_start(out=W8[:], in_=wo_l).then_inc(s_w, 16)

    @blk.vector
    def _(e):
        e.wait_ge(s_dma, 192)
        p.svi(e.reciprocal(out=rinvS[:], in_=Sg[:]), "v")
        p.svw(e, "v")
        for dc in range(DC):
            ins = e.tensor_scalar_mul(out=BUFB[dc][:], in0=BUFB[dc][:],
                                      scalar1=rinvS[:, dc:dc + 1])
        ins.then_inc(s_ev, 1)

    @blk.gpsimd
    def _(e):
        pass

    C = {"mm": 144, "ev": 147, "act": 17}

    def ln_fm(getin, getout, with_gb):
        mm0, ev0, act0 = C["mm"], C["ev"], C["act"]

        @blk.vector
        def _(e):
            e.wait_ge(s_ev, ev0)
            for dm in range(DC):
                ins = e.tensor_copy(out=UB[dm][:], in_=getin(dm))
            p.svi(ins, "v")
            p.svw(e, "v")
            for dm in range(DC):
                ins = e.tensor_tensor(out=USQ[dm][:], in0=UB[dm][:], in1=UB[dm][:],
                                      op=ALU.mult)
            ins.then_inc(s_ev, 1)

        @blk.tensor
        def _(e):
            e.wait_ge(s_ev, ev0 + 1)
            for dm in range(DC):
                e.matmul(PG[0][0:1, :], ones_c[:], UB[dm][:],
                         start=(dm == 0), stop=(dm == DC - 1))
            for dm in range(DC):
                ins = e.matmul(PG[1][0:1, :], ones_c[:], USQ[dm][:],
                               start=(dm == 0), stop=(dm == DC - 1))
            ins.then_inc(s_mm, 1)

        @blk.scalar
        def _(e):
            e.wait_ge(s_mm, mm0 + 1)
            e.activation(out=ROWS[0:1, 0:512], in_=PG[0][0:1, :], func=AF.Copy,
                         bias=0.0, scale=-1.0 / D).then_inc(s_act, 1)

        @blk.vector
        def _(e):
            e.wait_ge(s_act, act0 + 1)
            p.svi(e.tensor_tensor(out=ROWS[0:1, 512:1024], in0=ROWS[0:1, 0:512],
                                  in1=ROWS[0:1, 0:512], op=ALU.mult), "v")
            p.svw(e, "v")
            e.scalar_tensor_tensor(out=ROWS[0:1, 512:1024], in0=PG[1][0:1, :],
                                   scalar=1.0 / D, in1=ROWS[0:1, 512:1024],
                                   op0=ALU.mult, op1=ALU.subtract
                                   ).then_inc(s_ev, 1)

        @blk.scalar
        def _(e):
            e.wait_ge(s_ev, ev0 + 2)
            p.svi(e.activation(out=ROWS[0:1, 1024:1536], in_=ROWS[0:1, 512:1024],
                               func=AF.Ln, bias=EPS[0:1, 0:1], scale=1.0), "s")
            p.svw(e, "s")
            e.activation(out=RBF[0:1, :], in_=ROWS[0:1, 1024:1536],
                         func=AF.Exp, bias=0.0, scale=-0.5
                         ).then_inc(s_act, 1)

        @blk.vector
        def _(e):
            e.wait_ge(s_act, act0 + 2)
            e.tensor_tensor(out=R2[0:1, :], in0=ROWS[0:1, 0:512],
                            in1=RBF[0:1, :], op=ALU.mult
                            ).then_inc(s_ev, 1)

        @blk.tensor
        def _(e):
            e.wait_ge(s_ev, ev0 + 3)
            for dm in range(DC):
                if dm >= 1:
                    e.wait_ge(s_ev, ev0 + 3 + dm)
                if with_gb:
                    e.matmul(PG[0][:], gbr[0:1, 128 * dm:128 * (dm + 1)],
                             RBF[0:1, :], start=True, stop=True)
                    ins = e.matmul(PG[1][:], gbr[0:2, 128 * dm:128 * (dm + 1)],
                                   R2[0:2, :], start=True, stop=True)
                else:
                    e.matmul(PG[0][:], ones_r[:], RBF[0:1, :], start=True, stop=True)
                    ins = e.matmul(PG[1][:], ones_r[:], R2[0:1, :], start=True, stop=True)
                ins.then_inc(s_mm, 1)

        @blk.vector
        def _(e):
            for dm in range(DC):
                e.wait_ge(s_mm, mm0 + 1 + dm + 1)
                p.svi(e.tensor_tensor(out=getout(dm), in0=getin(dm), in1=PG[0][:],
                                      op=ALU.mult), "v")
                p.svw(e, "v")
                e.tensor_tensor(out=getout(dm), in0=getout(dm), in1=PG[1][:],
                                op=ALU.add).then_inc(s_ev, 1)

        C["mm"] = mm0 + 9
        C["ev"] = ev0 + 11
        C["act"] = act0 + 2

    def attn_tile(tt):
        mm0, ev0 = C["mm"], C["ev"]

        @blk.gpsimd
        def _(e):
            e.wait_ge(s_mm, 128)
            if tt >= 1:
                e.wait_ge(s_st, 128 * tt)
            for dm in range(DC):
                e.dma_start(out=Xt[dm][:],
                            in_=xT[128 * dm:128 * (dm + 1), 512 * tt:512 * (tt + 1)]
                            ).then_inc(s_xt, 16)

        @blk.tensor
        def _(e):
            e.wait_ge(s_ev, ev0)
            for ec in range(DC):
                b = ec % 2
                if ec >= 2:
                    e.wait_ge(s_ev, ev0 + (ec - 1))
                for dc in range(DC):
                    ins = e.matmul(P[b][:], BUFB[dc][:, 128 * ec:128 * (ec + 1)],
                                   QF[dc][:, 512 * tt:512 * (tt + 1)],
                                   start=(dc == 0), stop=(dc == DC - 1))
                ins.then_inc(s_mm, 1)

        @blk.vector
        def _(e):
            for ec in range(DC):
                b = ec % 2
                e.wait_ge(s_mm, mm0 + ec + 1)
                e.tensor_copy(out=AT[ec][:], in_=P[b][:]).then_inc(s_ev, 1)

        @blk.tensor
        def _(e):
            e.wait_ge(s_ev, ev0 + 8)
            e.wait_ge(s_w, 64)
            e.wait_ge(s_act, 17)
            for dm in range(DC):
                b = 2 + dm % 2
                if dm >= 2:
                    e.wait_ge(s_ev, ev0 + 8 + (dm - 1))
                for ec in range(DC):
                    ins = e.matmul(P[b][:], W8[:, dm * D + ec * 128: dm * D + (ec + 1) * 128],
                                   AT[ec][:], start=(ec == 0), stop=(ec == DC - 1))
                ins.then_inc(s_mm, 1)

        @blk.vector
        def _(e):
            for dm in range(DC):
                b = 2 + dm % 2
                e.wait_ge(s_mm, mm0 + 8 + dm + 1)
                if dm == 0:
                    e.wait_ge(s_xt, 128 * (tt + 1))
                e.scalar_tensor_tensor(out=U[dm][:], in0=P[b][:],
                                       scalar=bo_s[:, dm:dm + 1],
                                       in1=Xt[dm][:],
                                       op0=ALU.add, op1=ALU.add
                                       ).then_inc(s_ev, 1)

        C["mm"] = mm0 + 16
        C["ev"] = ev0 + 16

        ln_fm(lambda dm: U[dm][:], lambda dm: Y[dm][:], True)

        ev1 = C["ev"]

        @blk.vector
        def _(e):
            e.wait_ge(s_ev, ev1)
            for dm in range(DC):
                ins = e.tensor_add(out=Xt[dm][:], in0=Xt[dm][:], in1=Y[dm][:])
            ins.then_inc(s_ev, 1)
        C["ev"] = ev1 + 1

        def xsl(dm):
            return Xt[dm][:]
        ln_fm(xsl, xsl, False)

        ev2 = C["ev"]

        @blk.gpsimd
        def _(e):
            e.wait_ge(s_ev, ev2)
            for dm in range(DC):
                e.dma_start(out=x1T[128 * dm:128 * (dm + 1), 512 * tt:512 * (tt + 1)],
                            in_=Xt[dm][:]).then_inc(s_st, 16)

    for tt in range(4):
        attn_tile(tt)

    @blk.gpsimd
    def _(e):
        e.wait_ge(s_st, 16 * 32)

    return [(s_st, 16 * 32), (s_ev, C["ev"]), (s_mm, C["mm"]),
            (s_act, C["act"]), (s_dma, 192), (s_w, 64)]


def emit_barrier(blk, finals):
    for eng in ["tensor", "vector", "scalar", "gpsimd", "sync"]:
        def body(e, finals=finals):
            for sem, val in finals:
                e.wait_ge(sem, val)
        getattr(blk, eng)(body)


def build_block0():
    p = Prog()
    dio = decl_block0(p)
    PFG = [p.ps(f"PFg{i}", [128, 512], F32) for i in range(6)]
    PBG = [p.ps(f"PBg{i}", [128, 1024], BF16) for i in range(2)]
    p.begin()
    emit_block0(p, dio, PFG, PBG)
    p.end()
    return p.nc



def _prep_block0_inputs(inputs):
    x = np.asarray(inputs["L0_attn_x"], np.float32)[0]
    Wq = np.asarray(inputs["L0_attn_Wq"], np.float32)
    Wk = np.asarray(inputs["L0_attn_Wk"], np.float32)
    Wv = np.asarray(inputs["L0_attn_Wv"], np.float32)
    Wo = np.asarray(inputs["L0_attn_Wo"], np.float32)
    bo = np.asarray(inputs["L0_attn_bo"], np.float32)
    g = np.asarray(inputs["L0_attn_gamma"], np.float32)
    b = np.asarray(inputs["L0_attn_beta"], np.float32)
    shared = {
        "wq_r": _pmaj(_rhs_blocks(Wq)),
        "wv_r": _pmaj(_rhs_blocks(Wv)),
        "wk_l": _pmaj(_lhsT_blocks(Wk)),
        "wo_l": _pmaj(_lhsT_blocks(Wo)),
        "bo_pc": _colmat(bo),
        "gb2": np.ascontiguousarray(np.stack([g, b])),
        "ident": np.eye(128, dtype=BF),
    }
    maps = []
    for c in range(NCORES):
        m = dict(shared)
        m["xT"] = np.ascontiguousarray(x[T * c:T * (c + 1)].T)
        maps.append(m)
    return maps


def run_block0(inputs):
    nc = build_block0()
    maps = _prep_block0_inputs(inputs)
    res = run_bass_kernel_spmd(nc, maps, core_ids=list(range(NCORES)))
    return [r["x1T"] for r in res.results]


def _prep_fused_maps(inputs):
    m0 = _prep_block0_inputs(inputs)
    m1 = _prep_l0ffn_maps(inputs, 6)
    m2 = _prep_l1_maps(inputs)
    maps = []
    for c in range(NCORES):
        d = dict(m0[c])
        m1c = dict(m1[c])
        m1c.pop("ident", None)
        d.update(m1c)
        d.update(m2[c])
        maps.append(d)
    return maps


def kernel(**inputs):
    nc = build_fused(6)
    maps = _prep_fused_maps(inputs)
    res = run_bass_kernel_spmd(nc, maps, core_ids=list(range(NCORES)))
    xo = [r["xoT"] for r in res.results]
    out = np.concatenate([a.T for a in xo], axis=0)[None]
    return np.ascontiguousarray(out, dtype=np.float32)


def kernel_unfused(**inputs):
    x1 = run_block0(inputs)
    x2 = run_l0ffn(inputs, x1, 6)
    xo = run_l1(inputs, x2)
    out = np.concatenate([a.T for a in xo], axis=0)[None]
    return np.ascontiguousarray(out, dtype=np.float32)



INNER = 1024
DIMH = 64
NHEAD = 16
W = 256
DI = 2816
DIC = 22
QK_SCALE = 8.0


def _rotary_tables():
    N = 16384
    half = np.arange(0, DIMH, 2, dtype=np.float64)
    inv_freq = 1.0 / 10000.0 ** (half / DIMH)
    t = np.arange(-256, N, dtype=np.float64)
    freqs = t[:, None] * inv_freq[None, :]
    freqs = np.concatenate([freqs, freqs], axis=1)
    cos = np.cos(freqs)
    sin = np.sin(freqs)
    base_half = (half + 0.4 * DIMH) / (1.4 * DIMH)
    base = np.concatenate([base_half, base_half])
    tmod = ((t.astype(np.int64) % W) + W) % W
    srel = base[None, :] ** (tmod[:, None] / (W // 2))
    inv = base[None, :] ** (-tmod[:, None] / (W // 2))
    return (cos * srel, sin * srel, cos * inv, sin * inv), base


def _prep_l0ffn_shared(inputs):
    Wqkv = np.asarray(inputs["L0_ffn_Wqkv"], np.float32)
    Wo = np.asarray(inputs["L0_ffn_Wo"], np.float32)
    W1 = np.asarray(inputs["L0_ffn_W1"], np.float32)
    W2 = np.asarray(inputs["L0_ffn_W2"], np.float32)
    b1 = np.asarray(inputs["L0_ffn_b1"], np.float32)
    b2 = np.asarray(inputs["L0_ffn_b2"], np.float32)
    qs = np.asarray(inputs["L0_ffn_q_scale"], np.float32)
    ks = np.asarray(inputs["L0_ffn_k_scale"], np.float32)
    lng = np.asarray(inputs["L0_ffn_ln_g"], np.float32)
    lnb = np.asarray(inputs["L0_ffn_ln_b"], np.float32)
    DIM0 = 2730
    sh = {}
    for L in range(6):
        sh[f"wqk_l{L}"] = _pmaj(_lhsT_blocks(Wqkv[L][:, :2048]))
        sh[f"wv_r{L}"] = _pmaj(_rhs_blocks(Wqkv[L][:, 2048:3072]))
        sh[f"wo_l{L}"] = _pmaj(_lhsT_blocks(Wo[L]))
        w1a = np.zeros((D, DI), np.float32); w1a[:, :DIM0] = W1[L][:, :DIM0]
        w1g = np.zeros((D, DI), np.float32); w1g[:, :DIM0] = W1[L][:, DIM0:]
        sh[f"w1a_l{L}"] = _lhsT_blocks(w1a)
        sh[f"w1g_l{L}"] = _lhsT_blocks(w1g)
        w2p = np.zeros((DI, D), np.float32); w2p[:DIM0] = W2[L]
        sh[f"w2_l{L}"] = _lhsT_blocks(w2p)
        b1a = np.zeros(DI, np.float32); b1a[:DIM0] = b1[L][:DIM0]
        b1g = np.zeros(DI, np.float32); b1g[:DIM0] = b1[L][DIM0:]
        sh[f"b1a_pc{L}"] = _colmat(b1a)
        sh[f"b1g_pc{L}"] = _colmat(b1g)
        sh[f"b2_pc{L}"] = _colmat(b2[L])
        sh[f"lngb{L}"] = np.ascontiguousarray(np.stack([lng[L], lnb[L]]))
        sh[f"qs_c{L}"] = np.tile(qs[L], 2).reshape(128, 1).copy()
        sh[f"ks_c{L}"] = np.tile(ks[L], 2).reshape(128, 1).copy()
        sh[f"qsr_c{L}"] = np.tile(np.roll(qs[L].reshape(2, 32), 1, 0).ravel(), 2).reshape(128, 1).copy()
        sh[f"ksr_c{L}"] = np.tile(np.roll(ks[L].reshape(2, 32), 1, 0).ravel(), 2).reshape(128, 1).copy()
    (csq, ssq, csk, ssk), base = _rotary_tables()
    sh["_tab_csq"] = csq; sh["_tab_ssq"] = ssq
    sh["_tab_csk"] = csk; sh["_tab_ssk"] = ssk
    base2 = (base ** 2).astype(np.float32)
    sh["base2_c"] = np.tile(base2, 2).reshape(128, 1).copy()
    Rp = np.zeros((128, 128), np.float32)
    for h in range(2):
        o = 64 * h
        for j in range(32):
            Rp[o + 32 + j, o + j] = -1.0
            Rp[o + j, o + 32 + j] = 1.0
    sh["rperm"] = Rp.astype(BF)
    e16 = np.zeros((8, 128, 16), np.float32)
    e16t = np.zeros((8, 16, 128), np.float32)
    for c in range(8):
        for p in range(128):
            h = 2 * c + p // 64
            e16[c, p, h] = 1.0
            e16t[c, h, p] = 1.0
    sh["e16"] = _pmaj(e16).astype(BF)
    sh["e16t"] = _pmaj(e16t).astype(BF)
    sh["e16t8"] = (_pmaj(e16t) * QK_SCALE).astype(BF)
    mk = np.zeros((128, 384), np.float32)
    for i in range(128):
        mk[i, 0:128][np.arange(128) < i] = NEGM
        mk[i, 256:384][np.arange(128) > i] = NEGM
    sh["_maskW"] = mk
    mk0a = mk.copy()
    mk0a[:, 0:256] = NEGM
    sh["_maskW0A"] = mk0a
    mk0b = mk.copy()
    mk0b[:, 0:128] = NEGM
    sh["_maskW0B"] = mk0b
    return sh


def decl_l0ffn(p, nlayers=6, fused=False):
    dio = {}
    dio["x1T"] = dio0 = None
    if not fused:
        dio["x1T"] = p.din("x1T", [D, T])
        dio["ident"] = p.din("ident", [128, 128], BF16)
    win = {}
    for L in range(nlayers):
        for nm in ["wqk_l", "wv_r", "wo_l", "w1a_l", "w1g_l", "w2_l"]:
            shp = {"wqk_l": [128, 16 * 1024], "wv_r": [128, 8 * 1024],
                   "wo_l": [128, 8 * 1024], "w1a_l": [22, 128, 1024],
                   "w1g_l": [22, 128, 1024], "w2_l": [8, 128, 2816]}[nm]
            win[nm + str(L)] = p.din(nm + str(L), shp)
        win["b1a_pc%d" % L] = p.din("b1a_pc%d" % L, [128, DIC])
        win["b1g_pc%d" % L] = p.din("b1g_pc%d" % L, [128, DIC])
        win["b2_pc%d" % L] = p.din("b2_pc%d" % L, [128, DC])
        win["lngb%d" % L] = p.din("lngb%d" % L, [2, D])
        win["scl%d" % L] = p.din("scl%d" % L, [128, 4])
    dio["win"] = win
    dio["x2T"] = p.dint("x2T", [D, T]) if fused else p.dout("x2T", [D, T])
    dio["tabs"] = p.din("tabs", [4, 128, T + 2 * W])
    dio["maskW"] = p.din("maskW", [128, 384], BF16)
    dio["mask0"] = p.din("mask0", [128, 384], BF16)
    dio["mask0b"] = p.din("mask0b", [128, 384], BF16)
    dio["rperm"] = p.din("rperm", [128, 128], BF16)
    dio["e16"] = p.din("e16", [128, 8 * 16], BF16)
    dio["e16t"] = p.din("e16t", [16, 8 * 128], BF16)
    dio["e16t8"] = p.din("e16t8", [16, 8 * 128], BF16)
    dio["base2_c"] = p.din("base2_c", [128, 1])
    dio["hsc"] = p.din("hsc", [128, 1])
    dio["zd"] = [p.dint(f"zd{i}", [D, T]) for i in range(2)]
    dio["hz_in"] = p.dint("hz_in", [128, DC * 257])
    dio["hz_out"] = p.dint("hz_out", [NCORES * 128, DC * 257], addr_space="Shared")
    dio["zt_in"] = p.dint("zt_in", [128, DC])
    dio["zt_out"] = p.dint("zt_out", [NCORES * 128, DC], addr_space="Shared")
    return dio


def emit_l0ffn(p, dio, PFG, PBG, nlayers=6, prev=None):
    x1T = dio["x1T"]; x2T = dio["x2T"]; win = dio["win"]
    tabs_d = dio["tabs"]; maskW_d = dio["maskW"]; mask0_d = dio["mask0"]
    mask0b_d = dio["mask0b"]; rperm_d = dio["rperm"]; e16_d = dio["e16"]
    e16t_d = dio["e16t"]; e16t8_d = dio["e16t8"]
    base2_d = dio["base2_c"]; hsc_d = dio["hsc"]
    ident_in = dio["ident"]
    zd = dio["zd"]; hz_in = dio["hz_in"]; hz_out = dio["hz_out"]
    zt_in = dio["zt_in"]; zt_out = dio["zt_out"]

    AR = p.sb("ARN", [128, 134464], mybir.dt.uint8)

    def at(name, shape, dtype, off):
        return p.sb_at(name, shape, dtype, AR, off)
    FB = [at(f"FBf{i}", [128, 1025], BF16, 2080 * i) for i in range(DC)]
    XN = [at(f"XNf{i}", [128, 1024], BF16, 16640 + 2048 * i) for i in range(DC)]
    HT = [at(f"HTf{i}", [128, 1024], BF16, 33024 + 2048 * i) for i in range(DIC)]
    ZO = [at(f"ZOf{i}", [128, 512], F32, 78080 + 2048 * i) for i in range(DC)]
    W1S = [at(f"W1S{i}", [128, 1024], BF16, 94464 + 2048 * i) for i in range(4)]
    W2S = [at(f"W2S{i}", [128, 2816], BF16, 102656 + 5632 * i) for i in range(2)]
    GELO = [at(f"GELO{i}", [128, 512], BF16, 113920 + 1024 * i) for i in range(2)]
    USQ = [at(f"USQf{i}", [128, 512], BF16, 115968 + 1024 * i) for i in range(DC)]
    UB = [at(f"UBf{i}", [128, 512], BF16, 94464 + 1024 * i) for i in range(DC)]
    F2 = [at(f"F2f{i}", [128, 512], F32, 94464 + 2048 * i) for i in range(DC)]
    KP = [at(f"KP{i}", [128, 1024], BF16, 2048 * i) for i in range(DC)]
    VB = [at(f"VB{i}", [128, 1024], BF16, 16384 + 2048 * i) for i in range(6)]
    QH = [at(f"QH{i}", [128, 512], BF16, 28672 + 1024 * i) for i in range(DC)]
    ZBALL = at("ZBALL", [128, DC * 528], BF16, 36864)
    ZBp = [ZBALL[:, 528 * i:528 * i + 516] for i in range(DC)]
    AO = [at(f"AO{i}", [128, 256], BF16, 45312 + 512 * i) for i in range(DC)]
    ZRW = [at(f"ZRW{i}", [128, 256], F32, 49408 + 1024 * i) for i in range(DC)]
    ZOW = [at(f"ZOW{i}", [128, 256], F32, 57600 + 1024 * i) for i in range(DC)]
    QR = [at(f"QR{i}", [128, 512], BF16, 65792 + 1024 * i) for i in range(2)]
    QSQ = at("QSQ", [128, 512], BF16, 67840)
    RBS = at("RBS", [128, 512], BF16, 68864)
    T1 = at("T1", [128, 512], BF16, 69888)
    T3 = at("T3", [128, 512], BF16, 70912)
    PEx = [at(f"PEx{i}", [128, 384], BF16, 71936 + 768 * i) for i in range(2)]
    PTS = [at(f"PTS{i}", [128, 384], BF16, 73472 + 768 * i) for i in range(2)]
    TAB = [at(f"TAB{i}", [128, 520], BF16, 75008 + 1056 * i) for i in range(4)]
    SSR2 = at("SSR2", [16, 512], F32, 79232)
    RN16B = at("RN16B", [16, 512], BF16, 81280)
    HZALL = at("HZALL", [128, DC * 257], BF16, 130304)
    SSR = at("SSR", [16, 512], F32, 87680)
    RN16 = at("RN16", [16, 512], BF16, 89728)
    WQK = p.sb("WQK", [128, 16 * 1024], BF16)
    WV = p.sb("WV", [128, 8 * 1024], BF16)
    WO = p.sb("WO", [128, 8 * 1024], BF16)
    NM = p.sb("NM", [128, 16], F32)
    SS = p.sb("SS", [128, 16], F32)
    RI = p.sb("RI", [128, 16], F32)
    ZP = p.sb("ZPf", [128, 8], F32)
    ZPALL = p.sb("ZPALLf", [128, 64], F32)
    ZPF = p.sb("ZPFf", [128, 8], F32)
    TCOL = p.sb("TCOLf", [128, 8], F32)
    MK = p.sb("MKs", [128, 384], BF16)
    MK0 = p.sb("MK0s", [128, 384], BF16)
    MK0B = p.sb("MK0Bs", [128, 384], BF16)
    RPERM = p.sb("RPERMs", [128, 128], BF16)
    E16 = p.sb("E16s", [128, 8 * 16], BF16)
    E16T = p.sb("E16Ts", [16, 8 * 128], BF16)
    E16T8 = p.sb("E16T8s", [16, 8 * 128], BF16)
    B2C = p.sb("B2Cs", [128, 1], F32)
    HSC = p.sb("HSCs", [128, 1], F32)
    IDENT = p.sb("IDENTs", [128, 128], BF16)
    SCL = p.sb("SCLs", [128, 4], F32)
    B1A = p.sb("B1As", [128, DIC], F32)
    B1G = p.sb("B1Gs", [128, DIC], F32)
    B2F = p.sb("B2Fs", [128, DC], F32)
    GBR = p.sb("GBRs", [2, D], BF16)
    LN8 = p.sb("LN8s", [16, 1], F32)
    E24 = p.sb("E24s", [16, 1], F32)
    EPS = p.sb("EPSs", [1, 1], F32)
    RMU = at("RMUs", [1, 512], F32, 124160)
    RVA = at("RVAs", [1, 512], F32, 126208)
    RBF = at("RBFs", [1, 512], BF16, 128256)
    R2 = at("R2s", [2, 512], BF16, 129280)
    ONEC = p.sb("ONECs", [128, 1], BF16)
    ONER = p.sb("ONERs", [1, 128], BF16)

    PF = PFG
    PB = PBG

    s_ld = p.sem("l_ld")
    s_w = p.sem("l_w")
    s_w1 = [p.sem("l_w1a"), p.sem("l_w1b")]
    s_w2 = [p.sem("l_w2a"), p.sem("l_w2b")]
    s_wh = p.sem("l_wh")
    s_mm = p.sem("l_mm")
    s_ev = p.sem("l_ev")
    s_ac = p.sem("l_ac")
    s_dg = p.sem("l_dg")
    s_dh = p.sem("l_dh")
    s_st = p.sem("l_st")
    s_sta = p.sem("l_sta")
    ccs = [p.sem(f"l_cc{i}") for i in range(2 * nlayers + 1)]

    blk = p.block
    if prev:
        emit_barrier(blk, prev)
    C = {"mm": 0, "ev": 0, "ac": 0, "ld": 0, "w": 0, "wh": 0, "dg": 0,
         "dh": 0, "st": 0, "sta": 0, "w1": [0, 0], "w2": [0, 0]}

    @blk.gpsimd
    def _(e):
        e.dma_start(out=MK[:], in_=maskW_d).then_inc(s_ld, 16)
        e.dma_start(out=MK0[:], in_=mask0_d).then_inc(s_ld, 16)
        e.dma_start(out=MK0B[:], in_=mask0b_d).then_inc(s_ld, 16)
        e.dma_start(out=E16[:], in_=e16_d).then_inc(s_ld, 16)
        e.dma_start(out=E16T[:], in_=e16t_d).then_inc(s_ld, 16)
        e.dma_start(out=E16T8[:], in_=e16t8_d).then_inc(s_ld, 16)
        e.dma_start(out=B2C[:], in_=base2_d).then_inc(s_ld, 16)
        e.dma_start(out=HSC[:], in_=hsc_d).then_inc(s_ld, 16)
        e.dma_start(out=WQK[:], in_=win["wqk_l0"]).then_inc(s_w, 16)
        e.dma_start(out=WV[:], in_=win["wv_r0"]).then_inc(s_w, 16)
        e.dma_start(out=WO[:], in_=win["wo_l0"]).then_inc(s_w, 16)
    C["ld"] += 8 * 16
    C["w"] += 3 * 16

    @blk.sync
    def _(e):
        e.dma_start(out=RPERM[:], in_=rperm_d).then_inc(s_wh, 16)
        e.dma_start(out=IDENT[:], in_=ident_in).then_inc(s_wh, 16)
    C["wh"] += 2 * 16

    @blk.vector
    def _(e):
        p.svi(e.memset(ONEC[:], 1.0), "v")
        p.svi(e.memset(ONER[:], 1.0), "v")
        p.svi(e.memset(R2[:], 1.0), "v")
        p.svi(e.memset(EPS[:], 1.0e-5), "v")
        p.svi(e.memset(LN8[:], float(np.log(QK_SCALE))), "v")
        p.svi(e.memset(E24[:], 1.0e-24), "v")

    def zsrc(L):
        return x1T if L == 0 else zd[L % 2]

    def ln_fm_g(getin_bf, getout, gb_rows, mmck, n_tok):
        mm0, ev0, ac0 = C["mm"], C["ev"], C["ac"]

        @blk.vector
        def _(e):
            e.wait_ge(s_ev, ev0)
            for dm in range(DC):
                ins = e.tensor_tensor(out=USQ[dm][:, 0:n_tok], in0=getin_bf(dm),
                                      in1=getin_bf(dm), op=ALU.mult)
            ins.then_inc(s_ev, 1)

        @blk.tensor
        def _(e):
            e.wait_ge(s_ev, ev0 + 1)
            for dm in range(DC):
                e.matmul(PF[4][0:1, 0:n_tok], ONEC[:], getin_bf(dm),
                         start=(dm == 0), stop=(dm == DC - 1))
            for dm in range(DC):
                ins = e.matmul(PF[5][0:1, 0:n_tok], ONEC[:], USQ[dm][:, 0:n_tok],
                               start=(dm == 0), stop=(dm == DC - 1))
            ins.then_inc(s_mm, 1)

        @blk.scalar
        def _(e):
            e.wait_ge(s_mm, mm0 + 1)
            e.activation(out=RMU[0:1, 0:n_tok], in_=PF[4][0:1, 0:n_tok],
                         func=AF.Copy, bias=0.0, scale=-1.0 / D
                         ).then_inc(s_ac, 1)

        @blk.vector
        def _(e):
            e.wait_ge(s_ac, ac0 + 1)
            p.svi(e.tensor_tensor(out=RVA[0:1, 0:n_tok], in0=RMU[0:1, 0:n_tok],
                                  in1=RMU[0:1, 0:n_tok], op=ALU.mult), "v")
            p.svw(e, "v")
            e.scalar_tensor_tensor(out=RVA[0:1, 0:n_tok], in0=PF[5][0:1, 0:n_tok],
                                   scalar=1.0 / D, in1=RVA[0:1, 0:n_tok],
                                   op0=ALU.mult, op1=ALU.subtract
                                   ).then_inc(s_ev, 1)

        @blk.scalar
        def _(e):
            e.wait_ge(s_ev, ev0 + 2)
            p.svi(e.activation(out=RVA[0:1, 0:n_tok], in_=RVA[0:1, 0:n_tok],
                               func=AF.Ln, bias=EPS[0:1, 0:1], scale=1.0), "s")
            p.svw(e, "s")
            e.activation(out=RBF[0:1, 0:n_tok], in_=RVA[0:1, 0:n_tok],
                         func=AF.Exp, bias=0.0, scale=-0.5
                         ).then_inc(s_ac, 1)

        @blk.vector
        def _(e):
            e.wait_ge(s_ac, ac0 + 2)
            e.tensor_tensor(out=R2[0:1, 0:n_tok], in0=RMU[0:1, 0:n_tok],
                            in1=RBF[0:1, 0:n_tok], op=ALU.mult
                            ).then_inc(s_ev, 1)

        @blk.tensor
        def _(e):
            e.wait_ge(s_ev, ev0 + 3)
            for dm in range(DC):
                if dm >= 1:
                    e.wait_ge(s_ev, ev0 + 3 + dm)
                if gb_rows is not None:
                    e.matmul(PF[4][:, 0:n_tok], gb_rows[0:1, 128 * dm:128 * (dm + 1)],
                             RBF[0:1, 0:n_tok], start=True, stop=True)
                    ins = e.matmul(PF[5][:, 0:n_tok], gb_rows[0:2, 128 * dm:128 * (dm + 1)],
                                   R2[0:2, 0:n_tok], start=True, stop=True)
                else:
                    e.matmul(PF[4][:, 0:n_tok], ONER[:], RBF[0:1, 0:n_tok],
                             start=True, stop=True)
                    ins = e.matmul(PF[5][:, 0:n_tok], ONER[:], R2[0:1, 0:n_tok],
                                   start=True, stop=True)
                ins.then_inc(s_mm, 1)

        @blk.vector
        def _(e):
            for dm in range(DC):
                e.wait_ge(s_mm, mm0 + 1 + dm + 1)
                p.svi(e.tensor_tensor(out=getout(dm), in0=getin_bf(dm),
                                      in1=PF[4][:, 0:n_tok], op=ALU.mult), "v")
                p.svw(e, "v")
                e.tensor_tensor(out=getout(dm), in0=getout(dm),
                                in1=PF[5][:, 0:n_tok], op=ALU.add
                                ).then_inc(s_ev, 1)
        C["mm"] = mm0 + 9
        C["ev"] = ev0 + 11
        C["ac"] = ac0 + 2

    def emit_hz_exchange(L):
        cc1 = ccs[2 * L]
        zsl = zsrc(L)

        @blk.sync
        def _(e):
            e.wait_ge(s_st, C["st"])
            e.wait_ge(s_sta, C["sta"])
            e.dma_start(out=hz_in.rearrange("p (c t) -> p c t", c=DC),
                        in_=zsl[:, T - 257:T].rearrange("(c p) t -> p c t", p=128)
                        ).then_inc(s_dh, 16)
        C["dh"] += 16
        dh_snap = C["dh"]

        @blk.gpsimd
        def _(e):
            e.wait_ge(s_dh, dh_snap)
            e.collective_compute("AllGather", ALU.bypass, replica_groups=RG,
                                 ins=[hz_in.opt()], outs=[hz_out.opt()]
                                 ).then_inc(cc1)

    def emit_hz_read(L):
        cc1 = ccs[2 * L]

        @blk.gpsimd
        def _(e):
            e.wait_ge(cc1, 1)
            e.wait_ge(s_ev, C["ev"])
            for arm in e.switch_core_id(n=128):
                rr = (int(arm.logical) % NCORES + NCORES - 1) % NCORES
                e.dma_start(out=HZALL[:],
                            in_=hz_out[rr * 128:(rr + 1) * 128, :]
                            ).then_inc(s_ld, 16)
        C["ld"] += 16

    def emit_layer(L):
        zin = zsrc(L)
        zout = zd[(L + 1) % 2]
        cc2 = ccs[2 * L + 1]

        if L == 0:
            emit_hz_exchange(0)
            emit_hz_read(0)

        @blk.gpsimd
        def _(e):
            e.wait_ge(s_ev, C["ev"])
            e.wait_ge(s_mm, C["mm"])
            e.wait_ge(s_ac, C["ac"])
            e.dma_start(out=SCL[:], in_=win["scl%d" % L]).then_inc(s_ld, 16)
            e.dma_start(out=B1A[:], in_=win["b1a_pc%d" % L]).then_inc(s_ld, 16)
            e.dma_start(out=B1G[:], in_=win["b1g_pc%d" % L]).then_inc(s_ld, 16)
            e.dma_start(out=B2F[:], in_=win["b2_pc%d" % L]).then_inc(s_ld, 16)
            e.dma_start(out=GBR[:], in_=win["lngb%d" % L]).then_inc(s_ld, 16)
        C["ld"] += 16 * 5

        for a in range(5):
            emit_qkv_step(L, a, zin)
            if a == 0:
                emit_attn_window(L, 0, a, zin, zout)
            elif a < 4:
                emit_attn_window(L, 2 * a - 1, a, zin, zout)
                emit_attn_window(L, 2 * a, a, zin, zout)
            else:
                emit_attn_window(L, 7, a, zin, zout)

        @blk.gpsimd
        def _(e):
            e.wait_ge(s_mm, C["mm"])
            if L + 1 < nlayers:
                e.dma_start(out=WQK[:], in_=win["wqk_l%d" % (L + 1)]
                            ).then_inc(s_w, 16)
                e.dma_start(out=WV[:], in_=win["wv_r%d" % (L + 1)]
                            ).then_inc(s_w, 16)
                e.dma_start(out=WO[:], in_=win["wo_l%d" % (L + 1)]
                            ).then_inc(s_w, 16)
                C["w"] += 48

        tc_ev0 = C["ev"]

        @blk.vector
        def _(e):
            e.wait_ge(s_ev, tc_ev0)
            for i in range(DC):
                ins = e.tensor_copy(out=TCOL[:, i:i + 1], in_=ZOW[i][:, 255:256])
            ins.then_inc(s_ev, 1)
        C["ev"] += 1

        @blk.gpsimd
        def _(e):
            e.wait_ge(s_ev, C["ev"])
            e.dma_start(out=zt_in, in_=TCOL[:]).then_inc(s_dg, 16)
            e.wait_ge(s_dg, C["dg"] + 16)
            e.collective_compute("AllGather", ALU.bypass, replica_groups=RG,
                                 ins=[zt_in.opt()], outs=[zt_out.opt()]
                                 ).then_inc(cc2)
        C["dg"] += 16

        @blk.sync
        def _(e):
            e.wait_ge(cc2, 1)
            for arm in e.switch_core_id(n=128):
                rr = (int(arm.logical) % NCORES + NCORES - 1) % NCORES
                e.dma_start(out=ZPALL[:, 0:DC],
                            in_=zt_out[rr * 128:(rr + 1) * 128, :]
                            ).then_inc(s_dh, 16)
        C["dh"] += 16

        zp_dh = C["dh"]
        emit_ffn_group(L, 1, zout)

        if L + 1 < nlayers:
            emit_hz_exchange(L + 1)

        @blk.vector
        def _(e):
            e.wait_ge(s_dh, zp_dh)
            e.tensor_scalar_mul(out=ZP[:], in0=ZPALL[:, 0:DC],
                                scalar1=HSC[:, 0:1]).then_inc(s_ev, 1)
        C["ev"] += 1

        emit_ffn_group(L, 0, zout,
                       post_w1=(lambda: emit_hz_read(L + 1))
                       if L + 1 < nlayers else None)

    def emit_qkv_step(L, a, zin):
        t0 = -256 + 512 * a
        TT = 512 if a < 4 else 256
        ld0 = C["ld"]

        @blk.gpsimd
        def _(e):
            e.wait_ge(s_ev, C["ev"])
            e.wait_ge(s_mm, C["mm"])
            base = 2 * W + (T * 0) + t0
            for j in range(4):
                e.dma_start(out=TAB[j][:, 0:TT],
                            in_=tabs_d[j, :, 256 + t0:256 + t0 + TT]
                            ).then_inc(s_ld, 16)
            if a == 0:
                for i in range(DC):
                    e.dma_start(out=ZBp[i][:, 257:513], in_=zin[128 * i:128 * (i + 1), 0:256]
                                ).then_inc(s_ld, 16)
            else:
                for i in range(DC):
                    e.dma_start(out=ZBp[i][:, 0:TT + 1],
                                in_=zin[128 * i:128 * (i + 1), t0 - 1:t0 + TT]
                                ).then_inc(s_ld, 16)
        C["ld"] += 16 * (4 + 8)
        zb_ld = C["ld"]


        ev0 = C["ev"]

        @blk.vector
        def _(e):
            e.wait_ge(s_ld, zb_ld)
            if a == 0:
                zv = ZBALL[:].rearrange("p (i t) -> p i t", t=528)[:, :, 0:257]
                p.svi(e.tensor_copy(
                    out=zv,
                    in_=HZALL[:].rearrange("p (i t) -> p i t", t=257)), "v")
                p.svw(e, "v")
                p.svi(e.tensor_scalar_mul(out=zv, in0=zv, scalar1=HSC[:, 0:1]), "v")
                p.svw(e, "v")
            e.tensor_scalar_mul(out=TAB[0][:, 0:TT], in0=TAB[0][:, 0:TT],
                                scalar1=SCL[:, 0:1])
            e.tensor_scalar_mul(out=TAB[1][:, 0:TT], in0=TAB[1][:, 0:TT],
                                scalar1=SCL[:, 1:2])
            e.tensor_scalar_mul(out=TAB[2][:, 0:TT], in0=TAB[2][:, 0:TT],
                                scalar1=SCL[:, 2:3])
            ins = e.tensor_scalar_mul(out=TAB[3][:, 0:TT], in0=TAB[3][:, 0:TT],
                                      scalar1=SCL[:, 3:4])
            ins.then_inc(s_ev, 1)
        C["ev"] += 1

        @blk.vector
        def _(e):
            e.wait_ge(s_ev, ev0 + 1)
            e.wait_ge(s_mm, C["mm"])
            if a >= 1:
                for i in range(DC):
                    e.tensor_scalar_mul(out=KP[i][:, 0:256], in0=KP[i][:, 768:1024],
                                        scalar1=B2C[:, 0:1])
                for i in range(2):
                    ins = e.tensor_copy(out=VB[i][:], in_=VB[4 + i][:])
                ins.then_inc(s_ev, 1)
            else:
                e.nop().then_inc(s_ev, 1)
        C["ev"] += 1
        cp_ev = C["ev"]

        nq = 16
        cw_snap = C["w"]
        ld_snap = C["ld"]
        mm_entry = C["mm"]
        ac_entry = C["ac"]

        qkd = {}; ssd = {}; rotd = {}; bcd = {}
        mmv = mm_entry
        for it in range(nq + 3):
            if it < nq:
                mmv += 1; qkd[it] = mmv
            if 0 <= it - 1 < nq:
                mmv += 1; ssd[it - 1] = mmv
            if 0 <= it - 2 < nq:
                mmv += 1; rotd[it - 2] = mmv
                mmv += 1; bcd[it - 2] = mmv
        vq3d = {}; evacsqd = {}
        evv = C["ev"]
        for it in range(nq + 2):
            if 0 <= it - 2 < nq:
                evv += 1; vq3d[it - 2] = evv
            if it < nq:
                evv += 1; evacsqd[it] = evv
        ln_done = {}; exp_done = {}
        for m in range(nq // 2):
            ln_done[2 * m] = ac_entry + 4 * m + 1
            ln_done[2 * m + 1] = ac_entry + 4 * m + 2
            exp_done[2 * m] = ac_entry + 4 * m + 3
            exp_done[2 * m + 1] = ac_entry + 4 * m + 4

        def emit_Tq1(i):
            mc = i
            b = i % 2

            @blk.tensor
            def _(e):
                if i == 0:
                    e.wait_ge(s_ev, cp_ev)
                    e.wait_ge(s_w, cw_snap)
                    e.wait_ge(s_wh, 32)
                    e.wait_ge(s_ld, ld_snap)
                if i >= 2:
                    e.wait_ge(s_ev, evacsqd[i - 2])
                for kc in range(DC):
                    sh = 1 if kc < 4 else 0
                    ins = e.matmul(PF[b][:, 0:TT],
                                   WQK[:, mc * 1024 + kc * 128: mc * 1024 + (kc + 1) * 128],
                                   ZBp[kc][:, sh:sh + TT],
                                   start=(kc == 0), stop=(kc == DC - 1))
                ins.then_inc(s_mm, 1)

        def emit_Tq2(j):
            hp = j % 8

            @blk.tensor
            def _(e):
                e.wait_ge(s_ev, evacsqd[j])
                if j >= 1:
                    e.wait_ge(s_ac, ln_done[j - 1])
                e.matmul(PF[4][0:16, 0:TT], E16[:, 16 * hp:16 * (hp + 1)],
                         QSQ[:, 0:TT], start=True, stop=True
                         ).then_inc(s_mm, 1)

        def emit_Trot(k):
            @blk.tensor
            def _(e):
                e.wait_ge(s_ev, evacsqd[k])
                e.matmul(PF[2 + k % 2][:, 0:TT], RPERM[:], QR[k % 2][:, 0:TT],
                         start=True, stop=True).then_inc(s_mm, 1)

        def emit_Tbc(m):
            hp = m % 8
            ET = E16T8 if m < 8 else E16T
            RNS = RN16 if m % 2 == 0 else RN16B

            @blk.tensor
            def _(e):
                e.wait_ge(s_ac, exp_done[m])
                if m >= 1:
                    e.wait_ge(s_ev, vq3d[m - 1])
                e.matmul(PF[5][:, 0:TT], ET[:, 128 * hp:128 * (hp + 1)],
                         RNS[0:16, 0:TT], start=True, stop=True
                         ).then_inc(s_mm, 1)

        for it in range(nq + 3):
            if it < nq:
                emit_Tq1(it)
            if 0 <= it - 1 < nq:
                emit_Tq2(it - 1)
            if 0 <= it - 2 < nq:
                emit_Trot(it - 2)
                emit_Tbc(it - 2)

        def emit_Vq1(i):
            b = i % 2

            @blk.vector
            def _(e):
                if i >= 2:
                    e.wait_ge(s_mm, rotd[i - 2])
                elif i == 1:
                    e.wait_ge(s_mm, ssd[0])
                else:
                    e.wait_ge(s_mm, qkd[0])
                p.svi(e.tensor_copy(out=QR[i % 2][:, 0:TT], in_=PF[b][:, 0:TT]), "v")
                p.svw(e, "v")
                e.tensor_tensor(out=QSQ[:, 0:TT], in0=QR[i % 2][:, 0:TT],
                                in1=QR[i % 2][:, 0:TT], op=ALU.mult
                                ).then_inc(s_ev, 1)

        def emit_Vq3(j):
            hp = j % 8
            qk = j // 8

            @blk.vector
            def _(e):
                e.wait_ge(s_mm, bcd[j])
                p.svi(e.tensor_tensor(out=T3[:, 0:TT], in0=PF[2 + j % 2][:, 0:TT],
                                      in1=TAB[2 * qk + 1][:, 0:TT],
                                      op=ALU.mult), "v")
                p.svi(e.tensor_tensor(out=T1[:, 0:TT], in0=QR[j % 2][:, 0:TT],
                                      in1=TAB[2 * qk][:, 0:TT], op=ALU.mult), "v")
                p.svw(e, "v")
                p.svi(e.tensor_tensor(out=T1[:, 0:TT], in0=T1[:, 0:TT],
                                      in1=T3[:, 0:TT], op=ALU.add), "v")
                p.svw(e, "v")
                if qk == 0:
                    ins = e.tensor_tensor(out=QH[hp][:, 0:TT], in0=T1[:, 0:TT],
                                          in1=PF[5][:, 0:TT], op=ALU.mult)
                else:
                    if a == 4:
                        ins = e.tensor_tensor(out=KP[hp][:, 256:512],
                                              in0=T1[:, 0:256],
                                              in1=PF[5][:, 0:256], op=ALU.mult)
                    else:
                        e.tensor_tensor(out=KP[hp][:, 256:512], in0=T1[:, 0:256],
                                        in1=PF[5][:, 0:256], op=ALU.mult)
                        ins = e.tensor_tensor(out=KP[hp][:, 768:1024],
                                              in0=T1[:, 256:512],
                                              in1=PF[5][:, 256:512], op=ALU.mult)
                ins.then_inc(s_ev, 1)

        for it in range(nq + 2):
            if 0 <= it - 2 < nq:
                emit_Vq3(it - 2)
            if it < nq:
                emit_Vq1(it)

        def emit_Spair(m):
            j0, j1 = 2 * m, 2 * m + 1

            @blk.scalar
            def _(e):
                e.wait_ge(s_mm, ssd[j0])
                e.activation(out=SSR[0:16, 0:TT], in_=PF[4][0:16, 0:TT],
                             func=AF.Ln, bias=E24[0:16, 0:1], scale=1.0
                             ).then_inc(s_ac, 1)
                e.wait_ge(s_mm, ssd[j1])
                e.activation(out=SSR2[0:16, 0:TT], in_=PF[4][0:16, 0:TT],
                             func=AF.Ln, bias=E24[0:16, 0:1], scale=1.0
                             ).then_inc(s_ac, 1)
                e.wait_ge(s_ac, ln_done[j1])
                if m >= 1:
                    e.wait_ge(s_mm, bcd[j0 - 2])
                else:
                    e.wait_ge(s_mm, mm_entry)
                e.activation(out=RN16[0:16, 0:TT], in_=SSR[0:16, 0:TT],
                             func=AF.Exp, bias=0.0, scale=-0.5).then_inc(s_ac, 1)
                if m >= 1:
                    e.wait_ge(s_mm, bcd[j1 - 2])
                e.activation(out=RN16B[0:16, 0:TT], in_=SSR2[0:16, 0:TT],
                             func=AF.Exp, bias=0.0, scale=-0.5).then_inc(s_ac, 1)

        for m in range(nq // 2):
            emit_Spair(m)

        C["mm"] = mmv
        C["ev"] = evv
        C["ac"] = ac_entry + 2 * nq

        ev0 = C["ev"]

        @blk.vector
        def _(e):
            e.wait_ge(s_ev, ev0)
            if a < 4:
                for i in range(DC):
                    ins = e.tensor_scalar_mul(out=KP[i][:, 512:768],
                                              in0=KP[i][:, 256:512],
                                              scalar1=B2C[:, 0:1])
                ins.then_inc(s_ev, 1)
            else:
                e.nop().then_inc(s_ev, 1)
        C["ev"] += 1

        ntc = TT // 128
        for tc in range(ntc):
            mm0, ev0 = C["mm"], C["ev"]

            @blk.tensor
            def _(e):
                for oh in range(2):
                    if tc == 0 and oh == 0:
                        e.wait_ge(s_ev, ev0)
                    elif tc >= 1:
                        e.wait_ge(s_ev, ev0 + oh - 1)
                    for kc in range(DC):
                        sh = 1 if kc < 4 else 0
                        ins = e.matmul(PF[oh][:, 0:512],
                                       ZBp[kc][:, sh + 128 * tc: sh + 128 * (tc + 1)],
                                       WV[:, kc * 1024 + oh * 512: kc * 1024 + (oh + 1) * 512],
                                       start=(kc == 0), stop=(kc == DC - 1))
                    ins.then_inc(s_mm, 1)

            @blk.vector
            def _(e):
                for oh in range(2):
                    e.wait_ge(s_mm, mm0 + oh + 1)
                    e.tensor_copy(out=VB[2 + tc][:, 512 * oh:512 * (oh + 1)],
                                  in_=PF[oh][:, 0:512]).then_inc(s_ev, 1)
            C["mm"] += 2
            C["ev"] += 2

    def emit_attn_window(L, w, a, zin, zout):
        if a == 0:
            pat_hi = True
            qoff = 256
        elif a == 4:
            pat_hi = False
            qoff = 0
        else:
            pat_hi = (w == 2 * a)
            qoff = 0 if w == 2 * a - 1 else 256
        kbase = 512 if pat_hi else 0
        vbase = 2 if pat_hi else 0

        wo_st0 = C["sta"]
        zld0 = C["ld"]

        @blk.gpsimd
        def _(e):
            e.wait_ge(s_st, C["st"])
            e.wait_ge(s_sta, C["sta"])
            e.wait_ge(s_ev, C["ev"])
            for i in range(DC):
                e.dma_start(out=ZRW[i][:],
                            in_=zin[128 * i:128 * (i + 1), 256 * w:256 * (w + 1)]
                            ).then_inc(s_ld, 16)
        C["ld"] += 16 * DC
        zrw_ld = C["ld"]

        first_ev = C["ev"]
        mm_entry = C["mm"]
        ac_entry = C["ac"]
        n = 32
        LOOK = 2

        simd = {}; transd = {}; avd = {}
        mmv = mm_entry
        for i in range(n + LOOK):
            if i < n:
                mmv += 1; simd[i] = mmv
            j = i - LOOK
            if j >= 0:
                mmv += 1; transd[j] = mmv
                mmv += 1; avd[j] = mmv
        maxd = {}; scaled = {}; copyd = {}; aoev = {}
        evv = first_ev
        for i in range(n + 2):
            if i < n:
                evv += 1; maxd[i] = evv
            if 0 <= i - 1 < n:
                evv += 1; scaled[i - 1] = evv
            j = i - 2
            if j >= 0:
                evv += 1; copyd[j] = evv
                if j % 4 == 3:
                    evv += 1; aoev[j // 4] = evv
        expd = {i: ac_entry + i + 1 for i in range(n)}

        def wchain(i):
            hp, r = divmod(i, 4)
            qt, hh = divmod(r, 2)
            return hp, qt, hh

        def emit_T1(i):
            hp, qt, hh = wchain(i)
            hb = 64 * hh
            bs = i % 2
            mask = (MK0 if qt == 0 else MK0B) if w == 0 else MK

            @blk.tensor
            def _(e):
                if i == 0:
                    e.wait_ge(s_ev, first_ev)
                if i < LOOK:
                    e.wait_ge(s_ac, ac_entry)
                else:
                    e.wait_ge(s_ac, expd[i - LOOK])
                e.matmul(PF[bs][:, 0:384], IDENT[:], mask[:],
                         start=True, stop=False)
                e.matmul(PF[bs][:, 0:384],
                         QH[hp][hb:hb + 64, qoff + 128 * qt: qoff + 128 * (qt + 1)],
                         KP[hp][hb:hb + 64, kbase + 128 * qt: kbase + 128 * qt + 384],
                         start=False, stop=True,
                         tile_position=(hb, 0)).then_inc(s_mm, 1)

        def emit_T2T3(j):
            hp, qt, hh = wchain(j)
            hb = 64 * hh
            bs = j % 2
            bt = j % 2

            @blk.tensor
            def _(e):
                e.wait_ge(s_ev, scaled[j])
                for c in range(3):
                    ins = e.transpose(PB[bt][:, 128 * c:128 * (c + 1)],
                                      PEx[bs][:, 128 * c:128 * (c + 1)],
                                      IDENT[:])
                ins.then_inc(s_mm, 1)

            @blk.tensor
            def _(e):
                e.wait_ge(s_ev, copyd[j])
                for c in range(3):
                    kch = qt + c
                    vslot = vbase + kch
                    hcol = 64 * (2 * hp + hh)
                    ins = e.matmul(PF[2 + hp % 2][hb:hb + 64, 128 * qt:128 * (qt + 1)],
                                   VB[vslot][:, hcol:hcol + 64],
                                   PTS[bt][:, 128 * c:128 * (c + 1)],
                                   start=(c == 0), stop=(c == 2),
                                   tile_position=(0, hb))
                ins.then_inc(s_mm, 1)

        for i in range(n + LOOK):
            if i < n:
                emit_T1(i)
            if i - LOOK >= 0:
                emit_T2T3(i - LOOK)

        def emit_V1(i):
            hp, qt, hh = wchain(i)
            h = 2 * hp + hh
            bs = i % 2

            @blk.vector
            def _(e):
                e.wait_ge(s_mm, simd[i])
                e.tensor_reduce(out=NM[:, h:h + 1], in_=PF[bs][:, 0:384],
                                axis=AX.X, op=ALU.max, negate=True
                                ).then_inc(s_ev, 1)

        def emit_V2(i):
            hp, qt, hh = wchain(i)
            h = 2 * hp + hh
            bs = i % 2

            @blk.vector
            def _(e):
                e.wait_ge(s_ac, expd[i])
                p.svi(e.reciprocal(out=RI[:, h:h + 1], in_=SS[:, h:h + 1]), "v")
                p.svw(e, "v")
                e.tensor_scalar_mul(out=PEx[bs][:, 0:384],
                                    in0=PEx[bs][:, 0:384],
                                    scalar1=RI[:, h:h + 1]).then_inc(s_ev, 1)

        def emit_V3(j):
            bt = j % 2

            @blk.vector
            def _(e):
                e.wait_ge(s_mm, transd[j])
                e.tensor_copy(out=PTS[bt][:], in_=PB[bt][:, 0:384]
                              ).then_inc(s_ev, 1)

        def emit_AO(hp):
            @blk.vector
            def _(e):
                e.wait_ge(s_mm, avd[4 * hp + 3])
                e.tensor_copy(out=AO[hp][:], in_=PF[2 + hp % 2][:, 0:256]
                              ).then_inc(s_ev, 1)

        for i in range(n + 2):
            if i < n:
                emit_V1(i)
            if 0 <= i - 1 < n:
                emit_V2(i - 1)
            j = i - 2
            if j >= 0:
                emit_V3(j)
                if j % 4 == 3:
                    emit_AO(j // 4)

        def emit_S1(i):
            hp, qt, hh = wchain(i)
            h = 2 * hp + hh
            bs = i % 2

            @blk.scalar
            def _(e):
                e.wait_ge(s_ev, maxd[i])
                if i >= LOOK:
                    e.wait_ge(s_mm, transd[i - LOOK])
                else:
                    e.wait_ge(s_mm, mm_entry)
                e.activation(out=PEx[bs][:, 0:384], in_=PF[bs][:, 0:384],
                             func=AF.Exp, bias=NM[:, h:h + 1], scale=1.0,
                             accum_out=SS[:, h:h + 1]).then_inc(s_ac, 1)

        for i in range(n):
            emit_S1(i)

        C["mm"] = mmv
        C["ev"] = evv
        C["ac"] = ac_entry + n

        ao_ev = C["ev"]
        for dm in range(DC):
            mm0, ev0 = C["mm"], C["ev"]

            @blk.tensor
            def _(e):
                if dm == 0:
                    e.wait_ge(s_ev, ao_ev)
                if dm >= 2:
                    e.wait_ge(s_ev, ev0 - 1)
                for ec in range(DC):
                    ins = e.matmul(PF[4 + dm % 2][:, 0:256],
                                   WO[:, dm * 1024 + ec * 128: dm * 1024 + (ec + 1) * 128],
                                   AO[ec][:], start=(ec == 0), stop=(ec == DC - 1))
                ins.then_inc(s_mm, 1)

            @blk.vector
            def _(e):
                e.wait_ge(s_mm, mm0 + 1)
                if dm == 0:
                    e.wait_ge(s_ld, zrw_ld)
                    e.wait_ge(s_sta, wo_st0)
                e.tensor_add(out=ZOW[dm][:], in0=PF[4 + dm % 2][:, 0:256],
                             in1=ZRW[dm][:]).then_inc(s_ev, 1)
            C["mm"] += 1
            C["ev"] += 1
        zo_ev = C["ev"]

        @blk.sync
        def _(e):
            e.wait_ge(s_ev, zo_ev)
            for i in range(DC):
                e.dma_start(out=zout[128 * i:128 * (i + 1), 256 * w:256 * (w + 1)],
                            in_=ZOW[i][:]).then_inc(s_sta, 16)
        C["sta"] += 16 * DC

    def emit_ffn_group(L, g, zq, post_w1=None):
        t0 = 1024 * g

        st_grp0 = C["st"]
        sta_grp0 = C["sta"]

        @blk.gpsimd
        def _(e):
            e.wait_ge(s_st, st_grp0)
            e.wait_ge(s_sta, sta_grp0)
            e.wait_ge(s_ev, C["ev"])
            e.wait_ge(s_mm, C["mm"])
            for i in range(DC):
                if g == 1:
                    e.dma_start(out=FB[i][:, 0:1025],
                                in_=zq[128 * i:128 * (i + 1), t0 - 1:t0 + 1024]
                                ).then_inc(s_ld, 16)
                else:
                    e.dma_start(out=FB[i][:, 1:1025],
                                in_=zq[128 * i:128 * (i + 1), t0:t0 + 1024]
                                ).then_inc(s_ld, 16)
            for j in range(2):
                e.dma_start(out=W1S[j][:], in_=win["w1a_l%d" % L][j]).then_inc(s_w1[j], 16)
                e.dma_start(out=W1S[2 + j][:], in_=win["w1g_l%d" % L][j]).then_inc(s_w1[j], 16)
        C["ld"] += 16 * DC
        C["w1"][0] += 32
        C["w1"][1] += 32
        w1_snap = [C["w1"][0], C["w1"][1]]
        fb_ld = C["ld"]

        @blk.vector
        def _(e):
            e.wait_ge(s_ld, fb_ld)
            if g == 0:
                for i in range(DC):
                    ins = e.tensor_copy(out=FB[i][:, 0:1], in_=ZP[:, i:i + 1])
                ins.then_inc(s_ev, 1)
            else:
                e.nop().then_inc(s_ev, 1)
        C["ev"] += 1

        for u in range(2):
            def gin(dm, u=u):
                sh = 0 if dm < 4 else -1
                return FB[dm][:, 1 + sh + 512 * u: 1 + sh + 512 * u + 512]

            def gout(dm, u=u):
                return XN[dm][:, 512 * u:512 * (u + 1)]
            ln_fm_g(gin, gout, GBR, None, 512)
        xn_ev = C["ev"]

        for mc in range(DIC):
            for sub in range(2):
                j = mc * 2 + sub
                mm0, ev0, ac0 = C["mm"], C["ev"], C["ac"]
                pa = j % 2
                pg = 2 + j % 2

                @blk.tensor
                def _(e):
                    if j == 0:
                        e.wait_ge(s_ev, xn_ev)
                    e.wait_ge(s_w1[mc % 2], w1_snap[mc % 2] + 32 * (mc // 2))
                    if j >= 2:
                        e.wait_ge(s_ac, ac0 - 1)
                        e.wait_ge(s_ev, ev0 - 1)
                    for kc in range(DC):
                        e.matmul(PF[pa][:],
                                 W1S[mc % 2][:, kc * 128:(kc + 1) * 128],
                                 XN[kc][:, 512 * sub:512 * (sub + 1)],
                                 start=(kc == 0), stop=(kc == DC - 1))
                    for kc in range(DC):
                        ins = e.matmul(PF[pg][:],
                                       W1S[2 + mc % 2][:, kc * 128:(kc + 1) * 128],
                                       XN[kc][:, 512 * sub:512 * (sub + 1)],
                                       start=(kc == 0), stop=(kc == DC - 1))
                    ins.then_inc(s_mm, 1)

                @blk.scalar
                def _(e):
                    e.wait_ge(s_mm, mm0 + 1)
                    e.activation(out=GELO[j % 2][:], in_=PF[pg][:], func=AF.Gelu,
                                 bias=B1G[:, mc:mc + 1], scale=1.0
                                 ).then_inc(s_ac, 1)

                @blk.vector
                def _(e):
                    e.wait_ge(s_ac, ac0 + 1)
                    e.scalar_tensor_tensor(out=HT[mc][:, 512 * sub:512 * (sub + 1)],
                                           in0=PF[pa][:],
                                           scalar=B1A[:, mc:mc + 1],
                                           in1=GELO[j % 2][:],
                                           op0=ALU.add, op1=ALU.mult
                                           ).then_inc(s_ev, 1)
                C["mm"] = mm0 + 1
                C["ev"] = ev0 + 1
                C["ac"] = ac0 + 1

            @blk.gpsimd
            def _(e):
                e.wait_ge(s_mm, C["mm"])
                if mc + 2 < DIC:
                    e.dma_start(out=W1S[mc % 2][:],
                                in_=win["w1a_l%d" % L][mc + 2]).then_inc(s_w1[mc % 2], 16)
                    e.dma_start(out=W1S[2 + mc % 2][:],
                                in_=win["w1g_l%d" % L][mc + 2]).then_inc(s_w1[mc % 2], 16)
                    C["w1"][mc % 2] += 32
        h_ev = C["ev"]

        if post_w1 is not None:
            post_w1()


        @blk.gpsimd
        def _(e):
            e.wait_ge(s_ev, C["ev"])
            e.wait_ge(s_st, st_grp0)
            for i in range(2):
                e.dma_start(out=W2S[i][:], in_=win["w2_l%d" % L][i]).then_inc(s_w2[i], 16)
        C["w2"][0] += 16
        C["w2"][1] += 16
        w2_snap = [C["w2"][0], C["w2"][1]]

        for dm in range(DC):
            for sub in range(2):
                j = dm * 2 + sub
                mm0, ev0 = C["mm"], C["ev"]
                pb_ = j % 4

                @blk.tensor
                def _(e):
                    if j == 0:
                        e.wait_ge(s_ev, h_ev)
                    e.wait_ge(s_w2[dm % 2], w2_snap[dm % 2] + 16 * (dm // 2))
                    if j >= 4:
                        e.wait_ge(s_ev, ev0 - 3)
                    for kc in range(DIC):
                        ins = e.matmul(PF[pb_][:],
                                       W2S[dm % 2][:, kc * 128:(kc + 1) * 128],
                                       HT[kc][:, 512 * sub:512 * (sub + 1)],
                                       start=(kc == 0), stop=(kc == DIC - 1))
                    ins.then_inc(s_mm, 1)

                @blk.vector
                def _(e):
                    e.wait_ge(s_mm, mm0 + 1)
                    e.wait_ge(s_st, C["st"])
                    e.tensor_scalar_add(out=ZO[dm][:],
                                        in0=PF[pb_][:], scalar1=B2F[:, dm:dm + 1]
                                        ).then_inc(s_ev, 1)
                C["mm"] = mm0 + 1
                C["ev"] = ev0 + 1

                @blk.gpsimd
                def _(e):
                    e.wait_ge(s_ev, C["ev"])
                    e.dma_start(out=zq[128 * dm:128 * (dm + 1),
                                       t0 + 512 * sub:t0 + 512 * (sub + 1)],
                                in_=ZO[dm][:], accum_op=ALU.add).then_inc(s_st, 16)
                C["st"] += 16

            @blk.gpsimd
            def _(e):
                e.wait_ge(s_mm, C["mm"])
                if dm + 2 < DC:
                    e.dma_start(out=W2S[dm % 2][:],
                                in_=win["w2_l%d" % L][dm + 2]).then_inc(s_w2[dm % 2], 16)
                    C["w2"][dm % 2] += 16

    for L in range(nlayers):
        emit_layer(L)

    zfin = zd[nlayers % 2] if nlayers > 0 else x1T
    for g4 in range(4):
        t0 = 512 * g4

        @blk.gpsimd
        def _(e):
            e.wait_ge(s_st, C["st"])
            e.wait_ge(s_sta, C["sta"])
            e.wait_ge(s_ev, C["ev"])
            e.wait_ge(s_mm, C["mm"])
            for i in range(DC):
                e.dma_start(out=ZO[i][:],
                            in_=zfin[128 * i:128 * (i + 1), t0:t0 + 512]
                            ).then_inc(s_ld, 16)
            for i in range(DC):
                e.dma_start(out=F2[i][:],
                            in_=x1T[128 * i:128 * (i + 1), t0:t0 + 512]
                            ).then_inc(s_ld, 16)
        C["ld"] += 16 * 2 * DC
        f_ld = C["ld"]

        @blk.vector
        def _(e):
            e.wait_ge(s_ld, f_ld)
            for i in range(DC):
                p.svi(e.tensor_add(out=ZO[i][:], in0=ZO[i][:], in1=F2[i][:]), "v")
            p.svw(e, "v")
            for i in range(DC):
                ins = e.tensor_copy(out=UB[i][:], in_=ZO[i][:])
            ins.then_inc(s_ev, 1)
        C["ev"] += 1

        def ginf(dm):
            return UB[dm][:]

        def goutf(dm):
            return ZO[dm][:]
        ln_fm_g(ginf, goutf, None, None, 512)
        fin_ev = C["ev"]

        @blk.sync
        def _(e):
            e.wait_ge(s_ev, fin_ev)
            for i in range(DC):
                e.dma_start(out=x2T[128 * i:128 * (i + 1), t0:t0 + 512],
                            in_=ZO[i][:]).then_inc(s_sta, 16)
        C["sta"] += 16 * DC

    @blk.sync
    def _(e):
        e.wait_ge(s_sta, C["sta"])
        e.wait_ge(s_st, C["st"])

    return [(s_sta, C["sta"]), (s_st, C["st"]), (s_ev, C["ev"]),
            (s_mm, C["mm"]), (s_ac, C["ac"]), (s_ld, C["ld"]),
            (s_w, C["w"]), (s_w1[0], C["w1"][0]), (s_w1[1], C["w1"][1]),
            (s_w2[0], C["w2"][0]), (s_w2[1], C["w2"][1])]


def build_l0ffn(nlayers=6):
    p = Prog()
    dio = decl_l0ffn(p, nlayers)
    PFG = [p.ps(f"PFg{i}", [128, 512], F32) for i in range(6)]
    PBG = [p.ps(f"PBg{i}", [128, 1024], BF16) for i in range(2)]
    p.begin()
    emit_l0ffn(p, dio, PFG, PBG, nlayers)
    p.end()
    return p.nc


def _prep_l0ffn_maps(inputs, nlayers=6):
    sh = _prep_l0ffn_shared(inputs)
    tabs = np.stack([sh.pop("_tab_csq"), sh.pop("_tab_ssq"),
                     sh.pop("_tab_csk"), sh.pop("_tab_ssk")])
    maskW = sh.pop("_maskW").astype(BF)
    mask0a = sh.pop("_maskW0A").astype(BF)
    mask0b = sh.pop("_maskW0B").astype(BF)
    shared = {"maskW": maskW, "rperm": sh["rperm"], "e16": sh["e16"],
              "e16t": sh["e16t"], "e16t8": sh["e16t8"],
              "base2_c": sh["base2_c"],
              "ident": np.eye(128, dtype=BF)}
    for L in range(nlayers):
        for nm in ["wqk_l", "wv_r", "wo_l", "w1a_l", "w1g_l", "w2_l",
                   "b1a_pc", "b1g_pc", "b2_pc", "lngb"]:
            shared[nm + str(L)] = sh[nm + str(L)]
        shared["scl%d" % L] = np.concatenate(
            [sh[f"qs_c{L}"], sh[f"qsr_c{L}"], sh[f"ks_c{L}"], sh[f"ksr_c{L}"]],
            axis=1).astype(np.float32)
    maps = []
    for c in range(NCORES):
        m = dict(shared)
        s0 = T * c
        sl = tabs[:, s0 - 256 + 256: s0 + T + 256, :]
        slT = np.ascontiguousarray(np.tile(sl.transpose(0, 2, 1), (1, 2, 1))
                                   ).astype(np.float32)
        pad = np.zeros((4, 128, 2 * W + T - slT.shape[2]), np.float32)
        m["tabs"] = np.concatenate([slT, pad], axis=2)
        m["mask0"] = mask0a if c == 0 else maskW
        m["mask0b"] = mask0b if c == 0 else maskW
        m["hsc"] = np.full((128, 1), 0.0 if c == 0 else 1.0, np.float32)
        maps.append(m)
    return maps


def run_l0ffn(inputs, x1_list, nlayers=6):
    nc = build_l0ffn(nlayers)
    maps = _prep_l0ffn_maps(inputs, nlayers)
    for c in range(NCORES):
        maps[c]["x1T"] = np.ascontiguousarray(x1_list[c])
    res = run_bass_kernel_spmd(nc, maps, core_ids=list(range(NCORES)))
    return [r["x2T"] for r in res.results]



def decl_l1(p, fused=False):
    dio = {}
    if not fused:
        dio["x2T"] = p.din("x2T", [D, T])
        dio["hsc"] = p.din("hsc", [128, 1])
    dio["wth_l"] = p.din("wth_l", [128, 2 * D])
    dio["wph_r"] = p.din("wph_r", [128, DC * 256])
    dio["wg_r"] = p.din("wg_r", [128, DC * D])
    dio["wr_l"] = p.din("wr_l", [128, DC * D])
    dio["wk1_l"] = p.din("wk1_l", [128, DC * D])
    dio["wv1_l"] = p.din("wv1_l", [128, DC * D])
    dio["mur_pc"] = p.din("mur_pc", [128, DC])
    dio["muk_pc"] = p.din("muk_pc", [128, DC])
    dio["xoT"] = p.dout("xoT", [D, T])
    dio["c2_in"] = p.dint("c2_in", [256, D])
    dio["c2_out"] = p.dint("c2_out", [256, D], addr_space="Shared")
    dio["xt_in"] = p.dint("xt_in", [128, DC])
    dio["xt_out"] = p.dint("xt_out", [NCORES * 128, DC], addr_space="Shared")
    return dio


def emit_l1(p, dio, PFG, PBG, prev=None):
    x2T = dio["x2T"]; wth_l = dio["wth_l"]; wph_r = dio["wph_r"]
    wg_r = dio["wg_r"]; wr_l = dio["wr_l"]; wk_l = dio["wk1_l"]
    wv_l = dio["wv1_l"]; mur_pc = dio["mur_pc"]; muk_pc = dio["muk_pc"]
    hsc_d = dio["hsc"]; xoT = dio["xoT"]
    c2_in = dio["c2_in"]; c2_out = dio["c2_out"]
    xt_in = dio["xt_in"]; xt_out = dio["xt_out"]

    X2 = [p.sb(f"X2_{i}", [128, T + 8], F32) for i in range(DC)]
    X2B = [p.sb(f"X2B{i}", [128, T], BF16) for i in range(DC)]
    AR = p.sb("ARN1", [128, 81952], mybir.dt.uint8)

    def at(name, shape, dtype, off):
        return p.sb_at(name, shape, dtype, AR, off)
    THW = at("THW", [128, 2048], BF16, 0)
    THT = [at(f"THT{i}", [128, T], BF16, 4096 + 4096 * i) for i in range(2)]
    PH = [at(f"PH{i}", [128, 256], BF16, 12288 + 512 * i) for i in range(16)]
    GT = [at(f"GT{i}", [128, 1024], BF16, 20480 + 2048 * i) for i in range(16)]
    WGr = at("WGr", [128, DC * D], BF16, 53248)
    WPH = at("WPH", [128, 2048], BF16, 69632)
    WR = at("WR", [128, DC * D], BF16, 0)
    WK2 = at("WK2", [128, DC * D], BF16, 16384)
    WV2 = at("WV2", [128, DC * D], BF16, 32768)
    XRB = [at(f"XRB{i}", [128, 512], BF16, 49152 + 1024 * i) for i in range(DC)]
    XKB = [at(f"XKB{i}", [128, 512], BF16, 57344 + 1024 * i) for i in range(DC)]
    RS = [at(f"RS{i}", [128, 512], BF16, 65536 + 1024 * i) for i in range(DC)]
    KA = [at(f"KA{i}", [128, 512], BF16, 73728 + 1024 * i) for i in range(DC)]
    C2B = p.sb("C2B", [128, 2 * D], BF16)
    CS = [p.sb(f"CS{i}", [128, 512], F32) for i in range(2)]
    U = [p.sb_at(f"U1_{i}", [128, 512], F32, X2B[i], 0) for i in range(DC)]
    UB = [p.sb_at(f"UB1_{i}", [128, 512], BF16, X2B[i], 2048) for i in range(DC)]
    USQ = [p.sb_at(f"USQ1_{i}", [128, 512], BF16, X2B[i], 3072) for i in range(DC)]
    MUR = p.sb("MUR", [128, DC], F32)
    MUK = p.sb("MUK", [128, DC], F32)
    HSC = p.sb("HSC1", [128, 1], F32)
    TC2 = p.sb("TC2", [128, DC], F32)
    ZPA = p.sb("ZPA1", [128, 64], F32)
    ZP3 = p.sb("ZP31", [128, DC], F32)
    RMU = p.sb("RMU1", [1, 512], F32)
    RVA = p.sb("RVA1", [1, 512], F32)
    RBF = p.sb("RBF1", [1, 512], BF16)
    R2 = p.sb("R2_1", [2, 512], BF16)
    EPS = p.sb("EPS1", [1, 1], F32)
    ONEC = p.sb("ONEC1", [128, 1], BF16)
    ONER = p.sb("ONER1", [1, 128], BF16)

    PF = PFG

    s_ld = p.sem("m_ld")
    s_ldh = p.sem("m_ldh")
    s_w = p.sem("m_w")
    s_mm = p.sem("m_mm")
    s_ev = p.sem("m_ev")
    s_ac = p.sem("m_ac")
    s_dg = p.sem("m_dg")
    s_st = p.sem("m_st")
    cc1 = p.sem("m_cc1")
    cc2 = p.sem("m_cc2")
    s_cs = [p.sem("m_cs0"), p.sem("m_cs1")]

    blk = p.block
    if prev:
        emit_barrier(blk, prev)
    C = {"mm": 0, "ev": 0, "ac": 0, "ld": 0, "ldh": 0, "w": 0, "st": 0}

    @blk.gpsimd
    def _(e):
        for i in range(DC):
            e.dma_start(out=X2B[i][:], in_=x2T[128 * i:128 * (i + 1), :]).then_inc(s_ld, 16)
        e.dma_start(out=MUR[:], in_=mur_pc).then_inc(s_ld, 16)
        e.dma_start(out=MUK[:], in_=muk_pc).then_inc(s_ld, 16)
        e.dma_start(out=HSC[:], in_=hsc_d).then_inc(s_ld, 16)
        e.dma_start(out=THW[:], in_=wth_l).then_inc(s_w, 16)
        e.dma_start(out=WPH[:], in_=wph_r).then_inc(s_w, 16)
        e.dma_start(out=WGr[:], in_=wg_r).then_inc(s_w, 16)
    C["ld"] += 16 * (DC + 3)
    C["w"] += 48

    @blk.sync
    def _(e):
        for i in range(DC):
            e.dma_start(out=X2[i][:, 1:T + 1], in_=x2T[128 * i:128 * (i + 1), :]
                        ).then_inc(s_ldh, 16)
    C["ldh"] += 16 * DC

    @blk.vector
    def _(e):
        p.svi(e.memset(ONEC[:], 1.0), "v")
        p.svi(e.memset(ONER[:], 1.0), "v")
        p.svi(e.memset(R2[:], 1.0), "v")
        p.svi(e.memset(EPS[:], 1.0e-5), "v")

    def ln1(getin_bf, getout, n_tok=512):
        mm0, ev0, ac0 = C["mm"], C["ev"], C["ac"]

        @blk.vector
        def _(e):
            e.wait_ge(s_ev, ev0)
            for dm in range(DC):
                ins = e.tensor_tensor(out=USQ[dm][:, 0:n_tok], in0=getin_bf(dm),
                                      in1=getin_bf(dm), op=ALU.mult)
            ins.then_inc(s_ev, 1)

        @blk.tensor
        def _(e):
            e.wait_ge(s_ev, ev0 + 1)
            for dm in range(DC):
                e.matmul(PF[4][0:1, 0:n_tok], ONEC[:], getin_bf(dm),
                         start=(dm == 0), stop=(dm == DC - 1))
            for dm in range(DC):
                ins = e.matmul(PF[5][0:1, 0:n_tok], ONEC[:], USQ[dm][:, 0:n_tok],
                               start=(dm == 0), stop=(dm == DC - 1))
            ins.then_inc(s_mm, 1)

        @blk.scalar
        def _(e):
            e.wait_ge(s_mm, mm0 + 1)
            e.activation(out=RMU[0:1, 0:n_tok], in_=PF[4][0:1, 0:n_tok],
                         func=AF.Copy, bias=0.0, scale=-1.0 / D).then_inc(s_ac, 1)

        @blk.vector
        def _(e):
            e.wait_ge(s_ac, ac0 + 1)
            p.svi(e.tensor_tensor(out=RVA[0:1, 0:n_tok], in0=RMU[0:1, 0:n_tok],
                                  in1=RMU[0:1, 0:n_tok], op=ALU.mult), "v")
            p.svw(e, "v")
            e.scalar_tensor_tensor(out=RVA[0:1, 0:n_tok], in0=PF[5][0:1, 0:n_tok],
                                   scalar=1.0 / D, in1=RVA[0:1, 0:n_tok],
                                   op0=ALU.mult, op1=ALU.subtract).then_inc(s_ev, 1)

        @blk.scalar
        def _(e):
            e.wait_ge(s_ev, ev0 + 2)
            p.svi(e.activation(out=RVA[0:1, 0:n_tok], in_=RVA[0:1, 0:n_tok],
                               func=AF.Ln, bias=EPS[0:1, 0:1], scale=1.0), "s")
            p.svw(e, "s")
            e.activation(out=RBF[0:1, 0:n_tok], in_=RVA[0:1, 0:n_tok],
                         func=AF.Exp, bias=0.0, scale=-0.5).then_inc(s_ac, 1)

        @blk.vector
        def _(e):
            e.wait_ge(s_ac, ac0 + 2)
            e.tensor_tensor(out=R2[0:1, 0:n_tok], in0=RMU[0:1, 0:n_tok],
                            in1=RBF[0:1, 0:n_tok], op=ALU.mult).then_inc(s_ev, 1)

        @blk.tensor
        def _(e):
            e.wait_ge(s_ev, ev0 + 3)
            for dm in range(DC):
                if dm >= 1:
                    e.wait_ge(s_ev, ev0 + 3 + dm)
                e.matmul(PF[4][:, 0:n_tok], ONER[:], RBF[0:1, 0:n_tok],
                         start=True, stop=True)
                ins = e.matmul(PF[5][:, 0:n_tok], ONER[:], R2[0:1, 0:n_tok],
                               start=True, stop=True)
                ins.then_inc(s_mm, 1)

        @blk.vector
        def _(e):
            for dm in range(DC):
                e.wait_ge(s_mm, mm0 + 1 + dm + 1)
                p.svi(e.tensor_tensor(out=getout(dm), in0=getin_bf(dm),
                                      in1=PF[4][:, 0:n_tok], op=ALU.mult), "v")
                p.svw(e, "v")
                e.tensor_tensor(out=getout(dm), in0=getout(dm),
                                in1=PF[5][:, 0:n_tok], op=ALU.add).then_inc(s_ev, 1)
        C["mm"] = mm0 + 9
        C["ev"] = ev0 + 11
        C["ac"] = ac0 + 2

    th_mm0 = C["mm"]

    @blk.tensor
    def _(e):
        e.wait_ge(s_ld, 16 * (DC + 3))
        e.wait_ge(s_w, 48)
        p.svw(e, "v")
        for lc in range(2):
            for tt in range(4):
                i = lc * 4 + tt
                b = i % 2
                if i >= 2:
                    e.wait_ge(s_ev, i - 1)
                for kc in range(DC):
                    ins = e.matmul(PF[b][:], THW[:, lc * 1024 + kc * 128:lc * 1024 + (kc + 1) * 128],
                                   X2B[kc][:, 512 * tt:512 * (tt + 1)],
                                   start=(kc == 0), stop=(kc == DC - 1))
                ins.then_inc(s_mm, 1)

    @blk.vector
    def _(e):
        for lc in range(2):
            for tt in range(4):
                i = lc * 4 + tt
                b = i % 2
                e.wait_ge(s_mm, i + 1)
                e.tensor_copy(out=THT[lc][:, 512 * tt:512 * (tt + 1)],
                              in_=PF[b][:]).then_inc(s_ev, 1)

    @blk.tensor
    def _(e):
        for tc in range(16):
            b = 2 + tc % 2
            if tc >= 2:
                e.wait_ge(s_ev, 8 + tc - 1)
            for kc in range(DC):
                ins = e.matmul(PF[b][:, 0:256], X2B[kc][:, 128 * tc:128 * (tc + 1)],
                               WPH[:, kc * 256:(kc + 1) * 256],
                               start=(kc == 0), stop=(kc == DC - 1))
            ins.then_inc(s_mm, 1)

    @blk.vector
    def _(e):
        for tc in range(16):
            b = 2 + tc % 2
            e.wait_ge(s_mm, 8 + tc + 1)
            e.tensor_copy(out=PH[tc][:], in_=PF[b][:, 0:256]).then_inc(s_ev, 1)

    @blk.tensor
    def _(e):
        for tc in range(16):
            for oh in range(2):
                i = tc * 2 + oh
                b = 4 + i % 2
                if i >= 2:
                    e.wait_ge(s_ev, 24 + i - 1)
                for kc in range(DC):
                    ins = e.matmul(PF[b][:], X2B[kc][:, 128 * tc:128 * (tc + 1)],
                                   WGr[:, kc * 1024 + oh * 512:kc * 1024 + (oh + 1) * 512],
                                   start=(kc == 0), stop=(kc == DC - 1))
                ins.then_inc(s_mm, 1)

    @blk.vector
    def _(e):
        for tc in range(16):
            for oh in range(2):
                i = tc * 2 + oh
                b = 4 + i % 2
                e.wait_ge(s_mm, 24 + i + 1)
                e.tensor_copy(out=GT[tc][:, 512 * oh:512 * (oh + 1)],
                              in_=PF[b][:]).then_inc(s_ev, 1)
    C["mm"] = 56
    C["ev"] = 56
    @blk.tensor
    def _(e):
        e.wait_ge(s_ev, 56)
        for i in range(4):
            lc, oh = divmod(i, 2)
            b = i % 2
            if i >= 2:
                e.wait_ge(s_ev, 56 + i - 1)
            for tc in range(16):
                ins = e.matmul(PF[b][:], PH[tc][:, 128 * lc:128 * (lc + 1)],
                               GT[tc][:, 512 * oh:512 * (oh + 1)],
                               start=(tc == 0), stop=(tc == 15))
            ins.then_inc(s_mm, 1)

    @blk.vector
    def _(e):
        for i in range(4):
            b = i % 2
            e.wait_ge(s_mm, 56 + i + 1)
            if i >= 2:
                e.wait_ge(s_cs[b], 16)
            e.tensor_copy(out=CS[b][:], in_=PF[b][:]).then_inc(s_ev, 1)

    @blk.gpsimd
    def _(e):
        for i in range(4):
            lc, oh = divmod(i, 2)
            b = i % 2
            e.wait_ge(s_ev, 56 + i + 1)
            e.dma_start(out=c2_in[128 * lc:128 * (lc + 1), 512 * oh:512 * (oh + 1)],
                        in_=CS[b][:]).then_inc(s_cs[b], 16)
        e.wait_ge(s_cs[0], 32)
        e.wait_ge(s_cs[1], 32)
        e.collective_compute("AllReduce", ALU.add, replica_groups=RG,
                             ins=[c2_in.opt()], outs=[c2_out.opt()]).then_inc(cc1)
        e.wait_ge(cc1, 1)
        for lc in range(2):
            e.dma_start(out=C2B[:, 1024 * lc:1024 * (lc + 1)],
                        in_=c2_out[128 * lc:128 * (lc + 1), :]).then_inc(s_ld, 16)
    C["ld"] += 16 * 2
    C["mm"] = 60
    C["ev"] = 60

    @blk.vector
    def _(e):
        e.wait_ge(s_ld, C["ld"])
        for lc in range(2):
            ins = e.tensor_scalar_mul(out=C2B[:, 1024 * lc:1024 * (lc + 1)],
                                      in0=C2B[:, 1024 * lc:1024 * (lc + 1)],
                                      scalar1=1.0 / 32.0)
        ins.then_inc(s_ev, 1)
    C["ev"] = 61

    for tt in range(4):
        mm0, ev0 = C["mm"], C["ev"]

        @blk.tensor
        def _(e):
            if tt == 0:
                e.wait_ge(s_ev, 61)
            for oc in range(DC):
                b = oc % 2
                if oc >= 2:
                    e.wait_ge(s_ev, ev0 + oc - 1)
                elif tt > 0:
                    e.wait_ge(s_ev, ev0)
                for lc in range(2):
                    ins = e.matmul(PF[b][:], C2B[:, lc * 1024 + oc * 128:lc * 1024 + (oc + 1) * 128],
                                   THT[lc][:, 512 * tt:512 * (tt + 1)],
                                   start=(lc == 0), stop=(lc == 1))
                ins.then_inc(s_mm, 1)

        @blk.vector
        def _(e):
            for oc in range(DC):
                b = oc % 2
                e.wait_ge(s_mm, mm0 + oc + 1)
                if tt == 0 and oc == 0:
                    e.wait_ge(s_ldh, 16 * DC)
                e.tensor_add(out=U[oc][:], in0=PF[b][:],
                             in1=X2[oc][:, 1 + 512 * tt:1 + 512 * (tt + 1)]
                             ).then_inc(s_ev, 1)
        C["mm"] = mm0 + 8
        C["ev"] = ev0 + 8

        @blk.vector
        def _(e):
            e.wait_ge(s_ev, C["ev"])
            for dm in range(DC):
                ins = e.tensor_copy(out=UB[dm][:], in_=U[dm][:])
            ins.then_inc(s_ev, 1)
        C["ev"] += 1

        def gi3(dm):
            return UB[dm][:]

        def go3(dm, tt=tt):
            return X2[dm][:, 1 + 512 * tt:1 + 512 * (tt + 1)]
        ln1(gi3, go3)

    x3_ev = C["ev"]

    @blk.vector
    def _(e):
        e.wait_ge(s_ev, x3_ev)
        for i in range(DC):
            ins = e.tensor_copy(out=TC2[:, i:i + 1], in_=X2[i][:, T:T + 1])
        ins.then_inc(s_ev, 1)
    C["ev"] += 1

    @blk.gpsimd
    def _(e):
        e.wait_ge(s_ev, C["ev"])
        e.dma_start(out=xt_in, in_=TC2[:]).then_inc(s_dg, 16)
        e.wait_ge(s_dg, 16)
        e.collective_compute("AllGather", ALU.bypass, replica_groups=RG,
                             ins=[xt_in.opt()], outs=[xt_out.opt()]).then_inc(cc2)
        e.wait_ge(s_mm, C["mm"])
        e.dma_start(out=WR[:], in_=wr_l).then_inc(s_w, 16)
        e.dma_start(out=WK2[:], in_=wk_l).then_inc(s_w, 16)
        e.dma_start(out=WV2[:], in_=wv_l).then_inc(s_w, 16)
    C["w"] += 48

    @blk.sync
    def _(e):
        e.wait_ge(cc2, 1)
        e.dma_start(out=ZPA[:].rearrange("p (r c) -> p r c", c=DC),
                    in_=xt_out.rearrange("(r p) c -> p r c", p=128)).then_inc(s_ldh, 16)
    C["ldh"] += 16
    zp_ld = C["ldh"]

    @blk.vector
    def _(e):
        e.wait_ge(s_ldh, zp_ld)
        for arm in e.switch_core_id(n=128):
            rr = (int(arm.logical) % NCORES + NCORES - 1) % NCORES
            e.tensor_copy(out=ZP3[:], in_=ZPA[:, rr * DC:(rr + 1) * DC]).then_inc(p.sv["v"][0], 1)
        p.sv["v"][1] += 1
        p.svw(e, "v")
        p.svi(e.tensor_scalar_mul(out=ZP3[:], in0=ZP3[:], scalar1=HSC[:, 0:1]), "v")
        p.svw(e, "v")
        for i in range(DC):
            ins = e.tensor_copy(out=X2[i][:, 0:1], in_=ZP3[:, i:i + 1])
        ins.then_inc(s_ev, 1)
    C["ev"] += 1

    for tt in range(4):
        t0 = 512 * tt
        mm0, ev0, ac0 = C["mm"], C["ev"], C["ac"]

        @blk.vector
        def _(e):
            e.wait_ge(s_ev, ev0)
            for i in range(DC):
                p.svi(e.tensor_sub(out=U[i][:], in0=X2[i][:, 1 + t0:1 + t0 + 512],
                                   in1=X2[i][:, t0:t0 + 512]), "v")
                p.svw(e, "v")
                e.scalar_tensor_tensor(out=XRB[i][:], in0=U[i][:],
                                       scalar=MUR[:, i:i + 1],
                                       in1=X2[i][:, t0:t0 + 512],
                                       op0=ALU.mult, op1=ALU.add)
                ins = e.scalar_tensor_tensor(out=XKB[i][:], in0=U[i][:],
                                             scalar=MUK[:, i:i + 1],
                                             in1=X2[i][:, t0:t0 + 512],
                                             op0=ALU.mult, op1=ALU.add)
            ins.then_inc(s_ev, 1)
        C["ev"] += 1

        for which in range(2):
            for mc in range(DC):
                mm0b, ev0b, ac0b = C["mm"], C["ev"], C["ac"]
                b = which * 2 + mc % 2

                @blk.tensor
                def _(e):
                    if tt == 0 and which == 0 and mc == 0:
                        e.wait_ge(s_w, C["w"])
                    e.wait_ge(s_ev, C["ev"])
                    if mc >= 2:
                        e.wait_ge(s_ac, ac0b - 1)
                    W_ = WR if which == 0 else WK2
                    for kc in range(DC):
                        ins = e.matmul(PF[b][:], W_[:, mc * 1024 + kc * 128:mc * 1024 + (kc + 1) * 128],
                                       (XRB if which == 0 else XKB)[kc][:],
                                       start=(kc == 0), stop=(kc == DC - 1))
                    ins.then_inc(s_mm, 1)

                @blk.scalar
                def _(e):
                    e.wait_ge(s_mm, mm0b + 1)
                    e.activation(out=(RS if which == 0 else KA)[mc][:], in_=PF[b][:],
                                 func=AF.Sigmoid if which == 0 else AF.Relu,
                                 bias=0.0, scale=1.0).then_inc(s_ac, 1)
                C["mm"] = mm0b + 1
                C["ac"] = ac0b + 1

        @blk.vector
        def _(e):
            e.wait_ge(s_ac, C["ac"])
            for i in range(DC):
                ins = e.tensor_tensor(out=KA[i][:], in0=KA[i][:], in1=KA[i][:],
                                      op=ALU.mult)
            ins.then_inc(s_ev, 1)
        C["ev"] += 1
        ka_ev = C["ev"]

        for mc in range(DC):
            mm0b, ev0b = C["mm"], C["ev"]
            b = 4 + mc % 2

            @blk.tensor
            def _(e):
                if mc == 0:
                    e.wait_ge(s_ev, ka_ev)
                if mc >= 2:
                    e.wait_ge(s_ev, ev0b - 1)
                for kc in range(DC):
                    ins = e.matmul(PF[b][:], WV2[:, mc * 1024 + kc * 128:mc * 1024 + (kc + 1) * 128],
                                   KA[kc][:], start=(kc == 0), stop=(kc == DC - 1))
                ins.then_inc(s_mm, 1)

            @blk.vector
            def _(e):
                e.wait_ge(s_mm, mm0b + 1)
                p.svi(e.tensor_tensor(out=U[mc][:], in0=PF[b][:], in1=RS[mc][:],
                                      op=ALU.mult), "v")
                p.svw(e, "v")
                e.tensor_add(out=U[mc][:], in0=U[mc][:],
                             in1=X2[mc][:, 1 + t0:1 + t0 + 512]).then_inc(s_ev, 1)
            C["mm"] = mm0b + 1
            C["ev"] = ev0b + 1

        @blk.vector
        def _(e):
            e.wait_ge(s_ev, C["ev"])
            for dm in range(DC):
                ins = e.tensor_copy(out=UB[dm][:], in_=U[dm][:])
            ins.then_inc(s_ev, 1)
        C["ev"] += 1

        def gi4(dm):
            return UB[dm][:]

        def go4(dm, t0=t0):
            return X2[dm][:, 1 + t0:1 + t0 + 512]
        ln1(gi4, go4)
        fin_ev = C["ev"]

        @blk.sync
        def _(e):
            e.wait_ge(s_ev, fin_ev)
            for i in range(DC):
                e.dma_start(out=xoT[128 * i:128 * (i + 1), t0:t0 + 512],
                            in_=X2[i][:, 1 + t0:1 + t0 + 512]).then_inc(s_st, 16)
        C["st"] += 16 * DC

    @blk.sync
    def _(e):
        e.wait_ge(s_st, C["st"])

    return [(s_st, C["st"])]


def build_l1():
    p = Prog()
    dio = decl_l1(p)
    PFG = [p.ps(f"PFg{i}", [128, 512], F32) for i in range(6)]
    PBG = [p.ps(f"PBg{i}", [128, 1024], BF16) for i in range(2)]
    p.begin()
    emit_l1(p, dio, PFG, PBG)
    p.end()
    return p.nc


def build_fused(nlayers=6):
    p = Prog()
    d0 = decl_block0(p, fused=True)
    d1 = decl_l0ffn(p, nlayers, fused=True)
    d1["x1T"] = d0["x1T"]
    d1["ident"] = d0["ident"]
    d2 = decl_l1(p, fused=True)
    d2["x2T"] = d1["x2T"]
    d2["hsc"] = d1["hsc"]
    PFG = [p.ps(f"PFg{i}", [128, 512], F32) for i in range(6)]
    PBG = [p.ps(f"PBg{i}", [128, 1024], BF16) for i in range(2)]
    AU = p.stack.enter_context(
        p.nc.sbuf_tensor("AU", [128, 211808], mybir.dt.uint8))
    p.begin()
    p.set_arena(AU, 211808)
    f0 = emit_block0(p, d0, PFG, PBG)
    p.set_arena(AU, 211808)
    f1 = emit_l0ffn(p, d1, PFG, PBG, nlayers, prev=f0)
    p.set_arena(AU, 211808)
    f2 = emit_l1(p, d2, PFG, PBG, prev=f1)
    p.end()
    return p.nc


def _prep_l1_maps(inputs):
    Wth = np.asarray(inputs["L1_attn_W_theta"], np.float32)
    Wph = np.asarray(inputs["L1_attn_W_phi"], np.float32)
    Wg = np.asarray(inputs["L1_attn_W_g"], np.float32)
    Wr = np.asarray(inputs["L1_ffn_W_r"], np.float32)
    Wk = np.asarray(inputs["L1_ffn_W_k"], np.float32)
    Wv = np.asarray(inputs["L1_ffn_W_v"], np.float32)
    mur = np.asarray(inputs["L1_ffn_mu_r"], np.float32)
    muk = np.asarray(inputs["L1_ffn_mu_k"], np.float32)
    shared = {
        "wth_l": _pmaj(_lhsT_blocks(np.ascontiguousarray(Wth.T))),
        "wph_r": _pmaj(_rhs_blocks(np.ascontiguousarray(Wph.T))),
        "wg_r": _pmaj(_rhs_blocks(np.ascontiguousarray(Wg.T))),
        "wr_l": _pmaj(_lhsT_blocks(np.ascontiguousarray(Wr.T))),
        "wk1_l": _pmaj(_lhsT_blocks(np.ascontiguousarray(Wk.T))),
        "wv1_l": _pmaj(_lhsT_blocks(np.ascontiguousarray(Wv.T))),
        "mur_pc": _colmat(mur),
        "muk_pc": _colmat(muk),
    }
    maps = []
    for c in range(NCORES):
        m = dict(shared)
        m["hsc"] = np.full((128, 1), 0.0 if c == 0 else 1.0, np.float32)
        maps.append(m)
    return maps


def run_l1(inputs, x2_list):
    nc = build_l1()
    maps = _prep_l1_maps(inputs)
    for c in range(NCORES):
        maps[c]["x2T"] = np.ascontiguousarray(x2_list[c])
    res = run_bass_kernel_spmd(nc, maps, core_ids=list(range(NCORES)))
    return [r["xoT"] for r in res.results]
